# version 03
# baseline (speedup 1.0000x reference)
import contextlib
import numpy as np
import ml_dtypes
import concourse.bass as bass
import concourse.mybir as mybir
from concourse.bass_utils import run_bass_kernel_spmd

F32 = mybir.dt.float32
F16 = mybir.dt.float16
BF16 = mybir.dt.bfloat16
I32 = mybir.dt.int32
AF = mybir.ActivationFunctionType
ALU = mybir.AluOpType
AX = mybir.AxisListType

B, T, D, H, HD = 2, 4096, 2048, 16, 128
R = 1024
NC = 8
SCALE = HD ** -0.5
EPS = 1e-5
NEG = -60000.0
TINY = 1e-30
QMAX = 127.0
QOFF = D * R
XFLAT = 2 * QOFF + 128

CHUNKS = []
ITILES = []
for _h in range(H):
    for _k in range(8):
        for _c in range(_k + 1):
            CHUNKS.append((len(CHUNKS), _h, _k, _c, _c == _k))
        ITILES.append((len(ITILES), _h, _k, _k + 1, len(CHUNKS) - 1))
G_TOT = len(CHUNKS)
M_TOT = len(ITILES)
M_OF_G = {}
for (_m, _h, _k, _nch, _gl) in ITILES:
    for _c in range(_nch):
        M_OF_G[_gl - _nch + 1 + _c] = _m
G0 = {h: 36 * h for h in range(H)}


class Sem:

    def __init__(self, handle):
        self.h = handle
        self.n = 0

    def dma(self, inst):
        inst.then_inc(self.h, 16)
        self.n += 16
        return inst

    def one(self, inst):
        inst.then_inc(self.h, 1)
        self.n += 1
        return inst


def build_nc(dump=()):
    nc = bass.Bass("TRN2", target_bir_lowering=False)

    x_in = nc.dram_tensor("x_own", [R, D], F32, kind="ExternalInput")
    qT_in = nc.dram_tensor("qT", [D, D], F32, kind="ExternalInput")
    kT_in = nc.dram_tensor("kT", [D, D], F32, kind="ExternalInput")
    vT_in = nc.dram_tensor("vT", [D, D], F32, kind="ExternalInput")
    qb_in = nc.dram_tensor("qb", [1, D], F32, kind="ExternalInput")
    kb_in = nc.dram_tensor("kb", [1, D], F32, kind="ExternalInput")
    vb_in = nc.dram_tensor("vb", [1, D], F32, kind="ExternalInput")
    oT_in = nc.dram_tensor("oT", [D, D], F32, kind="ExternalInput")
    ob_in = nc.dram_tensor("ob", [1, D], F32, kind="ExternalInput")
    fk_in = nc.dram_tensor("fk", [D, D], F32, kind="ExternalInput")
    fkT_in = nc.dram_tensor("fkT", [D, D], F32, kind="ExternalInput")
    fb_in = nc.dram_tensor("fb", [1, D], F32, kind="ExternalInput")
    c_in = nc.dram_tensor("c_own", [1, D], F32, kind="ExternalInput")
    wm_in = nc.dram_tensor("wm_own", [D, 3072], F32, kind="ExternalInput")
    ident_in = nc.dram_tensor("ident", [128, 128], BF16, kind="ExternalInput")
    pat_in = nc.dram_tensor("pats", [2, 128, 512], F16, kind="ExternalInput")
    t512_in = nc.dram_tensor("t512c", [128, 8], F32, kind="ExternalInput")
    i0bc_in = nc.dram_tensor("i0bc", [128, 8], F32, kind="ExternalInput")
    out_t = nc.dram_tensor("out", [R, D], F32, kind="ExternalOutput")

    xmod_i = nc.dram_tensor("xmod_i", [1, 3072], F32)
    xmod_o = nc.dram_tensor("xmod_o", [4, 3072], F32)
    xqv_i = nc.dram_tensor("xqv_i", [1, XFLAT], BF16)
    xqv_o = nc.dram_tensor("xqv_o", [4, XFLAT], BF16)
    xsy_o = nc.dram_tensor("xsy_o", [1, 16], F32)
    xsy2_o = nc.dram_tensor("xsy2_o", [1, 16], BF16)
    vec_s = nc.dram_tensor("vec_s", [16, D], F32)
    nrm_s = nc.dram_tensor("nrm_s", [H, R], F32)
    x2_d = nc.dram_tensor("x2_d", [R, D], F32)

    dump_t = {}
    for name, shp, dt in (("d_h1", [128, 16 * R], BF16), ("d_qfm", [128, 16 * R], BF16),
                          ("d_kfm", [128, 16 * R], BF16), ("d_vfm", [128, 16 * R], BF16),
                          ("d_bounds", [128, 128], F32), ("d_afm", [128, 16 * R], BF16),
                          ("d_x2", [128, 8 * D], F32), ("d_qx", [128, 16 * R], BF16),
                          ("d_mod", [4, 3072], F32), ("d_vecs", [16, D], F32),
                          ("d_bca", [128, 2 * D], F32)):
        if name in dump:
            dump_t[name] = nc.dram_tensor(name, shp, dt, kind="ExternalOutput")

    ES = contextlib.ExitStack()
    with ES:
        E = ES.enter_context

        vf = Sem(E(nc.semaphore("vf")))

        def fence(eng, inst):
            vf.n += 1
            inst.then_inc(vf.h, 1)
            eng.wait_ge(vf.h, vf.n)
            return inst

        s_ld = Sem(E(nc.semaphore("s_ld")))
        s_gl = Sem(E(nc.semaphore("s_gl")))
        s_cc = Sem(E(nc.semaphore("s_cc")))
        s_mm = Sem(E(nc.semaphore("s_mm")))
        s_ve = Sem(E(nc.semaphore("s_ve")))
        s_ac = Sem(E(nc.semaphore("s_ac")))
        s_st = Sem(E(nc.semaphore("s_st")))
        a_s = Sem(E(nc.semaphore("a_s")))
        a_t = Sem(E(nc.semaphore("a_t")))
        a_p = Sem(E(nc.semaphore("a_p")))
        a_tr = Sem(E(nc.semaphore("a_tr")))
        a_e = Sem(E(nc.semaphore("a_e")))
        a_av = Sem(E(nc.semaphore("a_av")))
        a_oe = Sem(E(nc.semaphore("a_oe")))
        a_ot = Sem(E(nc.semaphore("a_ot")))
        a_of = Sem(E(nc.semaphore("a_of")))
        a_qv = Sem(E(nc.semaphore("a_qv")))
        a_vt = Sem(E(nc.semaphore("a_vt")))
        a_ve = Sem(E(nc.semaphore("a_ve")))

        ident = E(nc.sbuf_tensor("identt", [128, 128], BF16))
        t512 = E(nc.sbuf_tensor("t512t", [128, 8], F32))
        i0bc = E(nc.sbuf_tensor("i0bct", [128, 8], F32))
        sxv = E(nc.sbuf_tensor("sxv", [128, 8], F32))

        st_h1 = contextlib.ExitStack()
        h1fm = st_h1.enter_context(nc.sbuf_tensor("h1fm", [128, 16 * R], BF16, side="left"))
        st_kfm = contextlib.ExitStack()
        kfm = st_kfm.enter_context(nc.sbuf_tensor("kfm", [128, 16 * R], BF16, side="right"))

        with contextlib.ExitStack() as P:
            PE = P.enter_context
            cvec = PE(nc.sbuf_tensor("cvec", [128, 16], F32))
            csil = PE(nc.sbuf_tensor("csil", [128, 16], BF16))
            wmt = [PE(nc.sbuf_tensor(f"wmt{i}", [128, 3072], BF16)) for i in range(2)]
            modv = PE(nc.sbuf_tensor("modv", [1, 3072], F32))
            mps = [PE(nc.psum_tensor(f"mps{j}", [1, 512], F32)) for j in range(6)]
            blk = PE(nc.Block())

            @blk.sync
            def _(sync):
                s_ld.dma(sync.dma_start(out=ident[:, :], in_=ident_in.ap()[:, :]))
                s_ld.dma(sync.dma_start(out=t512[:, :], in_=t512_in.ap()[:, :]))
                s_ld.dma(sync.dma_start(out=i0bc[:, :], in_=i0bc_in.ap()[:, :]))
                s_ld.dma(sync.dma_start(out=cvec[:, :],
                                        in_=bass.AP(c_in, 0, [[1, 128], [128, 16]]), allow_slow_non_contiguous=True))

            ld0, mm0, gl0 = s_ld.n, s_mm.n, s_gl.n

            @blk.gpsimd
            def _(gpsimd):
                for dc in range(16):
                    if dc >= 2:
                        gpsimd.wait_ge(s_mm.h, mm0 + (dc - 1) * 6)
                    s_gl.dma(gpsimd.dma_start(
                        out=wmt[dc % 2][:, :],
                        in_=wm_in.ap()[dc * 128:(dc + 1) * 128, :]))

            @blk.scalar
            def _(scalar):
                scalar.wait_ge(s_ld.h, ld0)
                s_ac.one(scalar.activation(csil[:, :], cvec[:, :], AF.Silu))

            ac0 = s_ac.n

            @blk.tensor
            def _(tensor):
                tensor.wait_ge(s_ac.h, ac0)
                for dc in range(16):
                    tensor.wait_ge(s_gl.h, gl0 + (dc + 1) * 16)
                    for j in range(6):
                        s_mm.one(tensor.matmul(
                            mps[j][:, :],
                            csil[:, dc:dc + 1],
                            wmt[dc % 2][:, j * 512:(j + 1) * 512],
                            start=(dc == 0), stop=(dc == 15)))

            mm0e = s_mm.n

            @blk.vector
            def _(vector):
                vector.wait_ge(s_mm.h, mm0e)
                for j in range(6):
                    s_ve.one(vector.tensor_copy(modv[:, j * 512:(j + 1) * 512],
                                                mps[j][:, :]))

            ve0 = s_ve.n

            @blk.gpsimd
            def _(gpsimd):
                gpsimd.wait_ge(s_ve.h, ve0)
                s_gl.dma(gpsimd.dma_start(out=xmod_i.ap()[:, :], in_=modv[:, :]))
                gpsimd.wait_ge(s_gl.h, s_gl.n)
                s_cc.one(gpsimd.collective_compute(
                    "AllGather", ALU.bypass,
                    replica_groups=[[0, 1, 2, 3], [4, 5, 6, 7]],
                    ins=[xmod_i.ap().opt()], outs=[xmod_o.ap().opt()]))
                s_cc.one(gpsimd.collective_compute(
                    "AllReduce", ALU.add,
                    replica_groups=[[0, 1, 2, 3], [4, 5, 6, 7]],
                    ins=[bass.AP(xmod_i, 0, [[1, 1], [1, 16]]).opt()],
                    outs=[xsy_o.ap().opt()]))
                gpsimd.wait_ge(s_cc.h, s_cc.n)
                if "d_mod" in dump_t:
                    s_gl.dma(gpsimd.dma_start(out=dump_t["d_mod"].ap()[:, :],
                                              in_=xmod_o.ap()[:, :]))
                    gpsimd.wait_ge(s_gl.h, s_gl.n)

        with contextlib.ExitStack() as P:
            PE = P.enter_context
            vrow = PE(nc.sbuf_tensor("vrow", [1, D], F32))
            vrow2 = PE(nc.sbuf_tensor("vrow2", [1, D], F32))
            vrow3 = PE(nc.sbuf_tensor("vrow3", [1, D], F32))
            msk = PE(nc.sbuf_tensor("msk", [1, D], mybir.dt.uint8))
            mtmp = PE(nc.sbuf_tensor("mtmp", [1, D], F32))
            mtmp2 = PE(nc.sbuf_tensor("mtmp2", [1, D], F32))
            mi32 = PE(nc.sbuf_tensor("mi32", [1, D], I32))
            sc1 = PE(nc.sbuf_tensor("sc1", [1, 1], F32))
            sc2 = PE(nc.sbuf_tensor("sc2", [1, 1], F32))
            blk = PE(nc.Block())

            def mod_sl(i):
                return bass.AP(xmod_o, i * D, [[1, 1], [1, D]])

            @blk.gpsimd
            def _(gpsimd):
                s_gl.dma(gpsimd.dma_start(out=vec_s.ap()[0:1, :], in_=mod_sl(0)))
                s_gl.dma(gpsimd.dma_start(out=vec_s.ap()[3:4, :], in_=mod_sl(3)))
                s_gl.dma(gpsimd.dma_start(out=vrow[:, :], in_=mod_sl(1)))
                s_gl.dma(gpsimd.dma_start(out=vrow2[:, :], in_=mod_sl(4)))
                s_gl.dma(gpsimd.dma_start(out=vrow3[:, :], in_=mod_sl(2)))

            gl1 = s_gl.n

            @blk.vector
            def _(vector):
                vector.wait_ge(s_gl.h, gl1)
                s_ve.one(vector.tensor_scalar_add(mtmp[:, :], vrow[:, :], 1.0))
                s_ve.one(vector.tensor_copy(vrow[:, :], mtmp[:, :]))
                s_ve.one(vector.tensor_scalar_add(mtmp[:, :], vrow2[:, :], 1.0))
                s_ve.one(vector.tensor_copy(vrow2[:, :], mtmp[:, :]))
                s_ve.one(vector.tensor_scalar(msk[:, :], vrow3[:, :], 0.0, None, ALU.is_ge))
                s_ve.one(vector.tensor_scalar_max(mtmp[:, :], vrow3[:, :], 0.25))
                s_ve.one(vector.tensor_scalar_min(mtmp2[:, :], vrow3[:, :], -0.25))
                s_ve.one(vector.select(vrow3[:, :], msk[:, :], mtmp[:, :], mtmp2[:, :]))

            ve1 = s_ve.n

            @blk.gpsimd
            def _(gpsimd):
                gpsimd.wait_ge(s_ve.h, ve1)
                s_gl.dma(gpsimd.dma_start(out=vec_s.ap()[1:2, :], in_=vrow[:, :]))
                s_gl.dma(gpsimd.dma_start(out=vec_s.ap()[4:5, :], in_=vrow2[:, :]))
                s_gl.dma(gpsimd.dma_start(out=vec_s.ap()[2:3, :], in_=vrow3[:, :]))
                s_gl.dma(gpsimd.dma_start(out=vrow3[:, :], in_=mod_sl(5)))

            gl2 = s_gl.n

            @blk.vector
            def _(vector):
                vector.wait_ge(s_gl.h, gl2)
                s_ve.one(vector.tensor_scalar(msk[:, :], vrow3[:, :], 0.0, None, ALU.is_ge))
                s_ve.one(vector.tensor_scalar_max(mtmp[:, :], vrow3[:, :], 0.25))
                s_ve.one(vector.tensor_scalar_min(mtmp2[:, :], vrow3[:, :], -0.25))
                s_ve.one(vector.select(vrow3[:, :], msk[:, :], mtmp[:, :], mtmp2[:, :]))

            ve2 = s_ve.n

            @blk.gpsimd
            def _(gpsimd):
                gpsimd.wait_ge(s_ve.h, ve2)
                s_gl.dma(gpsimd.dma_start(out=vec_s.ap()[5:6, :], in_=vrow3[:, :]))
                s_gl.dma(gpsimd.dma_start(out=vrow[:, :], in_=fb_in.ap()[0:1, :]))

            gl3 = s_gl.n

            @blk.vector
            def _(vector):
                vector.wait_ge(s_gl.h, gl3)
                s_ve.one(vector.tensor_reduce(sc1[:, :], vrow[:, :], AX.X, ALU.max,
                                              apply_absolute_value=True))
                vector.wait_ge(s_ve.h, s_ve.n)
                s_ve.one(vector.tensor_scalar(sc2[:, :], sc1[:, :], TINY, 1.0 / QMAX,
                                              ALU.max, ALU.mult))
                vector.wait_ge(s_ve.h, s_ve.n)
                s_ve.one(vector.reciprocal(mtmp[0:1, 0:1], sc2[:, :]))
                vector.wait_ge(s_ve.h, s_ve.n)
                s_ve.one(vector.tensor_scalar(mtmp2[:, :], vrow[:, :], mtmp[0:1, 0:1],
                                              None, ALU.mult))
                s_ve.one(vector.tensor_copy(mi32[:, :], mtmp2[:, :]))
                s_ve.one(vector.tensor_scalar(mtmp[:, :], mi32[:, :], QMAX, -QMAX,
                                              ALU.min, ALU.max))
                s_ve.one(vector.tensor_scalar(mtmp2[:, :], mtmp[:, :], sc2[0:1, 0:1],
                                              None, ALU.mult))

            ve3 = s_ve.n

            @blk.gpsimd
            def _(gpsimd):
                gpsimd.wait_ge(s_ve.h, ve3)
                s_gl.dma(gpsimd.dma_start(out=vec_s.ap()[6:7, :], in_=mtmp2[:, :]))
                gpsimd.wait_ge(s_gl.h, s_gl.n)

        with contextlib.ExitStack() as P:
            PE = P.enter_context
            bca = PE(nc.sbuf_tensor("bca", [128, D], F32))
            bcb = PE(nc.sbuf_tensor("bcb", [128, D], F32))
            xt = [PE(nc.sbuf_tensor(f"xt{i}", [128, D], F32)) for i in range(2)]
            lnt = PE(nc.sbuf_tensor("lnt", [128, D], F32))
            h1t = [PE(nc.sbuf_tensor(f"h1t{i}", [128, D], BF16)) for i in range(2)]
            bnb = PE(nc.sbuf_tensor("bnb", [128, 24], F32))
            b6a = PE(nc.sbuf_tensor("b6a", [128, 6], F32))
            b6b = PE(nc.sbuf_tensor("b6b", [128, 6], F32))
            b6c = PE(nc.sbuf_tensor("b6c", [128, 6], F32))
            sq24 = PE(nc.sbuf_tensor("sq24", [128, 24], F32))
            s6a = PE(nc.sbuf_tensor("s6a", [128, 6], F32))
            s6b = PE(nc.sbuf_tensor("s6b", [128, 6], F32))
            s6c = PE(nc.sbuf_tensor("s6c", [128, 6], F32))
            vvs = PE(nc.sbuf_tensor("vvs", [128, 12], F32))
            lnu = PE(nc.sbuf_tensor("lnu", [128, D], F32))
            mvec = PE(nc.sbuf_tensor("mvec", [128, 1], F32))
            vvar = [PE(nc.sbuf_tensor(f"vvar{i}", [128, 1], F32)) for i in range(2)]
            sqv = [PE(nc.sbuf_tensor(f"sqv{i}", [128, 1], F32)) for i in range(2)]
            rstd = PE(nc.sbuf_tensor("rstd", [128, 1], F32))
            htp = [PE(nc.psum_tensor(f"htp{i}", [128, 1024], BF16)) for i in range(2)]
            blk = PE(nc.Block())

            ld1, ve1b, ac1, mm1 = s_ld.n, s_ve.n, s_ac.n, s_mm.n

            @blk.sync
            def _(sync):
                s_ld.dma(sync.dma_start(out=bca[:, :],
                                        in_=bass.AP(vec_s, 1 * D, [[0, 128], [1, D]])))
                s_ld.dma(sync.dma_start(out=bcb[:, :],
                                        in_=bass.AP(vec_s, 0 * D, [[0, 128], [1, D]])))
                for rt in range(8):
                    if rt >= 2:
                        sync.wait_ge(s_ve.h, ve1b + (rt - 1) * 2)
                    s_ld.dma(sync.dma_start(out=xt[rt % 2][:, :],
                                            in_=x_in.ap()[rt * 128:(rt + 1) * 128, :]))

            @blk.vector
            def _(vector):
                for rt in range(8):
                    xx = xt[rt % 2]
                    vector.wait_ge(s_ld.h, ld1 + 32 + (rt + 1) * 16)
                    if rt >= 2:
                        vector.wait_ge(s_mm.h, mm1 + (rt - 1) * 16)
                    for q in range(4):
                        vector.bn_stats(bnb[:, q * 6:(q + 1) * 6],
                                        xx[:, q * 512:(q + 1) * 512])
                    vector.tensor_tensor(b6a[:, :], bnb[:, 0:6], bnb[:, 6:12], ALU.add)
                    vector.tensor_tensor(b6b[:, :], bnb[:, 12:18], bnb[:, 18:24], ALU.add)
                    vector.tensor_tensor(sq24[:, :], bnb[:, :], bnb[:, :], ALU.mult)
                    vector.tensor_tensor(b6c[:, :], b6a[:, :], b6b[:, :], ALU.add)
                    vector.tensor_tensor(s6a[:, :], sq24[:, 0:6], sq24[:, 6:12], ALU.add)
                    vector.tensor_tensor(s6b[:, :], sq24[:, 12:18], sq24[:, 18:24], ALU.add)
                    vector.tensor_tensor(vvs[:, 0:1], b6c[:, 1:2], b6c[:, 4:5], ALU.add)
                    vector.tensor_tensor(s6c[:, :], s6a[:, :], s6b[:, :], ALU.add)
                    vector.tensor_tensor(vvs[:, 1:2], b6c[:, 2:3], b6c[:, 5:6], ALU.add)
                    vector.tensor_scalar(mvec[:, :], vvs[:, 0:1], 1.0 / 8.0, None, ALU.mult)
                    vector.tensor_tensor(vvs[:, 3:4], s6c[:, 1:2], s6c[:, 4:5], ALU.add)
                    vector.tensor_scalar(vvs[:, 2:3], vvs[:, 1:2], 1.0 / D, None, ALU.mult)
                    vector.tensor_scalar(vvs[:, 4:5], vvs[:, 3:4], 1.0 / 8.0, None, ALU.mult)
                    vector.tensor_tensor(vvs[:, 6:7], mvec[:, :], mvec[:, :], ALU.mult)
                    fence(vector, vector.tensor_tensor(vvs[:, 5:6], vvs[:, 2:3],
                                                       vvs[:, 4:5], ALU.add))
                    fence(vector, vector.tensor_tensor(vvs[:, 7:8], vvs[:, 5:6],
                                                       vvs[:, 6:7], ALU.subtract))
                    s_ve.one(vector.tensor_scalar(vvar[rt % 2][:, :], vvs[:, 7:8],
                                                  1.0, EPS, ALU.mult, ALU.add))
                    vector.wait_ge(s_ac.h, ac1 + (rt + 1 if rt <= 1 else 5 * rt - 3))
                    fence(vector, vector.reciprocal(rstd[:, :], sqv[rt % 2][:, :]))
                    vector.tensor_scalar(lnt[:, :], xx[:, :], mvec[:, :], rstd[:, :],
                                         ALU.subtract, ALU.mult)
                    vector.tensor_tensor(lnu[:, :], lnt[:, :], bca[:, :], ALU.mult)
                    s_ve.one(vector.tensor_tensor(h1t[rt % 2][:, :], lnu[:, :],
                                                  bcb[:, :], ALU.add))

            evac_cnt1 = {}

            @blk.scalar
            def _(scalar):
                def evacs(rt):
                    for grp in range(4):
                        scalar.wait_ge(s_mm.h, mm1 + rt * 16 + (grp + 1) * 4)
                        s_ac.one(scalar.copy(
                            bass.AP(h1fm, (4 * grp) * R + rt * 128,
                                    [[16 * R, 128], [R, 4], [1, 128]]),
                            htp[grp % 2][:, 0:512]))
                        evac_cnt1[(rt, grp)] = s_ac.n
                for rt in range(8):
                    scalar.wait_ge(s_ve.h, ve1b + rt * 2 + 1)
                    s_ac.one(scalar.activation(sqv[rt % 2][:, :], vvar[rt % 2][:, :],
                                               AF.Sqrt))
                    if rt >= 1:
                        evacs(rt - 1)
                evacs(7)

            @blk.tensor
            def _(tensor):
                for rt in range(8):
                    tensor.wait_ge(s_ve.h, ve1b + rt * 2 + 2)
                    for dc in range(16):
                        grp = dc // 4
                        gidx = rt * 4 + grp
                        if gidx >= 2 and dc % 4 == 0:
                            prt, pgrp = (gidx - 2) // 4, (gidx - 2) % 4
                            tensor.wait_ge(s_ac.h, evac_cnt1[(prt, pgrp)])
                        s_mm.one(tensor.transpose(
                            htp[grp % 2][:, (dc % 4) * 128:((dc % 4) + 1) * 128],
                            h1t[rt % 2][:, dc * 128:(dc + 1) * 128],
                            ident[:, :]))

            @blk.gpsimd
            def _(gpsimd):
                if "d_vecs" in dump_t:
                    gpsimd.wait_ge(s_ld.h, ld1 + 32)
                    s_gl.dma(gpsimd.dma_start(out=dump_t["d_vecs"].ap()[:, :],
                                              in_=vec_s.ap()[:, :]))
                    if "d_bca" in dump_t:
                        s_gl.dma(gpsimd.dma_start(out=dump_t["d_bca"].ap()[:, 0:D],
                                                  in_=bca[:, :]))
                        s_gl.dma(gpsimd.dma_start(out=dump_t["d_bca"].ap()[:, D:2 * D],
                                                  in_=bcb[:, :]))
                    gpsimd.wait_ge(s_gl.h, s_gl.n)
                if "d_h1" in dump_t:
                    gpsimd.wait_ge(s_ac.h, s_ac.n)
                    s_gl.dma(gpsimd.dma_start(out=dump_t["d_h1"].ap()[:, :],
                                              in_=h1fm[:, :]))
                    gpsimd.wait_ge(s_gl.h, s_gl.n)

        with contextlib.ExitStack() as P:
            PE = P.enter_context
            wful = PE(nc.sbuf_tensor("wful", [128, 16 * D], BF16))
            qfm = PE(nc.sbuf_tensor("qfm", [128, 16 * R], BF16))
            vfm = PE(nc.sbuf_tensor("vfm", [128, 16 * R], BF16))
            bqv = PE(nc.sbuf_tensor("bqv", [128, 64], F32))
            on128 = PE(nc.sbuf_tensor("on128", [128, 128], BF16))
            qnp = [PE(nc.sbuf_tensor(f"qnp{i}", [128, 1], F32)) for i in range(2)]
            qntmp = PE(nc.sbuf_tensor("qntmp", [128, 1], F32))
            qnbf = PE(nc.sbuf_tensor("qnbf", [1, 1], BF16))
            pp = PE(nc.psum_tensor("pp", [128, 1024], F32))
            npp = [PE(nc.psum_tensor(f"npp{i}", [128, 512], F32)) for i in range(2)]
            blk = PE(nc.Block())

            mm2, ve2b, gl2b, ac2 = s_mm.n, s_ve.n, s_gl.n, s_ac.n

            @blk.gpsimd
            def _(gpsimd):
                for j, src in enumerate((qb_in, kb_in, vb_in)):
                    s_gl.dma(gpsimd.dma_start(
                        out=bqv[:, j * 16:(j + 1) * 16],
                        in_=bass.AP(src, 0, [[1, 128], [128, 16]]),
                        allow_slow_non_contiguous=True))
                s_gl.one(gpsimd.memset(on128[:, :], 1.0))
                for j, src in enumerate((qT_in, kT_in, vT_in)):
                    for dc in range(16):
                        if j >= 1:
                            gpsimd.wait_ge(s_mm.h,
                                           mm2 + (j - 1) * 512 + 15 * 32 + 16 + dc + 1)
                        s_gl.dma(gpsimd.dma_start(
                            out=wful[:, dc * D:(dc + 1) * D],
                            in_=src.ap()[dc * 128:(dc + 1) * 128, :]))

            @blk.tensor
            def _(tensor):
                for j in range(3):
                    for ft in range(16):
                        idx = j * 16 + ft
                        if j <= 1 and idx >= 1:
                            tensor.wait_ge(s_ve.h, ve2b + idx * 2)
                        elif j == 2 and ft >= 1:
                            tensor.wait_ge(s_ve.h, ve2b + 130 + ft * 2)
                        for rc in range(2):
                            for dc in range(16):
                                if ft == 0 and rc == 0:
                                    tensor.wait_ge(s_gl.h,
                                                   gl2b + 49 + (j * 16 + dc + 1) * 16)
                                s_mm.one(tensor.matmul(
                                    pp[:, rc * 512:(rc + 1) * 512],
                                    wful[:, dc * D + ft * 128: dc * D + (ft + 1) * 128],
                                    h1fm[:, dc * R + rc * 512: dc * R + (rc + 1) * 512],
                                    start=(dc == 0), stop=(dc == 15)))
                    if j == 1:
                        tensor.wait_ge(s_ve.h, ve2b + 64 + 1)
                        for ch in range(32):
                            if ch >= 2:
                                tensor.wait_ge(s_ve.h, ve2b + 65 + (ch - 1) * 2)
                            s_mm.one(tensor.matmul(npp[ch % 2][:, :], on128[:, :],
                                                   vfm[:, ch * 512:(ch + 1) * 512],
                                                   start=True, stop=True))

            @blk.vector
            def _(vector):
                vector.wait_ge(s_gl.h, gl2b + 48)
                vector.tensor_scalar(bqv[:, 48:64], bqv[:, 16:32], SCALE, None,
                                     ALU.mult)

                def mm_of(j, ft):
                    return mm2 + (j * 512 if j <= 1 else 1056) + (ft + 1) * 32

                def evacs(j, dst, bofs, scl):
                    for ft in range(16):
                        vector.wait_ge(s_mm.h, mm_of(j, ft))
                        for rc in range(2):
                            sl = dst[:, ft * R + rc * 512: ft * R + (rc + 1) * 512]
                            if scl is None:
                                inst = vector.tensor_scalar(
                                    sl, pp[:, rc * 512:(rc + 1) * 512],
                                    bqv[:, bofs + ft:bofs + ft + 1], None, ALU.add)
                            else:
                                inst = vector.tensor_scalar(
                                    sl, pp[:, rc * 512:(rc + 1) * 512],
                                    scl, bqv[:, 32 + bofs + ft:33 + bofs + ft],
                                    ALU.mult, ALU.add)
                            s_ve.one(inst)

                evacs(0, qfm, 0, None)
                evacs(1, kfm, 16, SCALE)
                s_ve.one(vector.tensor_tensor(vfm[:, :], qfm[:, :], qfm[:, :], ALU.mult))
                for ch in range(32):
                    vector.wait_ge(s_mm.h, mm2 + 1024 + ch + 1)
                    s_ve.one(vector.tensor_reduce(qntmp[:, :], npp[ch % 2][:, :],
                                                  AX.X, ALU.max))
                    vector.wait_ge(s_ve.h, s_ve.n)
                    if ch == 0:
                        s_ve.one(vector.tensor_copy(qnp[0][:, :], qntmp[:, :]))
                    else:
                        s_ve.one(vector.tensor_tensor(
                            qnp[ch % 2][:, :], qnp[(ch + 1) % 2][:, :],
                            qntmp[:, :], ALU.max))
                        vector.wait_ge(s_ve.h, s_ve.n)
                s_ve.one(vector.tensor_copy(qnbf[:, :], qnp[1][0:1, 0:1]))
                vector.wait_ge(s_mm.h, mm2 + 1056)
                evacs(2, vfm, 32, None)

            @blk.gpsimd
            def _(gpsimd):
                gpsimd.wait_ge(s_ve.h, s_ve.n)
                for ft in range(16):
                    s_gl.dma(gpsimd.dma_start(
                        out=bass.AP(xqv_i, ft * 128 * R, [[R, 128], [1, R]]),
                        in_=qfm[:, ft * R:(ft + 1) * R]))
                    s_gl.dma(gpsimd.dma_start(
                        out=bass.AP(xqv_i, QOFF + ft * 128 * R, [[R, 128], [1, R]]),
                        in_=vfm[:, ft * R:(ft + 1) * R]))
                s_gl.dma(gpsimd.dma_start(
                    out=bass.AP(xqv_i, 2 * QOFF, [[1, 1], [1, 1]]),
                    in_=qnbf[:, :]))
                gpsimd.wait_ge(s_gl.h, s_gl.n)
                s_cc.one(gpsimd.collective_compute(
                    "AllGather", ALU.bypass,
                    replica_groups=[[0, 1, 2, 3], [4, 5, 6, 7]],
                    ins=[xqv_i.ap().opt()], outs=[xqv_o.ap().opt()]))
                s_cc.one(gpsimd.collective_compute(
                    "AllReduce", ALU.add,
                    replica_groups=[[0, 1, 2, 3], [4, 5, 6, 7]],
                    ins=[bass.AP(xqv_i, 0, [[1, 1], [1, 16]]).opt()],
                    outs=[xsy2_o.ap().opt()]))
                gpsimd.wait_ge(s_cc.h, s_cc.n)
                if "d_qfm" in dump_t:
                    s_gl.dma(gpsimd.dma_start(out=dump_t["d_qfm"].ap()[:, :], in_=qfm[:, :]))
                if "d_kfm" in dump_t:
                    s_gl.dma(gpsimd.dma_start(out=dump_t["d_kfm"].ap()[:, :], in_=kfm[:, :]))
                if "d_vfm" in dump_t:
                    s_gl.dma(gpsimd.dma_start(out=dump_t["d_vfm"].ap()[:, :], in_=vfm[:, :]))
                gpsimd.wait_ge(s_gl.h, s_gl.n)

        st_h1.close()

        st_afm = contextlib.ExitStack()
        afm = st_afm.enter_context(nc.sbuf_tensor("afm", [128, 16 * R], BF16, side="left"))
        st_att = contextlib.ExitStack()
        AE = st_att.enter_context
        bounds = AE(nc.sbuf_tensor("bounds", [128, 128], F32, side="left"))
        boundsr = AE(nc.sbuf_tensor("boundsr", [128, 128], F32, side="left"))
        qh = [AE(nc.sbuf_tensor(f"qh{i}", [128, T], BF16, side="left")) for i in range(2)]
        vhfm = [AE(nc.sbuf_tensor(f"vhfm{i}", [128, T], BF16, side="left")) for i in range(2)]
        vrm = [AE(nc.sbuf_tensor(f"vrm{i}", [128, T], BF16, side="left")) for i in range(2)]

        with contextlib.ExitStack() as P:
            PE = P.enter_context
            sqk = PE(nc.sbuf_tensor("sqk", [128, 16 * R], BF16))
            on128b = PE(nc.sbuf_tensor("on128b", [128, 128], BF16))
            normb = PE(nc.sbuf_tensor("normb", [128, 512], F32))
            qmt = PE(nc.sbuf_tensor("qmt", [1, 4], BF16))
            qmf = PE(nc.sbuf_tensor("qmf", [1, 4], F32))
            qms = PE(nc.sbuf_tensor("qms", [1, 1], F32))
            qms2 = PE(nc.sbuf_tensor("qms2", [1, 1], F32))
            qmaxv = PE(nc.sbuf_tensor("qmaxv", [128, 1], F32))
            nppb = [PE(nc.psum_tensor(f"nppb{i}", [128, 512], F32)) for i in range(2)]
            blk = PE(nc.Block())

            mm3, ve3b, ac3, gl3b, ld3 = s_mm.n, s_ve.n, s_ac.n, s_gl.n, s_ld.n

            @blk.gpsimd
            def _(gpsimd):
                s_gl.one(gpsimd.memset(on128b[:, :], 1.0))
                for h in range(H):
                    for ch in range(2):
                        i2 = h * 2 + ch
                        gpsimd.wait_ge(s_ac.h, ac3 + i2 + 1)
                        s_gl.dma(gpsimd.dma_start(
                            out=nrm_s.ap()[h:h + 1, ch * 512:(ch + 1) * 512],
                            in_=normb[0:1, 0:512]))
                s_gl.dma(gpsimd.dma_start(
                    out=qmt[:, :],
                    in_=bass.AP(xqv_o, 2 * QOFF, [[1, 1], [XFLAT, 4]]),
                    allow_slow_non_contiguous=True))

            @blk.vector
            def _(vector):
                vector.wait_ge(s_gl.h, gl3b + 1)
                s_ve.one(vector.tensor_tensor(sqk[:, :], kfm[:, :], kfm[:, :], ALU.mult))
                vector.wait_ge(s_gl.h, gl3b + 1 + 32 * 16 + 16)
                s_ve.one(vector.tensor_copy(qmf[:, :], qmt[:, :]))
                vector.wait_ge(s_ve.h, s_ve.n)
                s_ve.one(vector.tensor_reduce(qms[:, :], qmf[:, :], AX.X, ALU.max))

            @blk.tensor
            def _(tensor):
                tensor.wait_ge(s_ve.h, ve3b + 1)
                for h in range(H):
                    for ch in range(2):
                        i2 = h * 2 + ch
                        if i2 >= 2:
                            tensor.wait_ge(s_ac.h, ac3 + i2 - 1)
                        s_mm.one(tensor.matmul(
                            nppb[i2 % 2][:, :], on128b[:, :],
                            sqk[:, h * R + ch * 512: h * R + (ch + 1) * 512],
                            start=True, stop=True))

            @blk.scalar
            def _(scalar):
                for h in range(H):
                    for ch in range(2):
                        i2 = h * 2 + ch
                        scalar.wait_ge(s_mm.h, mm3 + i2 + 1)
                        if i2 >= 1:
                            scalar.wait_ge(s_gl.h, gl3b + 1 + i2 * 16)
                        s_ac.one(scalar.activation(normb[:, :], nppb[i2 % 2][:, :],
                                                   AF.Sqrt))
                scalar.wait_ge(s_ve.h, ve3b + 3)
                s_ac.one(scalar.activation(qms2[:, :], qms[:, :], AF.Sqrt))

            ac3e = s_ac.n

            @blk.gpsimd
            def _(gpsimd):
                gpsimd.wait_ge(s_ac.h, ac3e)
                s_gl.dma(gpsimd.dma_start(
                    out=bass.AP(vec_s, 9 * D, [[1, 1], [1, 1]]), in_=qms2[:, :]))

            gl3e = s_gl.n

            @blk.sync
            def _(sync):
                for h in range(H):
                    sync.wait_ge(s_gl.h, gl3b + 1 + (h + 1) * 32)
                    s_ld.dma(sync.dma_start(
                        out=boundsr[:, h * 8:(h + 1) * 8],
                        in_=bass.AP(nrm_s, h * R, [[1, 128], [128, 8]]),
                        allow_slow_non_contiguous=True))
                sync.wait_ge(s_gl.h, gl3e)
                s_ld.dma(sync.dma_start(out=qmaxv[:, :],
                                        in_=bass.AP(vec_s, 9 * D, [[0, 128], [1, 1]])))

            ld3e = s_ld.n

            @blk.vector
            def _(vector):
                vector.wait_ge(s_ld.h, ld3e)
                s_ve.one(vector.tensor_scalar(bounds[:, :], boundsr[:, :],
                                              qmaxv[:, :], None, ALU.mult))

            @blk.gpsimd
            def _(gpsimd):
                if "d_bounds" in dump_t:
                    gpsimd.wait_ge(s_ve.h, s_ve.n)
                    s_gl.dma(gpsimd.dma_start(out=dump_t["d_bounds"].ap()[:, :],
                                              in_=bounds[:, :]))
                    gpsimd.wait_ge(s_gl.h, s_gl.n)

        with contextlib.ExitStack() as P:
            PE = P.enter_context
            Tt = [PE(nc.sbuf_tensor(f"Tt{i}", [128, 512], F32)) for i in range(2)]
            Pt = [PE(nc.sbuf_tensor(f"Pt{i}", [128, 512], BF16)) for i in range(2)]
            PTs = [PE(nc.sbuf_tensor(f"PTs{i}", [128, 512], BF16)) for i in range(2)]
            pat = PE(nc.sbuf_tensor("pat", [128, 2 * 512], F16))
            bvs = [PE(nc.sbuf_tensor(f"bvs{i}", [128, 8], F32)) for i in range(2)]
            sums = [PE(nc.sbuf_tensor(f"sums{i}", [128, 8], F32)) for i in range(2)]
            sumt = [PE(nc.sbuf_tensor(f"sumt{i}", [128, 1], F32)) for i in range(2)]
            sumr = [PE(nc.sbuf_tensor(f"sumr{i}", [128, 1], F32)) for i in range(2)]
            arm = [PE(nc.sbuf_tensor(f"arm{i}", [128, 128], BF16)) for i in range(2)]
            S_ps = [PE(nc.psum_tensor(f"S_ps{i}", [128, 512], F32)) for i in range(2)]
            PT_ps = [PE(nc.psum_tensor(f"PT_ps{i}", [128, 1024], BF16)) for i in range(2)]
            O_ps = [PE(nc.psum_tensor(f"O_ps{i}", [128, 512], F32)) for i in range(2)]
            vtp = PE(nc.psum_tensor("vtp", [128, 1024], BF16))
            otp = PE(nc.psum_tensor("otp", [128, 1024], BF16))
            blk = PE(nc.Block())

            pld = s_ld.n + 16

            @blk.sync
            def _(sync):
                s_ld.dma(sync.dma_start(
                    out=pat[:, :],
                    in_=bass.AP(pat_in, 0, [[512, 128], [128 * 512, 2], [1, 512]])))
                for hh in range(H):
                    if hh >= 2:
                        sync.wait_ge(a_s.h, G0[hh - 1])
                        sync.wait_ge(a_av.h, G0[hh - 1])
                    for cp in range(4):
                        a_qv.dma(sync.dma_start(
                            out=bass.AP(qh[hh % 2], cp * 128,
                                        [[T, 128], [512, 8], [1, 128]]),
                            in_=bass.AP(xqv_o, cp * XFLAT + hh * 128 * R,
                                        [[R, 128], [128, 8], [1, 128]])))
                    for cp in range(4):
                        a_qv.dma(sync.dma_start(
                            out=bass.AP(vhfm[hh % 2], cp * 128,
                                        [[T, 128], [512, 8], [1, 128]]),
                            in_=bass.AP(xqv_o, cp * XFLAT + QOFF + hh * 128 * R,
                                        [[R, 128], [128, 8], [1, 128]])))

            vprep_T = {}
            vprep_V = {}
            for h in range(1, H):
                for grp in range(4):
                    vprep_T.setdefault(G0[h - 1] + 8 + 2 * grp, []).append((h, grp))
                    vprep_V.setdefault(G0[h - 1] + 9 + 2 * grp, []).append((h, grp))

            def v_transposes(tensor, h, grp):
                vt_idx = h * 4 + grp
                tensor.wait_ge(a_qv.h, 128 * (h + 1))
                if vt_idx >= 1:
                    tensor.wait_ge(a_ve.h, vt_idx)
                for s8 in range(8):
                    jt = grp * 8 + s8
                    inst = tensor.transpose(
                        vtp[:, s8 * 128:(s8 + 1) * 128],
                        vhfm[h % 2][:, jt * 128:(jt + 1) * 128],
                        ident[:, :])
                inst.then_inc(a_vt.h, 1)
                a_vt.n += 1

            def v_evac(vector, h, grp):
                vt_idx = h * 4 + grp
                vector.wait_ge(a_vt.h, vt_idx + 1)
                if h >= 2:
                    vector.wait_ge(a_av.h, G0[h - 1])
                vector.tensor_copy(vrm[h % 2][:, grp * 1024:(grp + 1) * 1024],
                                   vtp[:, :]).then_inc(a_ve.h, 1)
                a_ve.n += 1

            @blk.tensor
            def _(tensor):
                tensor.wait_ge(s_ld.h, pld)
                for grp in range(4):
                    v_transposes(tensor, 0, grp)

                def emit_S(g):
                    (gg, h, k, c, last) = CHUNKS[g]
                    if g >= 2:
                        tensor.wait_ge(a_t.h, g - 1)
                    tensor.matmul(S_ps[g % 2][:, :],
                                  kfm[:, h * R + k * 128: h * R + (k + 1) * 128],
                                  qh[h % 2][:, c * 512:(c + 1) * 512],
                                  start=True, stop=True).then_inc(a_s.h, 1)
                    a_s.n += 1

                def emit_TR(g):
                    tensor.wait_ge(a_p.h, g + 1)
                    if g >= 2:
                        tensor.wait_ge(a_e.h, g - 1)
                    for t4 in range(4):
                        inst = tensor.transpose(
                            PT_ps[g % 2][:, t4 * 128:(t4 + 1) * 128],
                            Pt[g % 2][:, t4 * 128:(t4 + 1) * 128],
                            ident[:, :])
                    inst.then_inc(a_tr.h, 1)
                    a_tr.n += 1

                def emit_AV(g):
                    (gg, h, k, c, last) = CHUNKS[g]
                    m = M_OF_G[g]
                    tensor.wait_ge(a_e.h, g + 1)
                    if c == 0 and m >= 2:
                        tensor.wait_ge(a_oe.h, m - 1)
                    for t4 in range(4):
                        jt = c * 4 + t4
                        inst = tensor.matmul(
                            O_ps[m % 2][:, 0:128],
                            PTs[g % 2][:, t4 * 128:(t4 + 1) * 128],
                            vrm[h % 2][:, jt * 128:(jt + 1) * 128],
                            start=(c == 0 and t4 == 0), stop=(last and t4 == 3))
                    inst.then_inc(a_av.h, 1)
                    a_av.n += 1
                    if last:
                        tensor.wait_ge(a_oe.h, m + 1)
                        if m >= 1:
                            tensor.wait_ge(a_of.h, m)
                        tensor.transpose(otp[:, 0:128], arm[m % 2][:, :],
                                         ident[:, :]).then_inc(a_ot.h, 1)
                        a_ot.n += 1

                for g in range(G_TOT + 2):
                    if g < G_TOT:
                        emit_S(g)
                    for (h, grp) in vprep_T.get(g, ()):
                        v_transposes(tensor, h, grp)
                    if 1 <= g <= G_TOT:
                        emit_TR(g - 1)
                    if 2 <= g <= G_TOT + 1:
                        emit_AV(g - 2)

            @blk.vector
            def _(vector):
                vector.wait_ge(s_ld.h, pld)
                for grp in range(4):
                    v_evac(vector, 0, grp)

                def emit_stt(g):
                    (gg, h, k, c, last) = CHUNKS[g]
                    m = M_OF_G[g]
                    if c == 0:
                        if m >= 2:
                            vector.wait_ge(a_p.h, ITILES[m - 2][4] + 1)
                        vector.tensor_scalar(bvs[m % 2][:, :], t512[:, :],
                                             bounds[:, h * 8 + k: h * 8 + k + 1],
                                             i0bc[:, k:k + 1],
                                             ALU.subtract, ALU.subtract)
                    vector.wait_ge(a_s.h, g + 1)
                    if g >= 2:
                        vector.wait_ge(a_p.h, g - 1)
                    kp = 1 if last else 0
                    vector.tensor_tensor(Tt[g % 2][:, :], S_ps[g % 2][:, :],
                                         pat[:, kp * 512:(kp + 1) * 512],
                                         ALU.add).then_inc(a_t.h, 1)
                    a_t.n += 1

                def emit_evac_v(g):
                    vector.wait_ge(a_tr.h, g + 1)
                    if g >= 2:
                        vector.wait_ge(a_av.h, g - 1)
                    vector.tensor_copy(PTs[g % 2][:, :],
                                       PT_ps[g % 2][:, 0:512]).then_inc(a_e.h, 1)
                    a_e.n += 1

                def emit_finish(m):
                    (mm_, h, k, nch, g_last) = ITILES[m]
                    vector.wait_ge(a_av.h, g_last + 1)
                    if m >= 2:
                        vector.wait_ge(a_ot.h, m - 1)
                    fence(vector, vector.tensor_reduce(sumr[m % 2][:, :],
                                                       sums[m % 2][:, 0:nch],
                                                       AX.X, ALU.add))
                    fence(vector, vector.reciprocal(sumt[m % 2][:, :],
                                                    sumr[m % 2][:, :]))
                    vector.tensor_scalar(arm[m % 2][:, :], O_ps[m % 2][:, 0:128],
                                         sumt[m % 2][:, :], None,
                                         ALU.mult).then_inc(a_oe.h, 1)
                    a_oe.n += 1

                for g in range(G_TOT + 1):
                    if g < G_TOT:
                        emit_stt(g)
                    for (h, grp) in vprep_V.get(g, ()):
                        v_evac(vector, h, grp)
                    if 1 <= g <= G_TOT and (g - 1) % 2 == 0:
                        emit_evac_v(g - 1)
                    if g >= 1 and CHUNKS[g - 1][4]:
                        emit_finish(M_OF_G[g - 1])

            @blk.scalar
            def _(scalar):
                scalar.wait_ge(s_ld.h, pld)

                def emit_exp(g):
                    (gg, h, k, c, last) = CHUNKS[g]
                    m = M_OF_G[g]
                    if c == 0 and m >= 2:
                        scalar.wait_ge(a_oe.h, m - 1)
                    scalar.wait_ge(a_t.h, g + 1)
                    if g >= 2:
                        scalar.wait_ge(a_tr.h, g - 1)
                    scalar.activation(Pt[g % 2][:, :], Tt[g % 2][:, :], AF.Exp,
                                      bias=bvs[m % 2][:, c:c + 1], scale=1.0,
                                      accum_out=sums[m % 2][:, c:c + 1]).then_inc(a_p.h, 1)
                    a_p.n += 1

                def emit_evac_a(g):
                    scalar.wait_ge(a_tr.h, g + 1)
                    if g >= 2:
                        scalar.wait_ge(a_av.h, g - 1)
                    scalar.copy(PTs[g % 2][:, :],
                                PT_ps[g % 2][:, 0:512]).then_inc(a_e.h, 1)
                    a_e.n += 1

                def emit_afm(m):
                    (mm_, h, k, nch, g_last) = ITILES[m]
                    scalar.wait_ge(a_ot.h, m + 1)
                    scalar.copy(afm[:, h * R + k * 128: h * R + (k + 1) * 128],
                                otp[:, 0:128]).then_inc(a_of.h, 1)
                    a_of.n += 1

                for g in range(G_TOT + 1):
                    if g < G_TOT:
                        emit_exp(g)
                    if 1 <= g <= G_TOT and (g - 1) % 2 == 1:
                        emit_evac_a(g - 1)
                    if g >= 3 and CHUNKS[g - 3][4]:
                        emit_afm(M_OF_G[g - 3])
                for gx in (G_TOT - 2, G_TOT - 1):
                    if CHUNKS[gx][4]:
                        emit_afm(M_OF_G[gx])

            @blk.gpsimd
            def _(gpsimd):
                if "d_afm" in dump_t:
                    gpsimd.wait_ge(a_of.h, M_TOT)
                    s_gl.dma(gpsimd.dma_start(out=dump_t["d_afm"].ap()[:, :],
                                              in_=afm[:, :]))
                    gpsimd.wait_ge(s_gl.h, s_gl.n)

        st_att.close()
        st_kfm.close()

        with contextlib.ExitStack() as P:
            PE = P.enter_context
            x2 = PE(nc.sbuf_tensor("x2", [128, 8 * D], F32))
            ow = PE(nc.sbuf_tensor("ow", [128, 16 * D], BF16))
            bga = PE(nc.sbuf_tensor("bga", [128, D], F32))
            bob = PE(nc.sbuf_tensor("bob", [128, D], F32))
            t1 = PE(nc.sbuf_tensor("t1", [128, 512], F32))
            t2 = PE(nc.sbuf_tensor("t2", [128, 512], F32))
            po = PE(nc.psum_tensor("po", [128, 8 * 512], F32))
            blk = PE(nc.Block())

            mm4, ve4, gl4, ld4 = s_mm.n, s_ve.n, s_gl.n, s_ld.n

            @blk.gpsimd
            def _(gpsimd):
                for dc in range(16):
                    s_gl.dma(gpsimd.dma_start(
                        out=ow[:, dc * D:(dc + 1) * D],
                        in_=oT_in.ap()[dc * 128:(dc + 1) * 128, :]))

            @blk.sync
            def _(sync):
                s_ld.dma(sync.dma_start(out=bga[:, :],
                                        in_=bass.AP(vec_s, 2 * D, [[0, 128], [1, D]])))
                s_ld.dma(sync.dma_start(out=bob[:, :],
                                        in_=bass.AP(ob_in, 0, [[0, 128], [1, D]])))
                for rt in range(8):
                    s_ld.dma(sync.dma_start(
                        out=x2[:, rt * D:(rt + 1) * D],
                        in_=x_in.ap()[rt * 128:(rt + 1) * 128, :]))

            @blk.tensor
            def _(tensor):
                tensor.wait_ge(s_gl.h, gl4 + 256)
                for fg in range(4):
                    if fg >= 1:
                        tensor.wait_ge(s_ve.h, ve4 + fg * 8)
                    for dc in range(16):
                        for rt in range(8):
                            s_mm.one(tensor.matmul(
                                po[:, rt * 512:(rt + 1) * 512],
                                afm[:, dc * R + rt * 128: dc * R + (rt + 1) * 128],
                                ow[:, dc * D + fg * 512: dc * D + (fg + 1) * 512],
                                start=(dc == 0), stop=(dc == 15)))

            @blk.vector
            def _(vector):
                vector.wait_ge(s_ld.h, ld4 + 16 * 10)
                for fg in range(4):
                    vector.wait_ge(s_mm.h, mm4 + (fg + 1) * 128)
                    for rt in range(8):
                        vector.tensor_tensor(t1[:, :], po[:, rt * 512:(rt + 1) * 512],
                                             bob[:, fg * 512:(fg + 1) * 512], ALU.add)
                        vector.tensor_tensor(t2[:, :], t1[:, :],
                                             bga[:, fg * 512:(fg + 1) * 512], ALU.mult)
                        xsl = x2[:, rt * D + fg * 512: rt * D + (fg + 1) * 512]
                        s_ve.one(vector.tensor_tensor(xsl, xsl, t2[:, :], ALU.add))

            ve4e = s_ve.n

            @blk.gpsimd
            def _(gpsimd):
                gpsimd.wait_ge(s_ve.h, ve4e)
                for rt in range(8):
                    s_gl.dma(gpsimd.dma_start(out=x2_d.ap()[rt * 128:(rt + 1) * 128, :],
                                              in_=x2[:, rt * D:(rt + 1) * D]))
                gpsimd.wait_ge(s_gl.h, s_gl.n)
                if "d_x2" in dump_t:
                    s_gl.dma(gpsimd.dma_start(out=dump_t["d_x2"].ap()[:, :], in_=x2[:, :]))
                    gpsimd.wait_ge(s_gl.h, s_gl.n)

        st_afm.close()

        st_qx = contextlib.ExitStack()
        qxfm = st_qx.enter_context(nc.sbuf_tensor("qxfm", [128, 16 * R], BF16, side="left"))

        with contextlib.ExitStack() as P:
            PE = P.enter_context
            bcb1 = PE(nc.sbuf_tensor("bcb1", [128, D], F32))
            bsb = PE(nc.sbuf_tensor("bsb", [128, D], F32))
            lnt2 = PE(nc.sbuf_tensor("lnt2", [128, D], F32))
            h2t = PE(nc.sbuf_tensor("h2t", [128, D], F32))
            q_rm = [PE(nc.sbuf_tensor(f"q_rm{i}", [128, D], BF16)) for i in range(2)]
            qi = PE(nc.sbuf_tensor("qi", [128, D], I32))
            bnb2 = PE(nc.sbuf_tensor("bnb2", [128, 24], F32))
            c6a = PE(nc.sbuf_tensor("c6a", [128, 6], F32))
            c6b = PE(nc.sbuf_tensor("c6b", [128, 6], F32))
            c6c = PE(nc.sbuf_tensor("c6c", [128, 6], F32))
            sq24b = PE(nc.sbuf_tensor("sq24b", [128, 24], F32))
            d6a = PE(nc.sbuf_tensor("d6a", [128, 6], F32))
            d6b = PE(nc.sbuf_tensor("d6b", [128, 6], F32))
            d6c = PE(nc.sbuf_tensor("d6c", [128, 6], F32))
            vvs2 = PE(nc.sbuf_tensor("vvs2", [128, 12], F32))
            lnu2 = PE(nc.sbuf_tensor("lnu2", [128, D], F32))
            amx2 = PE(nc.sbuf_tensor("amx2", [128, 1], F32))
            mv2 = PE(nc.sbuf_tensor("mv2", [128, 1], F32))
            vva = [PE(nc.sbuf_tensor(f"vva{i}", [128, 1], F32)) for i in range(2)]
            sqa = [PE(nc.sbuf_tensor(f"sqa{i}", [128, 1], F32)) for i in range(2)]
            rst2 = PE(nc.sbuf_tensor("rst2", [128, 1], F32))
            amx = PE(nc.sbuf_tensor("amx", [128, 1], F32))
            r1v = PE(nc.sbuf_tensor("r1v", [128, 1], F32))
            htq = [PE(nc.psum_tensor(f"htq{i}", [128, 1024], BF16)) for i in range(2)]
            x2t = [PE(nc.sbuf_tensor(f"x2t{i}", [128, D], F32)) for i in range(2)]
            blk = PE(nc.Block())

            ve5, ac5, mm5, ld5 = s_ve.n, s_ac.n, s_mm.n, s_ld.n

            @blk.sync
            def _(sync):
                s_ld.dma(sync.dma_start(out=bcb1[:, :],
                                        in_=bass.AP(vec_s, 4 * D, [[0, 128], [1, D]])))
                s_ld.dma(sync.dma_start(out=bsb[:, :],
                                        in_=bass.AP(vec_s, 3 * D, [[0, 128], [1, D]])))
                for rt in range(8):
                    if rt >= 2:
                        sync.wait_ge(s_ve.h, ve5 + (rt - 1) * 2)
                    s_ld.dma(sync.dma_start(out=x2t[rt % 2][:, :],
                                            in_=x2_d.ap()[rt * 128:(rt + 1) * 128, :]))

            @blk.vector
            def _(vector):
                for rt in range(8):
                    xx = x2t[rt % 2][:, :]
                    vector.wait_ge(s_ld.h, ld5 + 32 + (rt + 1) * 16)
                    if rt >= 2:
                        vector.wait_ge(s_mm.h, mm5 + (rt - 1) * 16)
                    for q in range(4):
                        vector.bn_stats(bnb2[:, q * 6:(q + 1) * 6],
                                        xx[:, q * 512:(q + 1) * 512])
                    vector.tensor_tensor(c6a[:, :], bnb2[:, 0:6], bnb2[:, 6:12], ALU.add)
                    vector.tensor_tensor(c6b[:, :], bnb2[:, 12:18], bnb2[:, 18:24], ALU.add)
                    vector.tensor_tensor(sq24b[:, :], bnb2[:, :], bnb2[:, :], ALU.mult)
                    vector.tensor_tensor(c6c[:, :], c6a[:, :], c6b[:, :], ALU.add)
                    vector.tensor_tensor(d6a[:, :], sq24b[:, 0:6], sq24b[:, 6:12], ALU.add)
                    vector.tensor_tensor(d6b[:, :], sq24b[:, 12:18], sq24b[:, 18:24], ALU.add)
                    vector.tensor_tensor(vvs2[:, 0:1], c6c[:, 1:2], c6c[:, 4:5], ALU.add)
                    vector.tensor_tensor(d6c[:, :], d6a[:, :], d6b[:, :], ALU.add)
                    vector.tensor_tensor(vvs2[:, 1:2], c6c[:, 2:3], c6c[:, 5:6], ALU.add)
                    vector.tensor_scalar(mv2[:, :], vvs2[:, 0:1], 1.0 / 8.0, None, ALU.mult)
                    vector.tensor_tensor(vvs2[:, 3:4], d6c[:, 1:2], d6c[:, 4:5], ALU.add)
                    vector.tensor_scalar(vvs2[:, 2:3], vvs2[:, 1:2], 1.0 / D, None, ALU.mult)
                    vector.tensor_scalar(vvs2[:, 4:5], vvs2[:, 3:4], 1.0 / 8.0, None, ALU.mult)
                    vector.tensor_tensor(vvs2[:, 6:7], mv2[:, :], mv2[:, :], ALU.mult)
                    fence(vector, vector.tensor_tensor(vvs2[:, 5:6], vvs2[:, 2:3],
                                                       vvs2[:, 4:5], ALU.add))
                    fence(vector, vector.tensor_tensor(vvs2[:, 7:8], vvs2[:, 5:6],
                                                       vvs2[:, 6:7], ALU.subtract))
                    s_ve.one(vector.tensor_scalar(vva[rt % 2][:, :], vvs2[:, 7:8],
                                                  1.0, EPS, ALU.mult, ALU.add))
                    vector.wait_ge(s_ac.h, ac5 + (rt + 1 if rt <= 1 else 5 * rt - 3))
                    fence(vector, vector.reciprocal(rst2[:, :], sqa[rt % 2][:, :]))
                    vector.tensor_scalar(lnt2[:, :], xx[:, :], mv2[:, :], rst2[:, :],
                                         ALU.subtract, ALU.mult)
                    vector.tensor_tensor(lnu2[:, :], lnt2[:, :], bcb1[:, :], ALU.mult)
                    vector.tensor_tensor(h2t[:, :], lnu2[:, :], bsb[:, :], ALU.add)
                    fence(vector, vector.tensor_reduce(amx[:, :], h2t[:, :], AX.X,
                                                       ALU.max,
                                                       apply_absolute_value=True))
                    fence(vector, vector.tensor_scalar(amx2[:, :], amx[:, :], TINY,
                                                       None, ALU.max))
                    fence(vector, vector.reciprocal(r1v[:, :], amx2[:, :]))
                    vector.tensor_scalar(sxv[:, rt:rt + 1], amx2[:, :], 1.0 / QMAX,
                                         None, ALU.mult)
                    vector.tensor_scalar(lnu2[:, :], h2t[:, :], r1v[:, :], QMAX,
                                         ALU.mult, ALU.mult)
                    vector.tensor_copy(qi[:, :], lnu2[:, :])
                    s_ve.one(vector.tensor_scalar(q_rm[rt % 2][:, :], qi[:, :],
                                                  QMAX, -QMAX, ALU.min, ALU.max))

            evac_cnt5 = {}

            @blk.scalar
            def _(scalar):
                def evacs(rt):
                    for grp in range(4):
                        scalar.wait_ge(s_mm.h, mm5 + rt * 16 + (grp + 1) * 4)
                        s_ac.one(scalar.copy(
                            bass.AP(qxfm, (4 * grp) * R + rt * 128,
                                    [[16 * R, 128], [R, 4], [1, 128]]),
                            htq[grp % 2][:, 0:512]))
                        evac_cnt5[(rt, grp)] = s_ac.n
                for rt in range(8):
                    scalar.wait_ge(s_ve.h, ve5 + rt * 2 + 1)
                    s_ac.one(scalar.activation(sqa[rt % 2][:, :], vva[rt % 2][:, :],
                                               AF.Sqrt))
                    if rt >= 1:
                        evacs(rt - 1)
                evacs(7)

            @blk.tensor
            def _(tensor):
                for rt in range(8):
                    tensor.wait_ge(s_ve.h, ve5 + rt * 2 + 2)
                    for dc in range(16):
                        grp = dc // 4
                        gidx = rt * 4 + grp
                        if gidx >= 2 and dc % 4 == 0:
                            prt, pgrp = (gidx - 2) // 4, (gidx - 2) % 4
                            tensor.wait_ge(s_ac.h, evac_cnt5[(prt, pgrp)])
                        s_mm.one(tensor.transpose(
                            htq[grp % 2][:, (dc % 4) * 128:((dc % 4) + 1) * 128],
                            q_rm[rt % 2][:, dc * 128:(dc + 1) * 128],
                            ident[:, :]))

            @blk.gpsimd
            def _(gpsimd):
                if "d_qx" in dump_t:
                    gpsimd.wait_ge(s_ac.h, s_ac.n)
                    s_gl.dma(gpsimd.dma_start(out=dump_t["d_qx"].ap()[:, :], in_=qxfm[:, :]))
                    gpsimd.wait_ge(s_gl.h, s_gl.n)

        with contextlib.ExitStack() as P:
            PE = P.enter_context
            akv = PE(nc.sbuf_tensor("akv", [128, 16], F32))
            rkv = PE(nc.sbuf_tensor("rkv", [128, 16], F32))
            skv = PE(nc.sbuf_tensor("skv", [128, 16], F32))
            brk = PE(nc.sbuf_tensor("brk", [128, D], F32))
            qk = PE(nc.sbuf_tensor("qk", [128, 16 * D], BF16))
            ki32 = PE(nc.sbuf_tensor("ki32", [128, D], I32))
            p6 = PE(nc.psum_tensor("p6", [128, 8 * 512], F32))
            P6i = PE(contextlib.ExitStack())
            kt = [P6i.enter_context(nc.sbuf_tensor(f"kt{i}", [128, D], F32))
                  for i in range(2)]
            blk = P6i.enter_context(nc.Block())

            ve6, gl6, ld6, mm6, st6 = s_ve.n, s_gl.n, s_ld.n, s_mm.n, s_st.n

            @blk.sync
            def _(sync):
                for ft in range(16):
                    if ft >= 2:
                        sync.wait_ge(s_ve.h, ve6 + ft - 1)
                    s_ld.dma(sync.dma_start(out=kt[ft % 2][:, :],
                                            in_=fkT_in.ap()[ft * 128:(ft + 1) * 128, :]))

            @blk.vector
            def _(vector):
                for ft in range(16):
                    vector.wait_ge(s_ld.h, ld6 + (ft + 1) * 16)
                    s_ve.one(vector.tensor_reduce(
                        akv[:, ft:ft + 1], kt[ft % 2][:, :], AX.X, ALU.max,
                        apply_absolute_value=True))
                    if ft == 15:
                        vector.wait_ge(s_ve.h, s_ve.n)
                s_ve.one(vector.tensor_scalar(skv[:, :], akv[:, :], TINY, 1.0 / QMAX,
                                              ALU.max, ALU.mult))
                vector.wait_ge(s_ve.h, s_ve.n)
                s_ve.one(vector.reciprocal(rkv[:, :], skv[:, :]))

            @blk.gpsimd
            def _(gpsimd):
                gpsimd.wait_ge(s_ve.h, ve6 + 18)
                s_gl.dma(gpsimd.dma_start(
                    out=bass.AP(vec_s, 7 * D, [[1, 128], [128, 16]]), in_=skv[:, :],
                    allow_slow_non_contiguous=True))
                s_gl.dma(gpsimd.dma_start(
                    out=bass.AP(vec_s, 8 * D, [[1, 128], [128, 16]]), in_=rkv[:, :],
                    allow_slow_non_contiguous=True))

            @blk.sync
            def _(sync):
                sync.wait_ge(s_gl.h, gl6 + 32)
                s_ld.dma(sync.dma_start(out=brk[:, :],
                                        in_=bass.AP(vec_s, 8 * D, [[0, 128], [1, D]])))
                for dc in range(16):
                    if dc >= 2:
                        sync.wait_ge(s_ve.h, ve6 + 18 + dc - 1)
                    s_ld.dma(sync.dma_start(out=kt[dc % 2][:, :],
                                            in_=fk_in.ap()[dc * 128:(dc + 1) * 128, :]))

            @blk.vector
            def _(vector):
                for dc in range(16):
                    vector.wait_ge(s_ld.h, ld6 + 16 * 16 + 16 + (dc + 1) * 16)
                    vector.tensor_tensor(ki32[:, :], kt[dc % 2][:, :], brk[:, :], ALU.mult)
                    s_ve.one(vector.tensor_scalar(qk[:, dc * D:(dc + 1) * D], ki32[:, :],
                                                  QMAX, -QMAX, ALU.min, ALU.max))

            P6i.close()
            bsk = PE(nc.sbuf_tensor("bsk", [128, D], F32))
            bbq = PE(nc.sbuf_tensor("bbq", [128, D], F32))
            bgb = PE(nc.sbuf_tensor("bgb", [128, D], F32))
            t3 = PE(nc.sbuf_tensor("t3", [128, D], F32))
            t4 = PE(nc.sbuf_tensor("t4", [128, D], F32))
            t5 = PE(nc.sbuf_tensor("t5", [128, D], F32))
            ay2 = PE(nc.sbuf_tensor("ay2", [128, 1], F32))
            x2u = [PE(nc.sbuf_tensor(f"x2u{i}", [128, D], F32)) for i in range(2)]
            outt = [PE(nc.sbuf_tensor(f"outt{i}", [128, D], F32)) for i in range(2)]
            ay = PE(nc.sbuf_tensor("ay", [128, 1], F32))
            syv = PE(nc.sbuf_tensor("syv", [128, 1], F32))
            ryv = PE(nc.sbuf_tensor("ryv", [128, 1], F32))
            blk2 = PE(nc.Block())
            ld6b = s_ld.n

            @blk2.sync
            def _(sync):
                s_ld.dma(sync.dma_start(out=bsk[:, :],
                                        in_=bass.AP(vec_s, 7 * D, [[0, 128], [1, D]])))
                s_ld.dma(sync.dma_start(out=bbq[:, :],
                                        in_=bass.AP(vec_s, 6 * D, [[0, 128], [1, D]])))
                s_ld.dma(sync.dma_start(out=bgb[:, :],
                                        in_=bass.AP(vec_s, 5 * D, [[0, 128], [1, D]])))
                for rt in range(8):
                    if rt >= 2:
                        sync.wait_ge(s_ve.h, ve6 + 34 + rt - 1)
                    s_ld.dma(sync.dma_start(out=x2u[rt % 2][:, :],
                                            in_=x2_d.ap()[rt * 128:(rt + 1) * 128, :]))

            @blk2.tensor
            def _(tensor):
                for rt in range(8):
                    half = rt % 2
                    if rt >= 2:
                        tensor.wait_ge(s_ve.h, ve6 + 34 + (rt - 1))
                    for dc in range(16):
                        for fg in range(4):
                            s_mm.one(tensor.matmul(
                                p6[:, (half * 4 + fg) * 512:(half * 4 + fg + 1) * 512],
                                qxfm[:, dc * R + rt * 128: dc * R + (rt + 1) * 128],
                                qk[:, dc * D + fg * 512: dc * D + (fg + 1) * 512],
                                start=(dc == 0), stop=(dc == 15)))

            @blk2.vector
            def _(vector):
                for rt in range(8):
                    half = rt % 2
                    vector.wait_ge(s_mm.h, mm6 + (rt + 1) * 64)
                    vector.wait_ge(s_ld.h, ld6b + 48 + (rt + 1) * 16)
                    if rt >= 2:
                        vector.wait_ge(s_st.h, st6 + (rt - 1) * 16)
                    psl = p6[:, half * 2048:(half + 1) * 2048]
                    vector.scalar_tensor_tensor(t3[:, :], psl, sxv[:, rt:rt + 1],
                                                bsk[:, :], ALU.mult, ALU.mult)
                    vector.tensor_tensor(t4[:, :], t3[:, :], bbq[:, :], ALU.add)
                    vector.tensor_scalar(t5[:, :], t4[:, :], 0.0, None, ALU.max)
                    fence(vector, vector.tensor_reduce(ay[:, :], t5[:, :], AX.X,
                                                       ALU.max,
                                                       apply_absolute_value=True))
                    fence(vector, vector.tensor_scalar(ay2[:, :], ay[:, :], TINY,
                                                       None, ALU.max))
                    fence(vector, vector.reciprocal(ryv[:, :], ay2[:, :]))
                    vector.tensor_scalar(syv[:, :], ay2[:, :], 1.0 / QMAX, None, ALU.mult)
                    vector.tensor_scalar(t3[:, :], t5[:, :], ryv[:, :], QMAX,
                                         ALU.mult, ALU.mult)
                    vector.tensor_copy(ki32[:, :], t3[:, :])
                    vector.tensor_scalar(t4[:, :], ki32[:, :], QMAX, -QMAX,
                                         ALU.min, ALU.max)
                    vector.scalar_tensor_tensor(t3[:, :], t4[:, :], syv[:, :],
                                                bgb[:, :], ALU.mult, ALU.mult)
                    s_ve.one(vector.tensor_tensor(outt[rt % 2][:, :],
                                                  x2u[rt % 2][:, :],
                                                  t3[:, :], ALU.add))

            @blk2.gpsimd
            def _(gpsimd):
                for rt in range(8):
                    gpsimd.wait_ge(s_ve.h, ve6 + 34 + rt + 1)
                    s_st.dma(gpsimd.dma_start(out=out_t.ap()[rt * 128:(rt + 1) * 128, :],
                                              in_=outt[rt % 2][:, :]))
                gpsimd.wait_ge(s_st.h, s_st.n)

        st_qx.close()

    return nc


def _host_consts():
    ident = np.eye(128, dtype=ml_dtypes.bfloat16)
    jj = np.arange(512, dtype=np.float32)[None, :]
    ip = np.arange(128, dtype=np.float32)[:, None]
    pat_off = (jj - ip).astype(np.float32)
    t512 = np.broadcast_to(512.0 * np.arange(8, dtype=np.float32)[None, :],
                           (128, 8)).copy()
    return ident, pat_off, t512


def _run(dump, L0_attn_x, L0_attn_q_w, L0_attn_q_b, L0_attn_k_w, L0_attn_k_b,
         L0_attn_v_w, L0_attn_v_b, L0_attn_o_w, L0_attn_o_b,
         L0_ffn_kernel, L0_ffn_bias, L0_c, L0_wmod):
    ident, pat_off, t512 = _host_consts()
    x = np.ascontiguousarray(np.asarray(L0_attn_x, dtype=np.float32).reshape(B * T, D))
    qT = np.ascontiguousarray(np.asarray(L0_attn_q_w, dtype=np.float32).T)
    kT = np.ascontiguousarray(np.asarray(L0_attn_k_w, dtype=np.float32).T)
    vT = np.ascontiguousarray(np.asarray(L0_attn_v_w, dtype=np.float32).T)
    oT = np.ascontiguousarray(np.asarray(L0_attn_o_w, dtype=np.float32).T)
    fk = np.ascontiguousarray(np.asarray(L0_ffn_kernel, dtype=np.float32))
    fkT = np.ascontiguousarray(fk.T)
    qb = np.asarray(L0_attn_q_b, dtype=np.float32).reshape(1, D)
    kb = np.asarray(L0_attn_k_b, dtype=np.float32).reshape(1, D)
    vb = np.asarray(L0_attn_v_b, dtype=np.float32).reshape(1, D)
    ob = np.asarray(L0_attn_o_b, dtype=np.float32).reshape(1, D)
    fb = np.asarray(L0_ffn_bias, dtype=np.float32).reshape(1, D)
    cc = np.asarray(L0_c, dtype=np.float32)
    wm = np.ascontiguousarray(np.asarray(L0_wmod, dtype=np.float32))

    nc = build_nc(dump)
    in_maps = []
    for core in range(NC):
        b, cp = core // 4, core % 4
        gtiles = cp + 4 * np.arange(8)
        rows = (b * T + (gtiles[:, None] * 128 + np.arange(128)[None, :])).reshape(-1)
        x_own = np.ascontiguousarray(x[rows])
        jj = np.arange(512, dtype=np.float32)[None, :]
        ip = np.arange(128, dtype=np.float32)[:, None]
        pat_diag = np.where(jj > 128.0 * cp + ip, np.float32(NEG), jj - ip)
        pats = np.stack([pat_off, pat_diag]).astype(np.float16)
        i0 = (512.0 * np.arange(8) + 128.0 * cp).astype(np.float32)
        i0bc = np.broadcast_to(i0[None, :], (128, 8)).copy()
        in_maps.append({
            "x_own": x_own, "qT": qT, "kT": kT, "vT": vT,
            "qb": qb, "kb": kb, "vb": vb, "oT": oT, "ob": ob,
            "fk": fk, "fkT": fkT, "fb": fb,
            "c_own": np.ascontiguousarray(cc[b].reshape(1, D)),
            "wm_own": np.ascontiguousarray(wm[:, cp * 3072:(cp + 1) * 3072]),
            "ident": ident, "pats": pats,
            "t512c": t512, "i0bc": i0bc,
        })
    res = run_bass_kernel_spmd(nc, in_maps, core_ids=list(range(NC)))
    return res


def core_rows(core):
    b, cp = core // 4, core % 4
    gtiles = cp + 4 * np.arange(8)
    return (b * T + (gtiles[:, None] * 128 + np.arange(128)[None, :])).reshape(-1)


def kernel(**inputs):
    res = _run((), **inputs)
    out = np.empty((B * T, D), dtype=np.float32)
    for core in range(NC):
        out[core_rows(core)] = res.results[core]["out"]
    return out.reshape(B, T, D)


# version 04
# speedup vs baseline: 1.0398x; 1.0398x over previous
import contextlib
import numpy as np
import ml_dtypes
import concourse.bass as bass
import concourse.mybir as mybir
from concourse.bass_utils import run_bass_kernel_spmd

F32 = mybir.dt.float32
F16 = mybir.dt.float16
BF16 = mybir.dt.bfloat16
I32 = mybir.dt.int32
AF = mybir.ActivationFunctionType
ALU = mybir.AluOpType
AX = mybir.AxisListType

B, T, D, H, HD = 2, 4096, 2048, 16, 128
R = 1024
NC = 8
SCALE = HD ** -0.5
EPS = 1e-5
NEG = -60000.0
TINY = 1e-30
QMAX = 127.0
QOFF = D * R
XFLAT = 2 * QOFF + 128

CHUNKS = []
ITILES = []
for _h in range(H):
    for _k in range(8):
        for _c in range(_k + 1):
            CHUNKS.append((len(CHUNKS), _h, _k, _c, _c == _k))
        ITILES.append((len(ITILES), _h, _k, _k + 1, len(CHUNKS) - 1))
G_TOT = len(CHUNKS)
M_TOT = len(ITILES)
M_OF_G = {}
for (_m, _h, _k, _nch, _gl) in ITILES:
    for _c in range(_nch):
        M_OF_G[_gl - _nch + 1 + _c] = _m
G0 = {h: 36 * h for h in range(H)}


class Sem:

    def __init__(self, handle):
        self.h = handle
        self.n = 0

    def dma(self, inst):
        inst.then_inc(self.h, 16)
        self.n += 16
        return inst

    def one(self, inst):
        inst.then_inc(self.h, 1)
        self.n += 1
        return inst


def build_nc(dump=()):
    nc = bass.Bass("TRN2", target_bir_lowering=False)

    x_in = nc.dram_tensor("x_own", [R, D], F32, kind="ExternalInput")
    qT_in = nc.dram_tensor("qT", [D, D], F32, kind="ExternalInput")
    kT_in = nc.dram_tensor("kT", [D, D], F32, kind="ExternalInput")
    vT_in = nc.dram_tensor("vT", [D, D], F32, kind="ExternalInput")
    qb_in = nc.dram_tensor("qb", [1, D], F32, kind="ExternalInput")
    kb_in = nc.dram_tensor("kb", [1, D], F32, kind="ExternalInput")
    vb_in = nc.dram_tensor("vb", [1, D], F32, kind="ExternalInput")
    oT_in = nc.dram_tensor("oT", [D, D], F32, kind="ExternalInput")
    ob_in = nc.dram_tensor("ob", [1, D], F32, kind="ExternalInput")
    fk_in = nc.dram_tensor("fk", [D, D], F32, kind="ExternalInput")
    fkT_in = nc.dram_tensor("fkT", [D, D], F32, kind="ExternalInput")
    fb_in = nc.dram_tensor("fb", [1, D], F32, kind="ExternalInput")
    c_in = nc.dram_tensor("c_own", [1, D], F32, kind="ExternalInput")
    wm_in = nc.dram_tensor("wm_own", [D, 3072], F32, kind="ExternalInput")
    ident_in = nc.dram_tensor("ident", [128, 128], BF16, kind="ExternalInput")
    pat_in = nc.dram_tensor("pats", [2, 128, 512], F16, kind="ExternalInput")
    t512_in = nc.dram_tensor("t512c", [128, 8], F32, kind="ExternalInput")
    i0bc_in = nc.dram_tensor("i0bc", [128, 8], F32, kind="ExternalInput")
    out_t = nc.dram_tensor("out", [R, D], F32, kind="ExternalOutput")

    xmod_i = nc.dram_tensor("xmod_i", [1, 3072], F32)
    xmod_o = nc.dram_tensor("xmod_o", [4, 3072], F32)
    xqv_i = nc.dram_tensor("xqv_i", [1, XFLAT], BF16)
    xqv_o = nc.dram_tensor("xqv_o", [4, XFLAT], BF16)
    xsy_o = nc.dram_tensor("xsy_o", [1, 16], F32)
    xsy2_o = nc.dram_tensor("xsy2_o", [1, 16], BF16)
    vec_s = nc.dram_tensor("vec_s", [16, D], F32)
    nrm_s = nc.dram_tensor("nrm_s", [H, R], F32)
    x2_d = nc.dram_tensor("x2_d", [R, D], F32)

    dump_t = {}
    for name, shp, dt in (("d_h1", [128, 16 * R], BF16), ("d_qfm", [128, 16 * R], BF16),
                          ("d_kfm", [128, 16 * R], BF16), ("d_vfm", [128, 16 * R], BF16),
                          ("d_bounds", [128, 128], F32), ("d_afm", [128, 16 * R], BF16),
                          ("d_x2", [128, 8 * D], F32), ("d_qx", [128, 16 * R], BF16),
                          ("d_mod", [4, 3072], F32), ("d_vecs", [16, D], F32),
                          ("d_bca", [128, 2 * D], F32)):
        if name in dump:
            dump_t[name] = nc.dram_tensor(name, shp, dt, kind="ExternalOutput")

    ES = contextlib.ExitStack()
    with ES:
        E = ES.enter_context

        vf = Sem(E(nc.semaphore("vf")))

        def fence(eng, inst):
            vf.n += 1
            inst.then_inc(vf.h, 1)
            eng.wait_ge(vf.h, vf.n)
            return inst

        s_ld = Sem(E(nc.semaphore("s_ld")))
        s_gl = Sem(E(nc.semaphore("s_gl")))
        s_cc = Sem(E(nc.semaphore("s_cc")))
        s_mm = Sem(E(nc.semaphore("s_mm")))
        s_ve = Sem(E(nc.semaphore("s_ve")))
        s_ac = Sem(E(nc.semaphore("s_ac")))
        s_st = Sem(E(nc.semaphore("s_st")))
        a_s = Sem(E(nc.semaphore("a_s")))
        a_t = Sem(E(nc.semaphore("a_t")))
        a_p = Sem(E(nc.semaphore("a_p")))
        a_tr = Sem(E(nc.semaphore("a_tr")))
        a_e = Sem(E(nc.semaphore("a_e")))
        a_av = Sem(E(nc.semaphore("a_av")))
        a_oe = Sem(E(nc.semaphore("a_oe")))
        a_ot = Sem(E(nc.semaphore("a_ot")))
        a_of = Sem(E(nc.semaphore("a_of")))
        a_qv = Sem(E(nc.semaphore("a_qv")))
        a_vt = Sem(E(nc.semaphore("a_vt")))
        a_ve = Sem(E(nc.semaphore("a_ve")))

        ident = E(nc.sbuf_tensor("identt", [128, 128], BF16))
        t512 = E(nc.sbuf_tensor("t512t", [128, 8], F32))
        i0bc = E(nc.sbuf_tensor("i0bct", [128, 8], F32))
        sxv = E(nc.sbuf_tensor("sxv", [128, 8], F32))

        st_h1 = contextlib.ExitStack()
        h1fm = st_h1.enter_context(nc.sbuf_tensor("h1fm", [128, 16 * R], BF16, side="left"))
        st_kfm = contextlib.ExitStack()
        kfm = st_kfm.enter_context(nc.sbuf_tensor("kfm", [128, 16 * R], BF16, side="right"))

        with contextlib.ExitStack() as P:
            PE = P.enter_context
            cvec = PE(nc.sbuf_tensor("cvec", [128, 16], F32))
            csil = PE(nc.sbuf_tensor("csil", [128, 16], BF16))
            wmt = [PE(nc.sbuf_tensor(f"wmt{i}", [128, 3072], BF16)) for i in range(2)]
            modv = PE(nc.sbuf_tensor("modv", [1, 3072], F32))
            mps = [PE(nc.psum_tensor(f"mps{j}", [1, 512], F32)) for j in range(6)]
            blk = PE(nc.Block())

            @blk.sync
            def _(sync):
                s_ld.dma(sync.dma_start(out=ident[:, :], in_=ident_in.ap()[:, :]))
                s_ld.dma(sync.dma_start(out=t512[:, :], in_=t512_in.ap()[:, :]))
                s_ld.dma(sync.dma_start(out=i0bc[:, :], in_=i0bc_in.ap()[:, :]))
                s_ld.dma(sync.dma_start(out=cvec[:, :],
                                        in_=bass.AP(c_in, 0, [[1, 128], [128, 16]]), allow_slow_non_contiguous=True))

            ld0, mm0, gl0 = s_ld.n, s_mm.n, s_gl.n

            @blk.gpsimd
            def _(gpsimd):
                for dc in range(16):
                    if dc >= 2:
                        gpsimd.wait_ge(s_mm.h, mm0 + (dc - 1) * 6)
                    s_gl.dma(gpsimd.dma_start(
                        out=wmt[dc % 2][:, :],
                        in_=wm_in.ap()[dc * 128:(dc + 1) * 128, :]))

            @blk.scalar
            def _(scalar):
                scalar.wait_ge(s_ld.h, ld0)
                s_ac.one(scalar.activation(csil[:, :], cvec[:, :], AF.Silu))

            ac0 = s_ac.n

            @blk.tensor
            def _(tensor):
                tensor.wait_ge(s_ac.h, ac0)
                for dc in range(16):
                    tensor.wait_ge(s_gl.h, gl0 + (dc + 1) * 16)
                    for j in range(6):
                        s_mm.one(tensor.matmul(
                            mps[j][:, :],
                            csil[:, dc:dc + 1],
                            wmt[dc % 2][:, j * 512:(j + 1) * 512],
                            start=(dc == 0), stop=(dc == 15)))

            mm0e = s_mm.n

            @blk.vector
            def _(vector):
                vector.wait_ge(s_mm.h, mm0e)
                for j in range(6):
                    s_ve.one(vector.tensor_copy(modv[:, j * 512:(j + 1) * 512],
                                                mps[j][:, :]))

            ve0 = s_ve.n

            @blk.gpsimd
            def _(gpsimd):
                gpsimd.wait_ge(s_ve.h, ve0)
                s_gl.dma(gpsimd.dma_start(out=xmod_i.ap()[:, :], in_=modv[:, :]))
                gpsimd.wait_ge(s_gl.h, s_gl.n)
                s_cc.one(gpsimd.collective_compute(
                    "AllGather", ALU.bypass,
                    replica_groups=[[0, 1, 2, 3], [4, 5, 6, 7]],
                    ins=[xmod_i.ap().opt()], outs=[xmod_o.ap().opt()]))
                s_cc.one(gpsimd.collective_compute(
                    "AllReduce", ALU.add,
                    replica_groups=[[0, 1, 2, 3], [4, 5, 6, 7]],
                    ins=[bass.AP(xmod_i, 0, [[1, 1], [1, 16]]).opt()],
                    outs=[xsy_o.ap().opt()]))
                gpsimd.wait_ge(s_cc.h, s_cc.n)
                if "d_mod" in dump_t:
                    s_gl.dma(gpsimd.dma_start(out=dump_t["d_mod"].ap()[:, :],
                                              in_=xmod_o.ap()[:, :]))
                    gpsimd.wait_ge(s_gl.h, s_gl.n)

        with contextlib.ExitStack() as P:
            PE = P.enter_context
            vrow = PE(nc.sbuf_tensor("vrow", [1, D], F32))
            vrow2 = PE(nc.sbuf_tensor("vrow2", [1, D], F32))
            vrow3 = PE(nc.sbuf_tensor("vrow3", [1, D], F32))
            msk = PE(nc.sbuf_tensor("msk", [1, D], mybir.dt.uint8))
            mtmp = PE(nc.sbuf_tensor("mtmp", [1, D], F32))
            mtmp2 = PE(nc.sbuf_tensor("mtmp2", [1, D], F32))
            mi32 = PE(nc.sbuf_tensor("mi32", [1, D], I32))
            sc1 = PE(nc.sbuf_tensor("sc1", [1, 1], F32))
            sc2 = PE(nc.sbuf_tensor("sc2", [1, 1], F32))
            blk = PE(nc.Block())

            def mod_sl(i):
                return bass.AP(xmod_o, i * D, [[1, 1], [1, D]])

            @blk.gpsimd
            def _(gpsimd):
                s_gl.dma(gpsimd.dma_start(out=vec_s.ap()[0:1, :], in_=mod_sl(0)))
                s_gl.dma(gpsimd.dma_start(out=vec_s.ap()[3:4, :], in_=mod_sl(3)))
                s_gl.dma(gpsimd.dma_start(out=vrow[:, :], in_=mod_sl(1)))
                s_gl.dma(gpsimd.dma_start(out=vrow2[:, :], in_=mod_sl(4)))
                s_gl.dma(gpsimd.dma_start(out=vrow3[:, :], in_=mod_sl(2)))

            gl1 = s_gl.n

            @blk.vector
            def _(vector):
                vector.wait_ge(s_gl.h, gl1)
                s_ve.one(vector.tensor_scalar_add(mtmp[:, :], vrow[:, :], 1.0))
                s_ve.one(vector.tensor_copy(vrow[:, :], mtmp[:, :]))
                s_ve.one(vector.tensor_scalar_add(mtmp[:, :], vrow2[:, :], 1.0))
                s_ve.one(vector.tensor_copy(vrow2[:, :], mtmp[:, :]))
                s_ve.one(vector.tensor_scalar(msk[:, :], vrow3[:, :], 0.0, None, ALU.is_ge))
                s_ve.one(vector.tensor_scalar_max(mtmp[:, :], vrow3[:, :], 0.25))
                s_ve.one(vector.tensor_scalar_min(mtmp2[:, :], vrow3[:, :], -0.25))
                s_ve.one(vector.select(vrow3[:, :], msk[:, :], mtmp[:, :], mtmp2[:, :]))

            ve1 = s_ve.n

            @blk.gpsimd
            def _(gpsimd):
                gpsimd.wait_ge(s_ve.h, ve1)
                s_gl.dma(gpsimd.dma_start(out=vec_s.ap()[1:2, :], in_=vrow[:, :]))
                s_gl.dma(gpsimd.dma_start(out=vec_s.ap()[4:5, :], in_=vrow2[:, :]))
                s_gl.dma(gpsimd.dma_start(out=vec_s.ap()[2:3, :], in_=vrow3[:, :]))
                s_gl.dma(gpsimd.dma_start(out=vrow3[:, :], in_=mod_sl(5)))

            gl2 = s_gl.n

            @blk.vector
            def _(vector):
                vector.wait_ge(s_gl.h, gl2)
                s_ve.one(vector.tensor_scalar(msk[:, :], vrow3[:, :], 0.0, None, ALU.is_ge))
                s_ve.one(vector.tensor_scalar_max(mtmp[:, :], vrow3[:, :], 0.25))
                s_ve.one(vector.tensor_scalar_min(mtmp2[:, :], vrow3[:, :], -0.25))
                s_ve.one(vector.select(vrow3[:, :], msk[:, :], mtmp[:, :], mtmp2[:, :]))

            ve2 = s_ve.n

            @blk.gpsimd
            def _(gpsimd):
                gpsimd.wait_ge(s_ve.h, ve2)
                s_gl.dma(gpsimd.dma_start(out=vec_s.ap()[5:6, :], in_=vrow3[:, :]))
                s_gl.dma(gpsimd.dma_start(out=vrow[:, :], in_=fb_in.ap()[0:1, :]))

            gl3 = s_gl.n

            @blk.vector
            def _(vector):
                vector.wait_ge(s_gl.h, gl3)
                s_ve.one(vector.tensor_reduce(sc1[:, :], vrow[:, :], AX.X, ALU.max,
                                              apply_absolute_value=True))
                vector.wait_ge(s_ve.h, s_ve.n)
                s_ve.one(vector.tensor_scalar(sc2[:, :], sc1[:, :], TINY, 1.0 / QMAX,
                                              ALU.max, ALU.mult))
                vector.wait_ge(s_ve.h, s_ve.n)
                s_ve.one(vector.reciprocal(mtmp[0:1, 0:1], sc2[:, :]))
                vector.wait_ge(s_ve.h, s_ve.n)
                s_ve.one(vector.tensor_scalar(mtmp2[:, :], vrow[:, :], mtmp[0:1, 0:1],
                                              None, ALU.mult))
                s_ve.one(vector.tensor_copy(mi32[:, :], mtmp2[:, :]))
                s_ve.one(vector.tensor_scalar(mtmp[:, :], mi32[:, :], QMAX, -QMAX,
                                              ALU.min, ALU.max))
                s_ve.one(vector.tensor_scalar(mtmp2[:, :], mtmp[:, :], sc2[0:1, 0:1],
                                              None, ALU.mult))

            ve3 = s_ve.n

            @blk.gpsimd
            def _(gpsimd):
                gpsimd.wait_ge(s_ve.h, ve3)
                s_gl.dma(gpsimd.dma_start(out=vec_s.ap()[6:7, :], in_=mtmp2[:, :]))
                gpsimd.wait_ge(s_gl.h, s_gl.n)

        with contextlib.ExitStack() as P:
            PE = P.enter_context
            bca = PE(nc.sbuf_tensor("bca", [128, D], F32))
            bcb = PE(nc.sbuf_tensor("bcb", [128, D], F32))
            xt = [PE(nc.sbuf_tensor(f"xt{i}", [128, D], F32)) for i in range(2)]
            lnt = PE(nc.sbuf_tensor("lnt", [128, D], F32))
            h1t = [PE(nc.sbuf_tensor(f"h1t{i}", [128, D], BF16)) for i in range(2)]
            bnb = PE(nc.sbuf_tensor("bnb", [128, 24], F32))
            b6a = PE(nc.sbuf_tensor("b6a", [128, 6], F32))
            b6b = PE(nc.sbuf_tensor("b6b", [128, 6], F32))
            b6c = PE(nc.sbuf_tensor("b6c", [128, 6], F32))
            sq24 = PE(nc.sbuf_tensor("sq24", [128, 24], F32))
            s6a = PE(nc.sbuf_tensor("s6a", [128, 6], F32))
            s6b = PE(nc.sbuf_tensor("s6b", [128, 6], F32))
            s6c = PE(nc.sbuf_tensor("s6c", [128, 6], F32))
            vvs = PE(nc.sbuf_tensor("vvs", [128, 12], F32))
            lnu = PE(nc.sbuf_tensor("lnu", [128, D], F32))
            mvec = PE(nc.sbuf_tensor("mvec", [128, 1], F32))
            vvar = [PE(nc.sbuf_tensor(f"vvar{i}", [128, 1], F32)) for i in range(2)]
            sqv = [PE(nc.sbuf_tensor(f"sqv{i}", [128, 1], F32)) for i in range(2)]
            rstd = PE(nc.sbuf_tensor("rstd", [128, 1], F32))
            htp = [PE(nc.psum_tensor(f"htp{i}", [128, 1024], BF16)) for i in range(2)]
            blk = PE(nc.Block())

            ld1, ve1b, ac1, mm1 = s_ld.n, s_ve.n, s_ac.n, s_mm.n

            @blk.sync
            def _(sync):
                s_ld.dma(sync.dma_start(out=bca[:, :],
                                        in_=bass.AP(vec_s, 1 * D, [[0, 128], [1, D]])))
                s_ld.dma(sync.dma_start(out=bcb[:, :],
                                        in_=bass.AP(vec_s, 0 * D, [[0, 128], [1, D]])))
                for rt in range(8):
                    if rt >= 2:
                        sync.wait_ge(s_ve.h, ve1b + (rt - 1) * 2)
                    s_ld.dma(sync.dma_start(out=xt[rt % 2][:, :],
                                            in_=x_in.ap()[rt * 128:(rt + 1) * 128, :]))

            @blk.vector
            def _(vector):
                for rt in range(8):
                    xx = xt[rt % 2]
                    vector.wait_ge(s_ld.h, ld1 + 32 + (rt + 1) * 16)
                    if rt >= 2:
                        vector.wait_ge(s_mm.h, mm1 + (rt - 1) * 16)
                    for q in range(4):
                        vector.bn_stats(bnb[:, q * 6:(q + 1) * 6],
                                        xx[:, q * 512:(q + 1) * 512])
                    vector.tensor_tensor(b6a[:, :], bnb[:, 0:6], bnb[:, 6:12], ALU.add)
                    vector.tensor_tensor(b6b[:, :], bnb[:, 12:18], bnb[:, 18:24], ALU.add)
                    vector.tensor_tensor(sq24[:, :], bnb[:, :], bnb[:, :], ALU.mult)
                    vector.tensor_tensor(b6c[:, :], b6a[:, :], b6b[:, :], ALU.add)
                    vector.tensor_tensor(s6a[:, :], sq24[:, 0:6], sq24[:, 6:12], ALU.add)
                    vector.tensor_tensor(s6b[:, :], sq24[:, 12:18], sq24[:, 18:24], ALU.add)
                    vector.tensor_tensor(vvs[:, 0:1], b6c[:, 1:2], b6c[:, 4:5], ALU.add)
                    vector.tensor_tensor(s6c[:, :], s6a[:, :], s6b[:, :], ALU.add)
                    vector.tensor_tensor(vvs[:, 1:2], b6c[:, 2:3], b6c[:, 5:6], ALU.add)
                    vector.tensor_scalar(mvec[:, :], vvs[:, 0:1], 1.0 / 8.0, None, ALU.mult)
                    vector.tensor_tensor(vvs[:, 3:4], s6c[:, 1:2], s6c[:, 4:5], ALU.add)
                    vector.tensor_scalar(vvs[:, 2:3], vvs[:, 1:2], 1.0 / D, None, ALU.mult)
                    vector.tensor_scalar(vvs[:, 4:5], vvs[:, 3:4], 1.0 / 8.0, None, ALU.mult)
                    vector.tensor_tensor(vvs[:, 6:7], mvec[:, :], mvec[:, :], ALU.mult)
                    fence(vector, vector.tensor_tensor(vvs[:, 5:6], vvs[:, 2:3],
                                                       vvs[:, 4:5], ALU.add))
                    fence(vector, vector.tensor_tensor(vvs[:, 7:8], vvs[:, 5:6],
                                                       vvs[:, 6:7], ALU.subtract))
                    s_ve.one(vector.tensor_scalar(vvar[rt % 2][:, :], vvs[:, 7:8],
                                                  1.0, EPS, ALU.mult, ALU.add))
                    vector.wait_ge(s_ac.h, ac1 + (rt + 1 if rt <= 1 else 5 * rt - 3))
                    fence(vector, vector.reciprocal(rstd[:, :], sqv[rt % 2][:, :]))
                    vector.tensor_scalar(lnt[:, :], xx[:, :], mvec[:, :], rstd[:, :],
                                         ALU.subtract, ALU.mult)
                    vector.tensor_tensor(lnu[:, :], lnt[:, :], bca[:, :], ALU.mult)
                    s_ve.one(vector.tensor_tensor(h1t[rt % 2][:, :], lnu[:, :],
                                                  bcb[:, :], ALU.add))

            evac_cnt1 = {}

            @blk.scalar
            def _(scalar):
                def evacs(rt):
                    for grp in range(4):
                        scalar.wait_ge(s_mm.h, mm1 + rt * 16 + (grp + 1) * 4)
                        s_ac.one(scalar.copy(
                            bass.AP(h1fm, (4 * grp) * R + rt * 128,
                                    [[16 * R, 128], [R, 4], [1, 128]]),
                            htp[grp % 2][:, 0:512]))
                        evac_cnt1[(rt, grp)] = s_ac.n
                for rt in range(8):
                    scalar.wait_ge(s_ve.h, ve1b + rt * 2 + 1)
                    s_ac.one(scalar.activation(sqv[rt % 2][:, :], vvar[rt % 2][:, :],
                                               AF.Sqrt))
                    if rt >= 1:
                        evacs(rt - 1)
                evacs(7)

            @blk.tensor
            def _(tensor):
                for rt in range(8):
                    tensor.wait_ge(s_ve.h, ve1b + rt * 2 + 2)
                    for dc in range(16):
                        grp = dc // 4
                        gidx = rt * 4 + grp
                        if gidx >= 2 and dc % 4 == 0:
                            prt, pgrp = (gidx - 2) // 4, (gidx - 2) % 4
                            tensor.wait_ge(s_ac.h, evac_cnt1[(prt, pgrp)])
                        s_mm.one(tensor.transpose(
                            htp[grp % 2][:, (dc % 4) * 128:((dc % 4) + 1) * 128],
                            h1t[rt % 2][:, dc * 128:(dc + 1) * 128],
                            ident[:, :]))

            @blk.gpsimd
            def _(gpsimd):
                if "d_vecs" in dump_t:
                    gpsimd.wait_ge(s_ld.h, ld1 + 32)
                    s_gl.dma(gpsimd.dma_start(out=dump_t["d_vecs"].ap()[:, :],
                                              in_=vec_s.ap()[:, :]))
                    if "d_bca" in dump_t:
                        s_gl.dma(gpsimd.dma_start(out=dump_t["d_bca"].ap()[:, 0:D],
                                                  in_=bca[:, :]))
                        s_gl.dma(gpsimd.dma_start(out=dump_t["d_bca"].ap()[:, D:2 * D],
                                                  in_=bcb[:, :]))
                    gpsimd.wait_ge(s_gl.h, s_gl.n)
                if "d_h1" in dump_t:
                    gpsimd.wait_ge(s_ac.h, s_ac.n)
                    s_gl.dma(gpsimd.dma_start(out=dump_t["d_h1"].ap()[:, :],
                                              in_=h1fm[:, :]))
                    gpsimd.wait_ge(s_gl.h, s_gl.n)

        with contextlib.ExitStack() as P:
            PE = P.enter_context
            wful = PE(nc.sbuf_tensor("wful", [128, 16 * D], BF16))
            qfm = PE(nc.sbuf_tensor("qfm", [128, 16 * R], BF16))
            vfm = PE(nc.sbuf_tensor("vfm", [128, 16 * R], BF16))
            bqv = PE(nc.sbuf_tensor("bqv", [128, 64], F32))
            on128 = PE(nc.sbuf_tensor("on128", [128, 128], BF16))
            qnp = [PE(nc.sbuf_tensor(f"qnp{i}", [128, 1], F32)) for i in range(2)]
            qntmp = PE(nc.sbuf_tensor("qntmp", [128, 1], F32))
            qnbf = PE(nc.sbuf_tensor("qnbf", [1, 1], BF16))
            pp = PE(nc.psum_tensor("pp", [128, 1024], F32))
            npp = [PE(nc.psum_tensor(f"npp{i}", [128, 512], F32)) for i in range(2)]
            blk = PE(nc.Block())

            mm2, ve2b, gl2b, ac2 = s_mm.n, s_ve.n, s_gl.n, s_ac.n

            @blk.gpsimd
            def _(gpsimd):
                for j, src in enumerate((qb_in, kb_in, vb_in)):
                    s_gl.dma(gpsimd.dma_start(
                        out=bqv[:, j * 16:(j + 1) * 16],
                        in_=bass.AP(src, 0, [[1, 128], [128, 16]]),
                        allow_slow_non_contiguous=True))
                s_gl.one(gpsimd.memset(on128[:, :], 1.0))
                for j, src in enumerate((qT_in, kT_in, vT_in)):
                    for dc in range(16):
                        if j >= 1:
                            gpsimd.wait_ge(s_mm.h,
                                           mm2 + (j - 1) * 512 + 15 * 32 + 16 + dc + 1)
                        s_gl.dma(gpsimd.dma_start(
                            out=wful[:, dc * D:(dc + 1) * D],
                            in_=src.ap()[dc * 128:(dc + 1) * 128, :]))

            @blk.tensor
            def _(tensor):
                for j in range(3):
                    for ft in range(16):
                        idx = j * 16 + ft
                        if j <= 1 and idx >= 1:
                            tensor.wait_ge(s_ve.h, ve2b + idx * 2)
                        elif j == 2 and ft >= 1:
                            tensor.wait_ge(s_ve.h, ve2b + 130 + ft * 2)
                        for rc in range(2):
                            for dc in range(16):
                                if ft == 0 and rc == 0:
                                    tensor.wait_ge(s_gl.h,
                                                   gl2b + 49 + (j * 16 + dc + 1) * 16)
                                s_mm.one(tensor.matmul(
                                    pp[:, rc * 512:(rc + 1) * 512],
                                    wful[:, dc * D + ft * 128: dc * D + (ft + 1) * 128],
                                    h1fm[:, dc * R + rc * 512: dc * R + (rc + 1) * 512],
                                    start=(dc == 0), stop=(dc == 15)))
                    if j == 1:
                        tensor.wait_ge(s_ve.h, ve2b + 64 + 1)
                        for ch in range(32):
                            if ch >= 2:
                                tensor.wait_ge(s_ve.h, ve2b + 65 + (ch - 1) * 2)
                            s_mm.one(tensor.matmul(npp[ch % 2][:, :], on128[:, :],
                                                   vfm[:, ch * 512:(ch + 1) * 512],
                                                   start=True, stop=True))

            @blk.vector
            def _(vector):
                vector.wait_ge(s_gl.h, gl2b + 48)
                vector.tensor_scalar(bqv[:, 48:64], bqv[:, 16:32], SCALE, None,
                                     ALU.mult)

                def mm_of(j, ft):
                    return mm2 + (j * 512 if j <= 1 else 1056) + (ft + 1) * 32

                def evacs(j, dst, bofs, scl):
                    for ft in range(16):
                        vector.wait_ge(s_mm.h, mm_of(j, ft))
                        for rc in range(2):
                            sl = dst[:, ft * R + rc * 512: ft * R + (rc + 1) * 512]
                            if scl is None:
                                inst = vector.tensor_scalar(
                                    sl, pp[:, rc * 512:(rc + 1) * 512],
                                    bqv[:, bofs + ft:bofs + ft + 1], None, ALU.add)
                            else:
                                inst = vector.tensor_scalar(
                                    sl, pp[:, rc * 512:(rc + 1) * 512],
                                    scl, bqv[:, 32 + bofs + ft:33 + bofs + ft],
                                    ALU.mult, ALU.add)
                            s_ve.one(inst)

                evacs(0, qfm, 0, None)
                evacs(1, kfm, 16, SCALE)
                s_ve.one(vector.tensor_tensor(vfm[:, :], qfm[:, :], qfm[:, :], ALU.mult))
                for ch in range(32):
                    vector.wait_ge(s_mm.h, mm2 + 1024 + ch + 1)
                    s_ve.one(vector.tensor_reduce(qntmp[:, :], npp[ch % 2][:, :],
                                                  AX.X, ALU.max))
                    vector.wait_ge(s_ve.h, s_ve.n)
                    if ch == 0:
                        s_ve.one(vector.tensor_copy(qnp[0][:, :], qntmp[:, :]))
                    else:
                        s_ve.one(vector.tensor_tensor(
                            qnp[ch % 2][:, :], qnp[(ch + 1) % 2][:, :],
                            qntmp[:, :], ALU.max))
                        vector.wait_ge(s_ve.h, s_ve.n)
                s_ve.one(vector.tensor_copy(qnbf[:, :], qnp[1][0:1, 0:1]))
                vector.wait_ge(s_mm.h, mm2 + 1056)
                evacs(2, vfm, 32, None)

            @blk.gpsimd
            def _(gpsimd):
                gpsimd.wait_ge(s_ve.h, s_ve.n)
                for ft in range(16):
                    s_gl.dma(gpsimd.dma_start(
                        out=bass.AP(xqv_i, ft * 128 * R, [[R, 128], [1, R]]),
                        in_=qfm[:, ft * R:(ft + 1) * R]))
                    s_gl.dma(gpsimd.dma_start(
                        out=bass.AP(xqv_i, QOFF + ft * 128 * R, [[R, 128], [1, R]]),
                        in_=vfm[:, ft * R:(ft + 1) * R]))
                s_gl.dma(gpsimd.dma_start(
                    out=bass.AP(xqv_i, 2 * QOFF, [[1, 1], [1, 1]]),
                    in_=qnbf[:, :]))
                gpsimd.wait_ge(s_gl.h, s_gl.n)
                s_cc.one(gpsimd.collective_compute(
                    "AllGather", ALU.bypass,
                    replica_groups=[[0, 1, 2, 3], [4, 5, 6, 7]],
                    ins=[xqv_i.ap().opt()], outs=[xqv_o.ap().opt()]))
                s_cc.one(gpsimd.collective_compute(
                    "AllReduce", ALU.add,
                    replica_groups=[[0, 1, 2, 3], [4, 5, 6, 7]],
                    ins=[bass.AP(xqv_i, 0, [[1, 1], [1, 16]]).opt()],
                    outs=[xsy2_o.ap().opt()]))
                gpsimd.wait_ge(s_cc.h, s_cc.n)
                if "d_qfm" in dump_t:
                    s_gl.dma(gpsimd.dma_start(out=dump_t["d_qfm"].ap()[:, :], in_=qfm[:, :]))
                if "d_kfm" in dump_t:
                    s_gl.dma(gpsimd.dma_start(out=dump_t["d_kfm"].ap()[:, :], in_=kfm[:, :]))
                if "d_vfm" in dump_t:
                    s_gl.dma(gpsimd.dma_start(out=dump_t["d_vfm"].ap()[:, :], in_=vfm[:, :]))
                gpsimd.wait_ge(s_gl.h, s_gl.n)

        st_h1.close()

        st_afm = contextlib.ExitStack()
        afm = st_afm.enter_context(nc.sbuf_tensor("afm", [128, 16 * R], BF16, side="left"))
        st_att = contextlib.ExitStack()
        AE = st_att.enter_context
        bounds = AE(nc.sbuf_tensor("bounds", [128, 128], F32, side="left"))
        boundsr = AE(nc.sbuf_tensor("boundsr", [128, 128], F32, side="left"))
        qh = [AE(nc.sbuf_tensor(f"qh{i}", [128, T], BF16, side="left")) for i in range(2)]
        vhfm = [AE(nc.sbuf_tensor(f"vhfm{i}", [128, T], BF16, side="left")) for i in range(2)]
        vrm = [AE(nc.sbuf_tensor(f"vrm{i}", [128, 32 * 129], BF16, side="left")) for i in range(2)]

        with contextlib.ExitStack() as P:
            PE = P.enter_context
            sqk = PE(nc.sbuf_tensor("sqk", [128, 16 * R], BF16))
            on128b = PE(nc.sbuf_tensor("on128b", [128, 128], BF16))
            normb = PE(nc.sbuf_tensor("normb", [128, 512], F32))
            qmt = PE(nc.sbuf_tensor("qmt", [1, 4], BF16))
            qmf = PE(nc.sbuf_tensor("qmf", [1, 4], F32))
            qms = PE(nc.sbuf_tensor("qms", [1, 1], F32))
            qms2 = PE(nc.sbuf_tensor("qms2", [1, 1], F32))
            qmaxv = PE(nc.sbuf_tensor("qmaxv", [128, 1], F32))
            nppb = [PE(nc.psum_tensor(f"nppb{i}", [128, 512], F32)) for i in range(2)]
            blk = PE(nc.Block())

            mm3, ve3b, ac3, gl3b, ld3 = s_mm.n, s_ve.n, s_ac.n, s_gl.n, s_ld.n

            @blk.gpsimd
            def _(gpsimd):
                s_gl.one(gpsimd.memset(on128b[:, :], 1.0))
                for h in range(H):
                    for ch in range(2):
                        i2 = h * 2 + ch
                        gpsimd.wait_ge(s_ac.h, ac3 + i2 + 1)
                        s_gl.dma(gpsimd.dma_start(
                            out=nrm_s.ap()[h:h + 1, ch * 512:(ch + 1) * 512],
                            in_=normb[0:1, 0:512]))
                s_gl.dma(gpsimd.dma_start(
                    out=qmt[:, :],
                    in_=bass.AP(xqv_o, 2 * QOFF, [[1, 1], [XFLAT, 4]]),
                    allow_slow_non_contiguous=True))

            @blk.vector
            def _(vector):
                vector.wait_ge(s_gl.h, gl3b + 1)
                s_ve.one(vector.tensor_tensor(sqk[:, :], kfm[:, :], kfm[:, :], ALU.mult))
                vector.wait_ge(s_gl.h, gl3b + 1 + 32 * 16 + 16)
                s_ve.one(vector.tensor_copy(qmf[:, :], qmt[:, :]))
                vector.wait_ge(s_ve.h, s_ve.n)
                s_ve.one(vector.tensor_reduce(qms[:, :], qmf[:, :], AX.X, ALU.max))

            @blk.tensor
            def _(tensor):
                tensor.wait_ge(s_ve.h, ve3b + 1)
                for h in range(H):
                    for ch in range(2):
                        i2 = h * 2 + ch
                        if i2 >= 2:
                            tensor.wait_ge(s_ac.h, ac3 + i2 - 1)
                        s_mm.one(tensor.matmul(
                            nppb[i2 % 2][:, :], on128b[:, :],
                            sqk[:, h * R + ch * 512: h * R + (ch + 1) * 512],
                            start=True, stop=True))

            @blk.scalar
            def _(scalar):
                for h in range(H):
                    for ch in range(2):
                        i2 = h * 2 + ch
                        scalar.wait_ge(s_mm.h, mm3 + i2 + 1)
                        if i2 >= 1:
                            scalar.wait_ge(s_gl.h, gl3b + 1 + i2 * 16)
                        s_ac.one(scalar.activation(normb[:, :], nppb[i2 % 2][:, :],
                                                   AF.Sqrt))
                scalar.wait_ge(s_ve.h, ve3b + 3)
                s_ac.one(scalar.activation(qms2[:, :], qms[:, :], AF.Sqrt))

            ac3e = s_ac.n

            @blk.gpsimd
            def _(gpsimd):
                gpsimd.wait_ge(s_ac.h, ac3e)
                s_gl.dma(gpsimd.dma_start(
                    out=bass.AP(vec_s, 9 * D, [[1, 1], [1, 1]]), in_=qms2[:, :]))

            gl3e = s_gl.n

            @blk.sync
            def _(sync):
                for h in range(H):
                    sync.wait_ge(s_gl.h, gl3b + 1 + (h + 1) * 32)
                    s_ld.dma(sync.dma_start(
                        out=boundsr[:, h * 8:(h + 1) * 8],
                        in_=bass.AP(nrm_s, h * R, [[1, 128], [128, 8]]),
                        allow_slow_non_contiguous=True))
                sync.wait_ge(s_gl.h, gl3e)
                s_ld.dma(sync.dma_start(out=qmaxv[:, :],
                                        in_=bass.AP(vec_s, 9 * D, [[0, 128], [1, 1]])))

            ld3e = s_ld.n

            @blk.vector
            def _(vector):
                vector.wait_ge(s_ld.h, ld3e)
                s_ve.one(vector.tensor_scalar(bounds[:, :], boundsr[:, :],
                                              qmaxv[:, :], None, ALU.mult))

            @blk.gpsimd
            def _(gpsimd):
                if "d_bounds" in dump_t:
                    gpsimd.wait_ge(s_ve.h, s_ve.n)
                    s_gl.dma(gpsimd.dma_start(out=dump_t["d_bounds"].ap()[:, :],
                                              in_=bounds[:, :]))
                    gpsimd.wait_ge(s_gl.h, s_gl.n)

        with contextlib.ExitStack() as P:
            PE = P.enter_context
            Tt = [PE(nc.sbuf_tensor(f"Tt{i}", [128, 512], F32)) for i in range(2)]
            Pt = [PE(nc.sbuf_tensor(f"Pt{i}", [128, 512], BF16)) for i in range(2)]
            PTs = [PE(nc.sbuf_tensor(f"PTs{i}", [128, 512], BF16)) for i in range(2)]
            pat = PE(nc.sbuf_tensor("pat", [128, 2 * 512], F16))
            bvs = [PE(nc.sbuf_tensor(f"bvs{i}", [128, 8], F32)) for i in range(2)]
            sums = [PE(nc.sbuf_tensor(f"sums{i}", [128, 8], F32)) for i in range(2)]
            sumt = [PE(nc.sbuf_tensor(f"sumt{i}", [128, 1], F32)) for i in range(2)]
            sumr = [PE(nc.sbuf_tensor(f"sumr{i}", [128, 1], F32)) for i in range(2)]
            arm = [PE(nc.sbuf_tensor(f"arm{i}", [128, 128], BF16)) for i in range(2)]
            S_ps = [PE(nc.psum_tensor(f"S_ps{i}", [128, 512], F32)) for i in range(2)]
            PT_ps = [PE(nc.psum_tensor(f"PT_ps{i}", [128, 1024], BF16)) for i in range(2)]
            O_ps = [PE(nc.psum_tensor(f"O_ps{i}", [128, 512], F32)) for i in range(2)]
            vtp = PE(nc.psum_tensor("vtp", [128, 1024], BF16))
            otp = PE(nc.psum_tensor("otp", [128, 1024], BF16))
            blk = PE(nc.Block())

            pld = s_ld.n + 16

            @blk.gpsimd
            def _(gpsimd):
                for i in range(2):
                    s_gl.one(gpsimd.memset(
                        bass.AP(vrm[i], 128, [[32 * 129, 128], [129, 32], [1, 1]]), 1.0))

            gl3bb = s_gl.n

            @blk.sync
            def _(sync):
                s_ld.dma(sync.dma_start(
                    out=pat[:, :],
                    in_=bass.AP(pat_in, 0, [[512, 128], [128 * 512, 2], [1, 512]])))
                for hh in range(H):
                    if hh >= 2:
                        sync.wait_ge(a_s.h, G0[hh - 1])
                        sync.wait_ge(a_av.h, G0[hh - 1])
                    for cp in range(4):
                        a_qv.dma(sync.dma_start(
                            out=bass.AP(qh[hh % 2], cp * 128,
                                        [[T, 128], [512, 8], [1, 128]]),
                            in_=bass.AP(xqv_o, cp * XFLAT + hh * 128 * R,
                                        [[R, 128], [128, 8], [1, 128]])))
                    for cp in range(4):
                        a_qv.dma(sync.dma_start(
                            out=bass.AP(vhfm[hh % 2], cp * 128,
                                        [[T, 128], [512, 8], [1, 128]]),
                            in_=bass.AP(xqv_o, cp * XFLAT + QOFF + hh * 128 * R,
                                        [[R, 128], [128, 8], [1, 128]])))

            vprep_T = {}
            vprep_V = {}
            for h in range(1, H):
                for grp in range(4):
                    vprep_T.setdefault(G0[h - 1] + 8 + 2 * grp, []).append((h, grp))
                    vprep_V.setdefault(G0[h - 1] + 9 + 2 * grp, []).append((h, grp))

            def v_transposes(tensor, h, grp):
                vt_idx = h * 4 + grp
                tensor.wait_ge(a_qv.h, 128 * (h + 1))
                if vt_idx >= 1:
                    tensor.wait_ge(a_ve.h, vt_idx)
                for s8 in range(8):
                    jt = grp * 8 + s8
                    inst = tensor.transpose(
                        vtp[:, s8 * 128:(s8 + 1) * 128],
                        vhfm[h % 2][:, jt * 128:(jt + 1) * 128],
                        ident[:, :])
                inst.then_inc(a_vt.h, 1)
                a_vt.n += 1

            def v_evac(vector, h, grp):
                vt_idx = h * 4 + grp
                vector.wait_ge(a_vt.h, vt_idx + 1)
                if h >= 2:
                    vector.wait_ge(a_av.h, G0[h - 1])
                vector.tensor_copy(
                    bass.AP(vrm[h % 2], grp * 8 * 129, [[32 * 129, 128], [129, 8], [1, 128]]),
                    vtp[:, :]).then_inc(a_ve.h, 1)
                a_ve.n += 1

            @blk.tensor
            def _(tensor):
                tensor.wait_ge(s_ld.h, pld)
                for grp in range(4):
                    v_transposes(tensor, 0, grp)

                def emit_S(g):
                    (gg, h, k, c, last) = CHUNKS[g]
                    if g >= 2:
                        tensor.wait_ge(a_t.h, g - 1)
                    tensor.matmul(S_ps[g % 2][:, :],
                                  kfm[:, h * R + k * 128: h * R + (k + 1) * 128],
                                  qh[h % 2][:, c * 512:(c + 1) * 512],
                                  start=True, stop=True).then_inc(a_s.h, 1)
                    a_s.n += 1

                def emit_TR(g):
                    tensor.wait_ge(a_p.h, g + 1)
                    if g >= 2:
                        tensor.wait_ge(a_e.h, g - 1)
                    for t4 in range(4):
                        inst = tensor.transpose(
                            PT_ps[g % 2][:, t4 * 128:(t4 + 1) * 128],
                            Pt[g % 2][:, t4 * 128:(t4 + 1) * 128],
                            ident[:, :])
                    inst.then_inc(a_tr.h, 1)
                    a_tr.n += 1

                def emit_AV(g):
                    (gg, h, k, c, last) = CHUNKS[g]
                    m = M_OF_G[g]
                    tensor.wait_ge(a_e.h, g + 1)
                    if c == 0 and m >= 2:
                        tensor.wait_ge(a_oe.h, m - 1)
                    for t4 in range(4):
                        jt = c * 4 + t4
                        inst = tensor.matmul(
                            O_ps[m % 2][:, 0:129],
                            PTs[g % 2][:, t4 * 128:(t4 + 1) * 128],
                            vrm[h % 2][:, jt * 129:jt * 129 + 129],
                            start=(c == 0 and t4 == 0), stop=(last and t4 == 3))
                    inst.then_inc(a_av.h, 1)
                    a_av.n += 1

                def emit_OT(m):
                    tensor.wait_ge(a_oe.h, m + 1)
                    if m >= 1:
                        tensor.wait_ge(a_of.h, m)
                    tensor.transpose(otp[:, 0:128], arm[m % 2][:, :],
                                     ident[:, :]).then_inc(a_ot.h, 1)
                    a_ot.n += 1

                tensor.wait_ge(s_gl.h, gl3bb)
                for g in range(G_TOT + 4):
                    if g < G_TOT:
                        emit_S(g)
                    for (h, grp) in vprep_T.get(g, ()):
                        v_transposes(tensor, h, grp)
                    if 1 <= g <= G_TOT:
                        emit_TR(g - 1)
                    if 2 <= g <= G_TOT + 1:
                        emit_AV(g - 2)
                    if 4 <= g <= G_TOT + 3 and CHUNKS[g - 4][4]:
                        emit_OT(M_OF_G[g - 4])

            @blk.vector
            def _(vector):
                vector.wait_ge(s_ld.h, pld)
                for grp in range(4):
                    v_evac(vector, 0, grp)

                def emit_stt(g):
                    (gg, h, k, c, last) = CHUNKS[g]
                    m = M_OF_G[g]
                    if c == 0:
                        if m >= 2:
                            vector.wait_ge(a_p.h, ITILES[m - 2][4] + 1)
                        vector.tensor_scalar(bvs[m % 2][:, :], t512[:, :],
                                             bounds[:, h * 8 + k: h * 8 + k + 1],
                                             i0bc[:, k:k + 1],
                                             ALU.subtract, ALU.subtract)
                    vector.wait_ge(a_s.h, g + 1)
                    if g >= 2:
                        vector.wait_ge(a_p.h, g - 1)
                    kp = 1 if last else 0
                    vector.tensor_tensor(Tt[g % 2][:, :], S_ps[g % 2][:, :],
                                         pat[:, kp * 512:(kp + 1) * 512],
                                         ALU.add).then_inc(a_t.h, 1)
                    a_t.n += 1

                def emit_evac_v(g):
                    vector.wait_ge(a_tr.h, g + 1)
                    if g >= 2:
                        vector.wait_ge(a_av.h, g - 1)
                    vector.tensor_copy(PTs[g % 2][:, :],
                                       PT_ps[g % 2][:, 0:512]).then_inc(a_e.h, 1)
                    a_e.n += 1

                def emit_finish(m):
                    (mm_, h, k, nch, g_last) = ITILES[m]
                    vector.wait_ge(a_av.h, g_last + 1)
                    if m >= 2:
                        vector.wait_ge(a_ot.h, m - 1)
                    fence(vector, vector.reciprocal(sumt[m % 2][:, :],
                                                    O_ps[m % 2][:, 128:129]))
                    vector.tensor_scalar(arm[m % 2][:, :], O_ps[m % 2][:, 0:128],
                                         sumt[m % 2][:, :], None,
                                         ALU.mult).then_inc(a_oe.h, 1)
                    a_oe.n += 1

                for g in range(G_TOT + 1):
                    if g < G_TOT:
                        emit_stt(g)
                    for (h, grp) in vprep_V.get(g, ()):
                        v_evac(vector, h, grp)
                    if 1 <= g <= G_TOT and (g - 1) % 2 == 0:
                        emit_evac_v(g - 1)
                    if g >= 1 and CHUNKS[g - 1][4]:
                        emit_finish(M_OF_G[g - 1])

            @blk.scalar
            def _(scalar):
                scalar.wait_ge(s_ld.h, pld)

                def emit_exp(g):
                    (gg, h, k, c, last) = CHUNKS[g]
                    m = M_OF_G[g]
                    scalar.wait_ge(a_t.h, g + 1)
                    if g >= 2:
                        scalar.wait_ge(a_tr.h, g - 1)
                    scalar.activation(Pt[g % 2][:, :], Tt[g % 2][:, :], AF.Exp,
                                      bias=bvs[m % 2][:, c:c + 1],
                                      scale=1.0).then_inc(a_p.h, 1)
                    a_p.n += 1

                def emit_evac_a(g):
                    scalar.wait_ge(a_tr.h, g + 1)
                    if g >= 2:
                        scalar.wait_ge(a_av.h, g - 1)
                    scalar.copy(PTs[g % 2][:, :],
                                PT_ps[g % 2][:, 0:512]).then_inc(a_e.h, 1)
                    a_e.n += 1

                def emit_afm(m):
                    (mm_, h, k, nch, g_last) = ITILES[m]
                    scalar.wait_ge(a_ot.h, m + 1)
                    scalar.copy(afm[:, h * R + k * 128: h * R + (k + 1) * 128],
                                otp[:, 0:128]).then_inc(a_of.h, 1)
                    a_of.n += 1

                for g in range(G_TOT + 1):
                    if g < G_TOT:
                        emit_exp(g)
                    if 1 <= g <= G_TOT and (g - 1) % 2 == 1:
                        emit_evac_a(g - 1)
                    if g >= 3 and CHUNKS[g - 3][4]:
                        emit_afm(M_OF_G[g - 3])
                for gx in (G_TOT - 2, G_TOT - 1):
                    if CHUNKS[gx][4]:
                        emit_afm(M_OF_G[gx])

            @blk.gpsimd
            def _(gpsimd):
                if "d_afm" in dump_t:
                    gpsimd.wait_ge(a_of.h, M_TOT)
                    s_gl.dma(gpsimd.dma_start(out=dump_t["d_afm"].ap()[:, :],
                                              in_=afm[:, :]))
                    gpsimd.wait_ge(s_gl.h, s_gl.n)

        st_att.close()
        st_kfm.close()

        with contextlib.ExitStack() as P:
            PE = P.enter_context
            x2 = PE(nc.sbuf_tensor("x2", [128, 8 * D], F32))
            ow = PE(nc.sbuf_tensor("ow", [128, 16 * D], BF16))
            bga = PE(nc.sbuf_tensor("bga", [128, D], F32))
            bob = PE(nc.sbuf_tensor("bob", [128, D], F32))
            t1 = PE(nc.sbuf_tensor("t1", [128, 512], F32))
            t2 = PE(nc.sbuf_tensor("t2", [128, 512], F32))
            po = PE(nc.psum_tensor("po", [128, 8 * 512], F32))
            blk = PE(nc.Block())

            mm4, ve4, gl4, ld4 = s_mm.n, s_ve.n, s_gl.n, s_ld.n

            @blk.gpsimd
            def _(gpsimd):
                for dc in range(16):
                    s_gl.dma(gpsimd.dma_start(
                        out=ow[:, dc * D:(dc + 1) * D],
                        in_=oT_in.ap()[dc * 128:(dc + 1) * 128, :]))

            @blk.sync
            def _(sync):
                s_ld.dma(sync.dma_start(out=bga[:, :],
                                        in_=bass.AP(vec_s, 2 * D, [[0, 128], [1, D]])))
                s_ld.dma(sync.dma_start(out=bob[:, :],
                                        in_=bass.AP(ob_in, 0, [[0, 128], [1, D]])))
                for rt in range(8):
                    s_ld.dma(sync.dma_start(
                        out=x2[:, rt * D:(rt + 1) * D],
                        in_=x_in.ap()[rt * 128:(rt + 1) * 128, :]))

            @blk.tensor
            def _(tensor):
                tensor.wait_ge(s_gl.h, gl4 + 256)
                for fg in range(4):
                    if fg >= 1:
                        tensor.wait_ge(s_ve.h, ve4 + fg * 8)
                    for dc in range(16):
                        for rt in range(8):
                            s_mm.one(tensor.matmul(
                                po[:, rt * 512:(rt + 1) * 512],
                                afm[:, dc * R + rt * 128: dc * R + (rt + 1) * 128],
                                ow[:, dc * D + fg * 512: dc * D + (fg + 1) * 512],
                                start=(dc == 0), stop=(dc == 15)))

            @blk.vector
            def _(vector):
                vector.wait_ge(s_ld.h, ld4 + 16 * 10)
                for fg in range(4):
                    vector.wait_ge(s_mm.h, mm4 + (fg + 1) * 128)
                    for rt in range(8):
                        vector.tensor_tensor(t1[:, :], po[:, rt * 512:(rt + 1) * 512],
                                             bob[:, fg * 512:(fg + 1) * 512], ALU.add)
                        vector.tensor_tensor(t2[:, :], t1[:, :],
                                             bga[:, fg * 512:(fg + 1) * 512], ALU.mult)
                        xsl = x2[:, rt * D + fg * 512: rt * D + (fg + 1) * 512]
                        s_ve.one(vector.tensor_tensor(xsl, xsl, t2[:, :], ALU.add))

            ve4e = s_ve.n

            @blk.gpsimd
            def _(gpsimd):
                gpsimd.wait_ge(s_ve.h, ve4e)
                for rt in range(8):
                    s_gl.dma(gpsimd.dma_start(out=x2_d.ap()[rt * 128:(rt + 1) * 128, :],
                                              in_=x2[:, rt * D:(rt + 1) * D]))
                gpsimd.wait_ge(s_gl.h, s_gl.n)
                if "d_x2" in dump_t:
                    s_gl.dma(gpsimd.dma_start(out=dump_t["d_x2"].ap()[:, :], in_=x2[:, :]))
                    gpsimd.wait_ge(s_gl.h, s_gl.n)

        st_afm.close()

        st_qx = contextlib.ExitStack()
        qxfm = st_qx.enter_context(nc.sbuf_tensor("qxfm", [128, 16 * R], BF16, side="left"))

        with contextlib.ExitStack() as P:
            PE = P.enter_context
            bcb1 = PE(nc.sbuf_tensor("bcb1", [128, D], F32))
            bsb = PE(nc.sbuf_tensor("bsb", [128, D], F32))
            lnt2 = PE(nc.sbuf_tensor("lnt2", [128, D], F32))
            h2t = PE(nc.sbuf_tensor("h2t", [128, D], F32))
            q_rm = [PE(nc.sbuf_tensor(f"q_rm{i}", [128, D], BF16)) for i in range(2)]
            qi = PE(nc.sbuf_tensor("qi", [128, D], I32))
            bnb2 = PE(nc.sbuf_tensor("bnb2", [128, 24], F32))
            c6a = PE(nc.sbuf_tensor("c6a", [128, 6], F32))
            c6b = PE(nc.sbuf_tensor("c6b", [128, 6], F32))
            c6c = PE(nc.sbuf_tensor("c6c", [128, 6], F32))
            sq24b = PE(nc.sbuf_tensor("sq24b", [128, 24], F32))
            d6a = PE(nc.sbuf_tensor("d6a", [128, 6], F32))
            d6b = PE(nc.sbuf_tensor("d6b", [128, 6], F32))
            d6c = PE(nc.sbuf_tensor("d6c", [128, 6], F32))
            vvs2 = PE(nc.sbuf_tensor("vvs2", [128, 12], F32))
            lnu2 = PE(nc.sbuf_tensor("lnu2", [128, D], F32))
            amx2 = PE(nc.sbuf_tensor("amx2", [128, 1], F32))
            mv2 = PE(nc.sbuf_tensor("mv2", [128, 1], F32))
            vva = [PE(nc.sbuf_tensor(f"vva{i}", [128, 1], F32)) for i in range(2)]
            sqa = [PE(nc.sbuf_tensor(f"sqa{i}", [128, 1], F32)) for i in range(2)]
            rst2 = PE(nc.sbuf_tensor("rst2", [128, 1], F32))
            amx = PE(nc.sbuf_tensor("amx", [128, 1], F32))
            r1v = PE(nc.sbuf_tensor("r1v", [128, 1], F32))
            htq = [PE(nc.psum_tensor(f"htq{i}", [128, 1024], BF16)) for i in range(2)]
            x2t = [PE(nc.sbuf_tensor(f"x2t{i}", [128, D], F32)) for i in range(2)]
            blk = PE(nc.Block())

            ve5, ac5, mm5, ld5 = s_ve.n, s_ac.n, s_mm.n, s_ld.n

            @blk.sync
            def _(sync):
                s_ld.dma(sync.dma_start(out=bcb1[:, :],
                                        in_=bass.AP(vec_s, 4 * D, [[0, 128], [1, D]])))
                s_ld.dma(sync.dma_start(out=bsb[:, :],
                                        in_=bass.AP(vec_s, 3 * D, [[0, 128], [1, D]])))
                for rt in range(8):
                    if rt >= 2:
                        sync.wait_ge(s_ve.h, ve5 + (rt - 1) * 2)
                    s_ld.dma(sync.dma_start(out=x2t[rt % 2][:, :],
                                            in_=x2_d.ap()[rt * 128:(rt + 1) * 128, :]))

            @blk.vector
            def _(vector):
                for rt in range(8):
                    xx = x2t[rt % 2][:, :]
                    vector.wait_ge(s_ld.h, ld5 + 32 + (rt + 1) * 16)
                    if rt >= 2:
                        vector.wait_ge(s_mm.h, mm5 + (rt - 1) * 16)
                    for q in range(4):
                        vector.bn_stats(bnb2[:, q * 6:(q + 1) * 6],
                                        xx[:, q * 512:(q + 1) * 512])
                    vector.tensor_tensor(c6a[:, :], bnb2[:, 0:6], bnb2[:, 6:12], ALU.add)
                    vector.tensor_tensor(c6b[:, :], bnb2[:, 12:18], bnb2[:, 18:24], ALU.add)
                    vector.tensor_tensor(sq24b[:, :], bnb2[:, :], bnb2[:, :], ALU.mult)
                    vector.tensor_tensor(c6c[:, :], c6a[:, :], c6b[:, :], ALU.add)
                    vector.tensor_tensor(d6a[:, :], sq24b[:, 0:6], sq24b[:, 6:12], ALU.add)
                    vector.tensor_tensor(d6b[:, :], sq24b[:, 12:18], sq24b[:, 18:24], ALU.add)
                    vector.tensor_tensor(vvs2[:, 0:1], c6c[:, 1:2], c6c[:, 4:5], ALU.add)
                    vector.tensor_tensor(d6c[:, :], d6a[:, :], d6b[:, :], ALU.add)
                    vector.tensor_tensor(vvs2[:, 1:2], c6c[:, 2:3], c6c[:, 5:6], ALU.add)
                    vector.tensor_scalar(mv2[:, :], vvs2[:, 0:1], 1.0 / 8.0, None, ALU.mult)
                    vector.tensor_tensor(vvs2[:, 3:4], d6c[:, 1:2], d6c[:, 4:5], ALU.add)
                    vector.tensor_scalar(vvs2[:, 2:3], vvs2[:, 1:2], 1.0 / D, None, ALU.mult)
                    vector.tensor_scalar(vvs2[:, 4:5], vvs2[:, 3:4], 1.0 / 8.0, None, ALU.mult)
                    vector.tensor_tensor(vvs2[:, 6:7], mv2[:, :], mv2[:, :], ALU.mult)
                    fence(vector, vector.tensor_tensor(vvs2[:, 5:6], vvs2[:, 2:3],
                                                       vvs2[:, 4:5], ALU.add))
                    fence(vector, vector.tensor_tensor(vvs2[:, 7:8], vvs2[:, 5:6],
                                                       vvs2[:, 6:7], ALU.subtract))
                    s_ve.one(vector.tensor_scalar(vva[rt % 2][:, :], vvs2[:, 7:8],
                                                  1.0, EPS, ALU.mult, ALU.add))
                    vector.wait_ge(s_ac.h, ac5 + (rt + 1 if rt <= 1 else 5 * rt - 3))
                    fence(vector, vector.reciprocal(rst2[:, :], sqa[rt % 2][:, :]))
                    vector.tensor_scalar(lnt2[:, :], xx[:, :], mv2[:, :], rst2[:, :],
                                         ALU.subtract, ALU.mult)
                    vector.tensor_tensor(lnu2[:, :], lnt2[:, :], bcb1[:, :], ALU.mult)
                    vector.tensor_tensor(h2t[:, :], lnu2[:, :], bsb[:, :], ALU.add)
                    fence(vector, vector.tensor_reduce(amx[:, :], h2t[:, :], AX.X,
                                                       ALU.max,
                                                       apply_absolute_value=True))
                    fence(vector, vector.tensor_scalar(amx2[:, :], amx[:, :], TINY,
                                                       None, ALU.max))
                    fence(vector, vector.reciprocal(r1v[:, :], amx2[:, :]))
                    vector.tensor_scalar(sxv[:, rt:rt + 1], amx2[:, :], 1.0 / QMAX,
                                         None, ALU.mult)
                    vector.tensor_scalar(lnu2[:, :], h2t[:, :], r1v[:, :], QMAX,
                                         ALU.mult, ALU.mult)
                    vector.tensor_copy(qi[:, :], lnu2[:, :])
                    s_ve.one(vector.tensor_scalar(q_rm[rt % 2][:, :], qi[:, :],
                                                  QMAX, -QMAX, ALU.min, ALU.max))

            evac_cnt5 = {}

            @blk.scalar
            def _(scalar):
                def evacs(rt):
                    for grp in range(4):
                        scalar.wait_ge(s_mm.h, mm5 + rt * 16 + (grp + 1) * 4)
                        s_ac.one(scalar.copy(
                            bass.AP(qxfm, (4 * grp) * R + rt * 128,
                                    [[16 * R, 128], [R, 4], [1, 128]]),
                            htq[grp % 2][:, 0:512]))
                        evac_cnt5[(rt, grp)] = s_ac.n
                for rt in range(8):
                    scalar.wait_ge(s_ve.h, ve5 + rt * 2 + 1)
                    s_ac.one(scalar.activation(sqa[rt % 2][:, :], vva[rt % 2][:, :],
                                               AF.Sqrt))
                    if rt >= 1:
                        evacs(rt - 1)
                evacs(7)

            @blk.tensor
            def _(tensor):
                for rt in range(8):
                    tensor.wait_ge(s_ve.h, ve5 + rt * 2 + 2)
                    for dc in range(16):
                        grp = dc // 4
                        gidx = rt * 4 + grp
                        if gidx >= 2 and dc % 4 == 0:
                            prt, pgrp = (gidx - 2) // 4, (gidx - 2) % 4
                            tensor.wait_ge(s_ac.h, evac_cnt5[(prt, pgrp)])
                        s_mm.one(tensor.transpose(
                            htq[grp % 2][:, (dc % 4) * 128:((dc % 4) + 1) * 128],
                            q_rm[rt % 2][:, dc * 128:(dc + 1) * 128],
                            ident[:, :]))

            @blk.gpsimd
            def _(gpsimd):
                if "d_qx" in dump_t:
                    gpsimd.wait_ge(s_ac.h, s_ac.n)
                    s_gl.dma(gpsimd.dma_start(out=dump_t["d_qx"].ap()[:, :], in_=qxfm[:, :]))
                    gpsimd.wait_ge(s_gl.h, s_gl.n)

        with contextlib.ExitStack() as P:
            PE = P.enter_context
            akv = PE(nc.sbuf_tensor("akv", [128, 16], F32))
            rkv = PE(nc.sbuf_tensor("rkv", [128, 16], F32))
            skv = PE(nc.sbuf_tensor("skv", [128, 16], F32))
            brk = PE(nc.sbuf_tensor("brk", [128, D], F32))
            qk = PE(nc.sbuf_tensor("qk", [128, 16 * D], BF16))
            ki32 = PE(nc.sbuf_tensor("ki32", [128, D], I32))
            p6 = PE(nc.psum_tensor("p6", [128, 8 * 512], F32))
            P6i = PE(contextlib.ExitStack())
            kt = [P6i.enter_context(nc.sbuf_tensor(f"kt{i}", [128, D], F32))
                  for i in range(2)]
            blk = P6i.enter_context(nc.Block())

            ve6, gl6, ld6, mm6, st6 = s_ve.n, s_gl.n, s_ld.n, s_mm.n, s_st.n

            @blk.sync
            def _(sync):
                for ft in range(16):
                    if ft >= 2:
                        sync.wait_ge(s_ve.h, ve6 + ft - 1)
                    s_ld.dma(sync.dma_start(out=kt[ft % 2][:, :],
                                            in_=fkT_in.ap()[ft * 128:(ft + 1) * 128, :]))

            @blk.vector
            def _(vector):
                for ft in range(16):
                    vector.wait_ge(s_ld.h, ld6 + (ft + 1) * 16)
                    s_ve.one(vector.tensor_reduce(
                        akv[:, ft:ft + 1], kt[ft % 2][:, :], AX.X, ALU.max,
                        apply_absolute_value=True))
                    if ft == 15:
                        vector.wait_ge(s_ve.h, s_ve.n)
                s_ve.one(vector.tensor_scalar(skv[:, :], akv[:, :], TINY, 1.0 / QMAX,
                                              ALU.max, ALU.mult))
                vector.wait_ge(s_ve.h, s_ve.n)
                s_ve.one(vector.reciprocal(rkv[:, :], skv[:, :]))

            @blk.gpsimd
            def _(gpsimd):
                gpsimd.wait_ge(s_ve.h, ve6 + 18)
                s_gl.dma(gpsimd.dma_start(
                    out=bass.AP(vec_s, 7 * D, [[1, 128], [128, 16]]), in_=skv[:, :],
                    allow_slow_non_contiguous=True))
                s_gl.dma(gpsimd.dma_start(
                    out=bass.AP(vec_s, 8 * D, [[1, 128], [128, 16]]), in_=rkv[:, :],
                    allow_slow_non_contiguous=True))

            @blk.sync
            def _(sync):
                sync.wait_ge(s_gl.h, gl6 + 32)
                s_ld.dma(sync.dma_start(out=brk[:, :],
                                        in_=bass.AP(vec_s, 8 * D, [[0, 128], [1, D]])))
                for dc in range(16):
                    if dc >= 2:
                        sync.wait_ge(s_ve.h, ve6 + 18 + dc - 1)
                    s_ld.dma(sync.dma_start(out=kt[dc % 2][:, :],
                                            in_=fk_in.ap()[dc * 128:(dc + 1) * 128, :]))

            @blk.vector
            def _(vector):
                for dc in range(16):
                    vector.wait_ge(s_ld.h, ld6 + 16 * 16 + 16 + (dc + 1) * 16)
                    vector.tensor_tensor(ki32[:, :], kt[dc % 2][:, :], brk[:, :], ALU.mult)
                    s_ve.one(vector.tensor_scalar(qk[:, dc * D:(dc + 1) * D], ki32[:, :],
                                                  QMAX, -QMAX, ALU.min, ALU.max))

            P6i.close()
            bsk = PE(nc.sbuf_tensor("bsk", [128, D], F32))
            bbq = PE(nc.sbuf_tensor("bbq", [128, D], F32))
            bgb = PE(nc.sbuf_tensor("bgb", [128, D], F32))
            t3 = PE(nc.sbuf_tensor("t3", [128, D], F32))
            t4 = PE(nc.sbuf_tensor("t4", [128, D], F32))
            t5 = PE(nc.sbuf_tensor("t5", [128, D], F32))
            ay2 = PE(nc.sbuf_tensor("ay2", [128, 1], F32))
            x2u = [PE(nc.sbuf_tensor(f"x2u{i}", [128, D], F32)) for i in range(2)]
            outt = [PE(nc.sbuf_tensor(f"outt{i}", [128, D], F32)) for i in range(2)]
            ay = PE(nc.sbuf_tensor("ay", [128, 1], F32))
            syv = PE(nc.sbuf_tensor("syv", [128, 1], F32))
            ryv = PE(nc.sbuf_tensor("ryv", [128, 1], F32))
            blk2 = PE(nc.Block())
            ld6b = s_ld.n

            @blk2.sync
            def _(sync):
                s_ld.dma(sync.dma_start(out=bsk[:, :],
                                        in_=bass.AP(vec_s, 7 * D, [[0, 128], [1, D]])))
                s_ld.dma(sync.dma_start(out=bbq[:, :],
                                        in_=bass.AP(vec_s, 6 * D, [[0, 128], [1, D]])))
                s_ld.dma(sync.dma_start(out=bgb[:, :],
                                        in_=bass.AP(vec_s, 5 * D, [[0, 128], [1, D]])))
                for rt in range(8):
                    if rt >= 2:
                        sync.wait_ge(s_ve.h, ve6 + 34 + rt - 1)
                    s_ld.dma(sync.dma_start(out=x2u[rt % 2][:, :],
                                            in_=x2_d.ap()[rt * 128:(rt + 1) * 128, :]))

            @blk2.tensor
            def _(tensor):
                for rt in range(8):
                    half = rt % 2
                    if rt >= 2:
                        tensor.wait_ge(s_ve.h, ve6 + 34 + (rt - 1))
                    for dc in range(16):
                        for fg in range(4):
                            s_mm.one(tensor.matmul(
                                p6[:, (half * 4 + fg) * 512:(half * 4 + fg + 1) * 512],
                                qxfm[:, dc * R + rt * 128: dc * R + (rt + 1) * 128],
                                qk[:, dc * D + fg * 512: dc * D + (fg + 1) * 512],
                                start=(dc == 0), stop=(dc == 15)))

            @blk2.vector
            def _(vector):
                for rt in range(8):
                    half = rt % 2
                    vector.wait_ge(s_mm.h, mm6 + (rt + 1) * 64)
                    vector.wait_ge(s_ld.h, ld6b + 48 + (rt + 1) * 16)
                    if rt >= 2:
                        vector.wait_ge(s_st.h, st6 + (rt - 1) * 16)
                    psl = p6[:, half * 2048:(half + 1) * 2048]
                    vector.scalar_tensor_tensor(t3[:, :], psl, sxv[:, rt:rt + 1],
                                                bsk[:, :], ALU.mult, ALU.mult)
                    vector.tensor_tensor(t4[:, :], t3[:, :], bbq[:, :], ALU.add)
                    vector.tensor_scalar(t5[:, :], t4[:, :], 0.0, None, ALU.max)
                    fence(vector, vector.tensor_reduce(ay[:, :], t5[:, :], AX.X,
                                                       ALU.max,
                                                       apply_absolute_value=True))
                    fence(vector, vector.tensor_scalar(ay2[:, :], ay[:, :], TINY,
                                                       None, ALU.max))
                    fence(vector, vector.reciprocal(ryv[:, :], ay2[:, :]))
                    vector.tensor_scalar(syv[:, :], ay2[:, :], 1.0 / QMAX, None, ALU.mult)
                    vector.tensor_scalar(t3[:, :], t5[:, :], ryv[:, :], QMAX,
                                         ALU.mult, ALU.mult)
                    vector.tensor_copy(ki32[:, :], t3[:, :])
                    vector.tensor_scalar(t4[:, :], ki32[:, :], QMAX, -QMAX,
                                         ALU.min, ALU.max)
                    vector.scalar_tensor_tensor(t3[:, :], t4[:, :], syv[:, :],
                                                bgb[:, :], ALU.mult, ALU.mult)
                    s_ve.one(vector.tensor_tensor(outt[rt % 2][:, :],
                                                  x2u[rt % 2][:, :],
                                                  t3[:, :], ALU.add))

            @blk2.gpsimd
            def _(gpsimd):
                for rt in range(8):
                    gpsimd.wait_ge(s_ve.h, ve6 + 34 + rt + 1)
                    s_st.dma(gpsimd.dma_start(out=out_t.ap()[rt * 128:(rt + 1) * 128, :],
                                              in_=outt[rt % 2][:, :]))
                gpsimd.wait_ge(s_st.h, s_st.n)

        st_qx.close()

    return nc


def _host_consts():
    ident = np.eye(128, dtype=ml_dtypes.bfloat16)
    jj = np.arange(512, dtype=np.float32)[None, :]
    ip = np.arange(128, dtype=np.float32)[:, None]
    pat_off = (jj - ip).astype(np.float32)
    t512 = np.broadcast_to(512.0 * np.arange(8, dtype=np.float32)[None, :],
                           (128, 8)).copy()
    return ident, pat_off, t512


def _run(dump, L0_attn_x, L0_attn_q_w, L0_attn_q_b, L0_attn_k_w, L0_attn_k_b,
         L0_attn_v_w, L0_attn_v_b, L0_attn_o_w, L0_attn_o_b,
         L0_ffn_kernel, L0_ffn_bias, L0_c, L0_wmod):
    ident, pat_off, t512 = _host_consts()
    x = np.ascontiguousarray(np.asarray(L0_attn_x, dtype=np.float32).reshape(B * T, D))
    qT = np.ascontiguousarray(np.asarray(L0_attn_q_w, dtype=np.float32).T)
    kT = np.ascontiguousarray(np.asarray(L0_attn_k_w, dtype=np.float32).T)
    vT = np.ascontiguousarray(np.asarray(L0_attn_v_w, dtype=np.float32).T)
    oT = np.ascontiguousarray(np.asarray(L0_attn_o_w, dtype=np.float32).T)
    fk = np.ascontiguousarray(np.asarray(L0_ffn_kernel, dtype=np.float32))
    fkT = np.ascontiguousarray(fk.T)
    qb = np.asarray(L0_attn_q_b, dtype=np.float32).reshape(1, D)
    kb = np.asarray(L0_attn_k_b, dtype=np.float32).reshape(1, D)
    vb = np.asarray(L0_attn_v_b, dtype=np.float32).reshape(1, D)
    ob = np.asarray(L0_attn_o_b, dtype=np.float32).reshape(1, D)
    fb = np.asarray(L0_ffn_bias, dtype=np.float32).reshape(1, D)
    cc = np.asarray(L0_c, dtype=np.float32)
    wm = np.ascontiguousarray(np.asarray(L0_wmod, dtype=np.float32))

    nc = build_nc(dump)
    in_maps = []
    for core in range(NC):
        b, cp = core // 4, core % 4
        gtiles = cp + 4 * np.arange(8)
        rows = (b * T + (gtiles[:, None] * 128 + np.arange(128)[None, :])).reshape(-1)
        x_own = np.ascontiguousarray(x[rows])
        jj = np.arange(512, dtype=np.float32)[None, :]
        ip = np.arange(128, dtype=np.float32)[:, None]
        pat_diag = np.where(jj > 128.0 * cp + ip, np.float32(NEG), jj - ip)
        pats = np.stack([pat_off, pat_diag]).astype(np.float16)
        i0 = (512.0 * np.arange(8) + 128.0 * cp).astype(np.float32)
        i0bc = np.broadcast_to(i0[None, :], (128, 8)).copy()
        in_maps.append({
            "x_own": x_own, "qT": qT, "kT": kT, "vT": vT,
            "qb": qb, "kb": kb, "vb": vb, "oT": oT, "ob": ob,
            "fk": fk, "fkT": fkT, "fb": fb,
            "c_own": np.ascontiguousarray(cc[b].reshape(1, D)),
            "wm_own": np.ascontiguousarray(wm[:, cp * 3072:(cp + 1) * 3072]),
            "ident": ident, "pats": pats,
            "t512c": t512, "i0bc": i0bc,
        })
    res = run_bass_kernel_spmd(nc, in_maps, core_ids=list(range(NC)))
    return res


def core_rows(core):
    b, cp = core // 4, core % 4
    gtiles = cp + 4 * np.arange(8)
    return (b * T + (gtiles[:, None] * 128 + np.arange(128)[None, :])).reshape(-1)


def kernel(**inputs):
    res = _run((), **inputs)
    out = np.empty((B * T, D), dtype=np.float32)
    for core in range(NC):
        out[core_rows(core)] = res.results[core]["out"]
    return out.reshape(B, T, D)


# version 05
# speedup vs baseline: 1.0852x; 1.0436x over previous
import contextlib
import numpy as np
import ml_dtypes
import concourse.bass as bass
import concourse.mybir as mybir
from concourse.bass_utils import run_bass_kernel_spmd

F32 = mybir.dt.float32
F16 = mybir.dt.float16
BF16 = mybir.dt.bfloat16
I32 = mybir.dt.int32
AF = mybir.ActivationFunctionType
ALU = mybir.AluOpType
AX = mybir.AxisListType

B, T, D, H, HD = 2, 4096, 2048, 16, 128
R = 1024
NC = 8
SCALE = HD ** -0.5
EPS = 1e-5
NEG = -60000.0
TINY = 1e-30
QMAX = 127.0
QOFF = D * R
XFLAT = 2 * QOFF + 128

CHUNKS = []
ITILES = []
for _h in range(H):
    for _k in range(8):
        for _c in range(_k + 1):
            CHUNKS.append((len(CHUNKS), _h, _k, _c, _c == _k))
        ITILES.append((len(ITILES), _h, _k, _k + 1, len(CHUNKS) - 1))
G_TOT = len(CHUNKS)
M_TOT = len(ITILES)
M_OF_G = {}
for (_m, _h, _k, _nch, _gl) in ITILES:
    for _c in range(_nch):
        M_OF_G[_gl - _nch + 1 + _c] = _m
G0 = {h: 36 * h for h in range(H)}


class Sem:

    def __init__(self, handle):
        self.h = handle
        self.n = 0

    def dma(self, inst):
        inst.then_inc(self.h, 16)
        self.n += 16
        return inst

    def one(self, inst):
        inst.then_inc(self.h, 1)
        self.n += 1
        return inst


def build_nc(dump=()):
    nc = bass.Bass("TRN2", target_bir_lowering=False)

    x_in = nc.dram_tensor("x_own", [R, D], F32, kind="ExternalInput")
    qT_in = nc.dram_tensor("qT", [D, D], F32, kind="ExternalInput")
    kT_in = nc.dram_tensor("kT", [D, D], F32, kind="ExternalInput")
    vT_in = nc.dram_tensor("vT", [D, D], F32, kind="ExternalInput")
    qb_in = nc.dram_tensor("qb", [1, D], F32, kind="ExternalInput")
    kb_in = nc.dram_tensor("kb", [1, D], F32, kind="ExternalInput")
    vb_in = nc.dram_tensor("vb", [1, D], F32, kind="ExternalInput")
    oT_in = nc.dram_tensor("oT", [D, D], F32, kind="ExternalInput")
    ob_in = nc.dram_tensor("ob", [1, D], F32, kind="ExternalInput")
    fk_in = nc.dram_tensor("fk", [D, D], F32, kind="ExternalInput")
    fkT_in = nc.dram_tensor("fkT", [D, D], F32, kind="ExternalInput")
    fb_in = nc.dram_tensor("fb", [1, D], F32, kind="ExternalInput")
    c_in = nc.dram_tensor("c_own", [1, D], F32, kind="ExternalInput")
    wm_in = nc.dram_tensor("wm_own", [D, 3072], F32, kind="ExternalInput")
    ident_in = nc.dram_tensor("ident", [128, 128], BF16, kind="ExternalInput")
    pat_in = nc.dram_tensor("pats", [2, 128, 512], F16, kind="ExternalInput")
    t512_in = nc.dram_tensor("t512c", [128, 8], F32, kind="ExternalInput")
    i0bc_in = nc.dram_tensor("i0bc", [128, 8], F32, kind="ExternalInput")
    out_t = nc.dram_tensor("out", [R, D], F32, kind="ExternalOutput")

    xmod_i = nc.dram_tensor("xmod_i", [1, 3072], F32)
    xmod_o = nc.dram_tensor("xmod_o", [4, 3072], F32)
    xqv_i = nc.dram_tensor("xqv_i", [1, XFLAT], BF16)
    xqv_o = nc.dram_tensor("xqv_o", [4, XFLAT], BF16)
    xsy_o = nc.dram_tensor("xsy_o", [1, 16], F32)
    xsy2_o = nc.dram_tensor("xsy2_o", [1, 16], BF16)
    vec_s = nc.dram_tensor("vec_s", [16, D], F32)
    nrm_s = nc.dram_tensor("nrm_s", [H, R], F32)
    x2_d = nc.dram_tensor("x2_d", [R, D], F32)

    dump_t = {}
    for name, shp, dt in (("d_h1", [128, 16 * R], BF16), ("d_qfm", [128, 16 * R], BF16),
                          ("d_kfm", [128, 16 * R], BF16), ("d_vfm", [128, 16 * R], BF16),
                          ("d_bounds", [128, 128], F32), ("d_afm", [128, 16 * R], BF16),
                          ("d_x2", [128, 8 * D], F32), ("d_qx", [128, 16 * R], BF16),
                          ("d_mod", [4, 3072], F32), ("d_vecs", [16, D], F32),
                          ("d_bca", [128, 2 * D], F32), ("d_xqo", [4, QOFF], BF16)):
        if name in dump:
            dump_t[name] = nc.dram_tensor(name, shp, dt, kind="ExternalOutput")

    ES = contextlib.ExitStack()
    with ES:
        E = ES.enter_context

        vf = Sem(E(nc.semaphore("vf")))

        def fence(eng, inst):
            vf.n += 1
            inst.then_inc(vf.h, 1)
            eng.wait_ge(vf.h, vf.n)
            return inst

        s_ld = Sem(E(nc.semaphore("s_ld")))
        s_gl = Sem(E(nc.semaphore("s_gl")))
        s_cc = Sem(E(nc.semaphore("s_cc")))
        s_mm = Sem(E(nc.semaphore("s_mm")))
        s_ve = Sem(E(nc.semaphore("s_ve")))
        s_ac = Sem(E(nc.semaphore("s_ac")))
        s_st = Sem(E(nc.semaphore("s_st")))
        a_s = Sem(E(nc.semaphore("a_s")))
        a_t = Sem(E(nc.semaphore("a_t")))
        a_p = Sem(E(nc.semaphore("a_p")))
        a_tr = Sem(E(nc.semaphore("a_tr")))
        a_e = Sem(E(nc.semaphore("a_e")))
        a_av = Sem(E(nc.semaphore("a_av")))
        a_oe = Sem(E(nc.semaphore("a_oe")))
        a_ot = Sem(E(nc.semaphore("a_ot")))
        a_of = Sem(E(nc.semaphore("a_of")))
        a_qv = Sem(E(nc.semaphore("a_qv")))
        a_vt = Sem(E(nc.semaphore("a_vt")))
        a_ve = Sem(E(nc.semaphore("a_ve")))

        ident = E(nc.sbuf_tensor("identt", [128, 128], BF16))
        t512 = E(nc.sbuf_tensor("t512t", [128, 8], F32))
        i0bc = E(nc.sbuf_tensor("i0bct", [128, 8], F32))
        sxv = E(nc.sbuf_tensor("sxv", [128, 8], F32))

        st_h1 = contextlib.ExitStack()
        h1fm = st_h1.enter_context(nc.sbuf_tensor("h1fm", [128, 16 * R], BF16, side="left"))
        st_kfm = contextlib.ExitStack()
        kfm = st_kfm.enter_context(nc.sbuf_tensor("kfm", [128, 16 * R], BF16, side="right"))

        with contextlib.ExitStack() as P:
            PE = P.enter_context
            cvec = PE(nc.sbuf_tensor("cvec", [128, 16], F32))
            csil = PE(nc.sbuf_tensor("csil", [128, 16], BF16))
            wmt = [PE(nc.sbuf_tensor(f"wmt{i}", [128, 3072], BF16)) for i in range(2)]
            modv = PE(nc.sbuf_tensor("modv", [1, 3072], F32))
            mps = [PE(nc.psum_tensor(f"mps{j}", [1, 512], F32)) for j in range(6)]
            blk = PE(nc.Block())

            @blk.sync
            def _(sync):
                s_ld.dma(sync.dma_start(out=ident[:, :], in_=ident_in.ap()[:, :]))
                s_ld.dma(sync.dma_start(out=t512[:, :], in_=t512_in.ap()[:, :]))
                s_ld.dma(sync.dma_start(out=i0bc[:, :], in_=i0bc_in.ap()[:, :]))
                s_ld.dma(sync.dma_start(out=cvec[:, :],
                                        in_=bass.AP(c_in, 0, [[1, 128], [128, 16]]), allow_slow_non_contiguous=True))

            ld0, mm0, gl0 = s_ld.n, s_mm.n, s_gl.n

            @blk.gpsimd
            def _(gpsimd):
                for dc in range(16):
                    if dc >= 2:
                        gpsimd.wait_ge(s_mm.h, mm0 + (dc - 1) * 6)
                    s_gl.dma(gpsimd.dma_start(
                        out=wmt[dc % 2][:, :],
                        in_=wm_in.ap()[dc * 128:(dc + 1) * 128, :]))

            @blk.scalar
            def _(scalar):
                scalar.wait_ge(s_ld.h, ld0)
                s_ac.one(scalar.activation(csil[:, :], cvec[:, :], AF.Silu))

            ac0 = s_ac.n

            @blk.tensor
            def _(tensor):
                tensor.wait_ge(s_ac.h, ac0)
                for dc in range(16):
                    tensor.wait_ge(s_gl.h, gl0 + (dc + 1) * 16)
                    for j in range(6):
                        s_mm.one(tensor.matmul(
                            mps[j][:, :],
                            csil[:, dc:dc + 1],
                            wmt[dc % 2][:, j * 512:(j + 1) * 512],
                            start=(dc == 0), stop=(dc == 15)))

            mm0e = s_mm.n

            @blk.vector
            def _(vector):
                vector.wait_ge(s_mm.h, mm0e)
                for j in range(6):
                    s_ve.one(vector.tensor_copy(modv[:, j * 512:(j + 1) * 512],
                                                mps[j][:, :]))

            ve0 = s_ve.n

            @blk.gpsimd
            def _(gpsimd):
                gpsimd.wait_ge(s_ve.h, ve0)
                s_gl.dma(gpsimd.dma_start(out=xmod_i.ap()[:, :], in_=modv[:, :]))
                gpsimd.wait_ge(s_gl.h, s_gl.n)
                s_cc.one(gpsimd.collective_compute(
                    "AllGather", ALU.bypass,
                    replica_groups=[[0, 1, 2, 3], [4, 5, 6, 7]],
                    ins=[xmod_i.ap().opt()], outs=[xmod_o.ap().opt()]))
                s_cc.one(gpsimd.collective_compute(
                    "AllReduce", ALU.add,
                    replica_groups=[[0, 1, 2, 3], [4, 5, 6, 7]],
                    ins=[bass.AP(xmod_i, 0, [[1, 1], [1, 16]]).opt()],
                    outs=[xsy_o.ap().opt()]))
                gpsimd.wait_ge(s_cc.h, s_cc.n)
                if "d_mod" in dump_t:
                    s_gl.dma(gpsimd.dma_start(out=dump_t["d_mod"].ap()[:, :],
                                              in_=xmod_o.ap()[:, :]))
                    gpsimd.wait_ge(s_gl.h, s_gl.n)

        with contextlib.ExitStack() as P:
            PE = P.enter_context
            vrow = PE(nc.sbuf_tensor("vrow", [1, D], F32))
            vrow2 = PE(nc.sbuf_tensor("vrow2", [1, D], F32))
            vrow3 = PE(nc.sbuf_tensor("vrow3", [1, D], F32))
            msk = PE(nc.sbuf_tensor("msk", [1, D], mybir.dt.uint8))
            mtmp = PE(nc.sbuf_tensor("mtmp", [1, D], F32))
            mtmp2 = PE(nc.sbuf_tensor("mtmp2", [1, D], F32))
            mi32 = PE(nc.sbuf_tensor("mi32", [1, D], I32))
            sc1 = PE(nc.sbuf_tensor("sc1", [1, 1], F32))
            sc2 = PE(nc.sbuf_tensor("sc2", [1, 1], F32))
            blk = PE(nc.Block())

            def mod_sl(i):
                return bass.AP(xmod_o, i * D, [[1, 1], [1, D]])

            @blk.gpsimd
            def _(gpsimd):
                s_gl.dma(gpsimd.dma_start(out=vec_s.ap()[0:1, :], in_=mod_sl(0)))
                s_gl.dma(gpsimd.dma_start(out=vec_s.ap()[3:4, :], in_=mod_sl(3)))
                s_gl.dma(gpsimd.dma_start(out=vrow[:, :], in_=mod_sl(1)))
                s_gl.dma(gpsimd.dma_start(out=vrow2[:, :], in_=mod_sl(4)))
                s_gl.dma(gpsimd.dma_start(out=vrow3[:, :], in_=mod_sl(2)))

            gl1 = s_gl.n

            @blk.vector
            def _(vector):
                vector.wait_ge(s_gl.h, gl1)
                s_ve.one(vector.tensor_scalar_add(mtmp[:, :], vrow[:, :], 1.0))
                s_ve.one(vector.tensor_copy(vrow[:, :], mtmp[:, :]))
                s_ve.one(vector.tensor_scalar_add(mtmp[:, :], vrow2[:, :], 1.0))
                s_ve.one(vector.tensor_copy(vrow2[:, :], mtmp[:, :]))
                s_ve.one(vector.tensor_scalar(msk[:, :], vrow3[:, :], 0.0, None, ALU.is_ge))
                s_ve.one(vector.tensor_scalar_max(mtmp[:, :], vrow3[:, :], 0.25))
                s_ve.one(vector.tensor_scalar_min(mtmp2[:, :], vrow3[:, :], -0.25))
                s_ve.one(vector.select(vrow3[:, :], msk[:, :], mtmp[:, :], mtmp2[:, :]))

            ve1 = s_ve.n

            @blk.gpsimd
            def _(gpsimd):
                gpsimd.wait_ge(s_ve.h, ve1)
                s_gl.dma(gpsimd.dma_start(out=vec_s.ap()[1:2, :], in_=vrow[:, :]))
                s_gl.dma(gpsimd.dma_start(out=vec_s.ap()[4:5, :], in_=vrow2[:, :]))
                s_gl.dma(gpsimd.dma_start(out=vec_s.ap()[2:3, :], in_=vrow3[:, :]))
                s_gl.dma(gpsimd.dma_start(out=vrow3[:, :], in_=mod_sl(5)))

            gl2 = s_gl.n

            @blk.vector
            def _(vector):
                vector.wait_ge(s_gl.h, gl2)
                s_ve.one(vector.tensor_scalar(msk[:, :], vrow3[:, :], 0.0, None, ALU.is_ge))
                s_ve.one(vector.tensor_scalar_max(mtmp[:, :], vrow3[:, :], 0.25))
                s_ve.one(vector.tensor_scalar_min(mtmp2[:, :], vrow3[:, :], -0.25))
                s_ve.one(vector.select(vrow3[:, :], msk[:, :], mtmp[:, :], mtmp2[:, :]))

            ve2 = s_ve.n

            @blk.gpsimd
            def _(gpsimd):
                gpsimd.wait_ge(s_ve.h, ve2)
                s_gl.dma(gpsimd.dma_start(out=vec_s.ap()[5:6, :], in_=vrow3[:, :]))
                s_gl.dma(gpsimd.dma_start(out=vrow[:, :], in_=fb_in.ap()[0:1, :]))

            gl3 = s_gl.n

            @blk.vector
            def _(vector):
                vector.wait_ge(s_gl.h, gl3)
                s_ve.one(vector.tensor_reduce(sc1[:, :], vrow[:, :], AX.X, ALU.max,
                                              apply_absolute_value=True))
                vector.wait_ge(s_ve.h, s_ve.n)
                s_ve.one(vector.tensor_scalar(sc2[:, :], sc1[:, :], TINY, 1.0 / QMAX,
                                              ALU.max, ALU.mult))
                vector.wait_ge(s_ve.h, s_ve.n)
                s_ve.one(vector.reciprocal(mtmp[0:1, 0:1], sc2[:, :]))
                vector.wait_ge(s_ve.h, s_ve.n)
                s_ve.one(vector.tensor_scalar(mtmp2[:, :], vrow[:, :], mtmp[0:1, 0:1],
                                              None, ALU.mult))
                s_ve.one(vector.tensor_copy(mi32[:, :], mtmp2[:, :]))
                s_ve.one(vector.tensor_scalar(mtmp[:, :], mi32[:, :], QMAX, -QMAX,
                                              ALU.min, ALU.max))
                s_ve.one(vector.tensor_scalar(mtmp2[:, :], mtmp[:, :], sc2[0:1, 0:1],
                                              None, ALU.mult))

            ve3 = s_ve.n

            @blk.gpsimd
            def _(gpsimd):
                gpsimd.wait_ge(s_ve.h, ve3)
                s_gl.dma(gpsimd.dma_start(out=vec_s.ap()[6:7, :], in_=mtmp2[:, :]))
                gpsimd.wait_ge(s_gl.h, s_gl.n)

        with contextlib.ExitStack() as P:
            PE = P.enter_context
            bca = PE(nc.sbuf_tensor("bca", [128, D], F32))
            bcb = PE(nc.sbuf_tensor("bcb", [128, D], F32))
            xt = [PE(nc.sbuf_tensor(f"xt{i}", [128, D], F32)) for i in range(2)]
            lnt = PE(nc.sbuf_tensor("lnt", [128, D], F32))
            h1t = [PE(nc.sbuf_tensor(f"h1t{i}", [128, D], BF16)) for i in range(2)]
            bnb = PE(nc.sbuf_tensor("bnb", [128, 24], F32))
            b6a = PE(nc.sbuf_tensor("b6a", [128, 6], F32))
            b6b = PE(nc.sbuf_tensor("b6b", [128, 6], F32))
            b6c = PE(nc.sbuf_tensor("b6c", [128, 6], F32))
            sq24 = PE(nc.sbuf_tensor("sq24", [128, 24], F32))
            s6a = PE(nc.sbuf_tensor("s6a", [128, 6], F32))
            s6b = PE(nc.sbuf_tensor("s6b", [128, 6], F32))
            s6c = PE(nc.sbuf_tensor("s6c", [128, 6], F32))
            vvs = PE(nc.sbuf_tensor("vvs", [128, 12], F32))
            lnu = PE(nc.sbuf_tensor("lnu", [128, D], F32))
            mvec = PE(nc.sbuf_tensor("mvec", [128, 1], F32))
            vvar = [PE(nc.sbuf_tensor(f"vvar{i}", [128, 1], F32)) for i in range(2)]
            sqv = [PE(nc.sbuf_tensor(f"sqv{i}", [128, 1], F32)) for i in range(2)]
            rstd = PE(nc.sbuf_tensor("rstd", [128, 1], F32))
            htp = [PE(nc.psum_tensor(f"htp{i}", [128, 1024], BF16)) for i in range(2)]
            blk = PE(nc.Block())

            ld1, ve1b, ac1, mm1 = s_ld.n, s_ve.n, s_ac.n, s_mm.n

            @blk.sync
            def _(sync):
                s_ld.dma(sync.dma_start(out=bca[:, :],
                                        in_=bass.AP(vec_s, 1 * D, [[0, 128], [1, D]])))
                s_ld.dma(sync.dma_start(out=bcb[:, :],
                                        in_=bass.AP(vec_s, 0 * D, [[0, 128], [1, D]])))
                for rt in range(8):
                    if rt >= 2:
                        sync.wait_ge(s_ve.h, ve1b + (rt - 1) * 2)
                    s_ld.dma(sync.dma_start(out=xt[rt % 2][:, :],
                                            in_=x_in.ap()[rt * 128:(rt + 1) * 128, :]))

            @blk.vector
            def _(vector):
                for rt in range(8):
                    xx = xt[rt % 2]
                    vector.wait_ge(s_ld.h, ld1 + 32 + (rt + 1) * 16)
                    if rt >= 2:
                        vector.wait_ge(s_mm.h, mm1 + (rt - 1) * 16)
                    for q in range(4):
                        vector.bn_stats(bnb[:, q * 6:(q + 1) * 6],
                                        xx[:, q * 512:(q + 1) * 512])
                    vector.tensor_tensor(b6a[:, :], bnb[:, 0:6], bnb[:, 6:12], ALU.add)
                    vector.tensor_tensor(b6b[:, :], bnb[:, 12:18], bnb[:, 18:24], ALU.add)
                    vector.tensor_tensor(sq24[:, :], bnb[:, :], bnb[:, :], ALU.mult)
                    vector.tensor_tensor(b6c[:, :], b6a[:, :], b6b[:, :], ALU.add)
                    vector.tensor_tensor(s6a[:, :], sq24[:, 0:6], sq24[:, 6:12], ALU.add)
                    vector.tensor_tensor(s6b[:, :], sq24[:, 12:18], sq24[:, 18:24], ALU.add)
                    vector.tensor_tensor(vvs[:, 0:1], b6c[:, 1:2], b6c[:, 4:5], ALU.add)
                    vector.tensor_tensor(s6c[:, :], s6a[:, :], s6b[:, :], ALU.add)
                    vector.tensor_tensor(vvs[:, 1:2], b6c[:, 2:3], b6c[:, 5:6], ALU.add)
                    vector.tensor_scalar(mvec[:, :], vvs[:, 0:1], 1.0 / 8.0, None, ALU.mult)
                    vector.tensor_tensor(vvs[:, 3:4], s6c[:, 1:2], s6c[:, 4:5], ALU.add)
                    vector.tensor_scalar(vvs[:, 2:3], vvs[:, 1:2], 1.0 / D, None, ALU.mult)
                    vector.tensor_scalar(vvs[:, 4:5], vvs[:, 3:4], 1.0 / 8.0, None, ALU.mult)
                    vector.tensor_tensor(vvs[:, 6:7], mvec[:, :], mvec[:, :], ALU.mult)
                    fence(vector, vector.tensor_tensor(vvs[:, 5:6], vvs[:, 2:3],
                                                       vvs[:, 4:5], ALU.add))
                    fence(vector, vector.tensor_tensor(vvs[:, 7:8], vvs[:, 5:6],
                                                       vvs[:, 6:7], ALU.subtract))
                    s_ve.one(vector.tensor_scalar(vvar[rt % 2][:, :], vvs[:, 7:8],
                                                  1.0, EPS, ALU.mult, ALU.add))
                    vector.wait_ge(s_ac.h, ac1 + (rt + 1 if rt <= 1 else 5 * rt - 3))
                    fence(vector, vector.reciprocal(rstd[:, :], sqv[rt % 2][:, :]))
                    vector.tensor_scalar(lnt[:, :], xx[:, :], mvec[:, :], rstd[:, :],
                                         ALU.subtract, ALU.mult)
                    vector.tensor_tensor(lnu[:, :], lnt[:, :], bca[:, :], ALU.mult)
                    s_ve.one(vector.tensor_tensor(h1t[rt % 2][:, :], lnu[:, :],
                                                  bcb[:, :], ALU.add))

            evac_cnt1 = {}

            @blk.scalar
            def _(scalar):
                def evacs(rt):
                    for grp in range(4):
                        scalar.wait_ge(s_mm.h, mm1 + rt * 16 + (grp + 1) * 4)
                        s_ac.one(scalar.copy(
                            bass.AP(h1fm, (4 * grp) * R + rt * 128,
                                    [[16 * R, 128], [R, 4], [1, 128]]),
                            htp[grp % 2][:, 0:512]))
                        evac_cnt1[(rt, grp)] = s_ac.n
                for rt in range(8):
                    scalar.wait_ge(s_ve.h, ve1b + rt * 2 + 1)
                    s_ac.one(scalar.activation(sqv[rt % 2][:, :], vvar[rt % 2][:, :],
                                               AF.Sqrt))
                    if rt >= 1:
                        evacs(rt - 1)
                evacs(7)

            @blk.tensor
            def _(tensor):
                for rt in range(8):
                    tensor.wait_ge(s_ve.h, ve1b + rt * 2 + 2)
                    for dc in range(16):
                        grp = dc // 4
                        gidx = rt * 4 + grp
                        if gidx >= 2 and dc % 4 == 0:
                            prt, pgrp = (gidx - 2) // 4, (gidx - 2) % 4
                            tensor.wait_ge(s_ac.h, evac_cnt1[(prt, pgrp)])
                        s_mm.one(tensor.transpose(
                            htp[grp % 2][:, (dc % 4) * 128:((dc % 4) + 1) * 128],
                            h1t[rt % 2][:, dc * 128:(dc + 1) * 128],
                            ident[:, :]))

            @blk.gpsimd
            def _(gpsimd):
                if "d_vecs" in dump_t:
                    gpsimd.wait_ge(s_ld.h, ld1 + 32)
                    s_gl.dma(gpsimd.dma_start(out=dump_t["d_vecs"].ap()[:, :],
                                              in_=vec_s.ap()[:, :]))
                    if "d_bca" in dump_t:
                        s_gl.dma(gpsimd.dma_start(out=dump_t["d_bca"].ap()[:, 0:D],
                                                  in_=bca[:, :]))
                        s_gl.dma(gpsimd.dma_start(out=dump_t["d_bca"].ap()[:, D:2 * D],
                                                  in_=bcb[:, :]))
                    gpsimd.wait_ge(s_gl.h, s_gl.n)
                if "d_h1" in dump_t:
                    gpsimd.wait_ge(s_ac.h, s_ac.n)
                    s_gl.dma(gpsimd.dma_start(out=dump_t["d_h1"].ap()[:, :],
                                              in_=h1fm[:, :]))
                    gpsimd.wait_ge(s_gl.h, s_gl.n)

        with contextlib.ExitStack() as P:
            PE = P.enter_context
            wful = PE(nc.sbuf_tensor("wful", [128, 16 * D], BF16))
            qfm = PE(nc.sbuf_tensor("qfm", [128, 16 * R], BF16))
            vfm = PE(nc.sbuf_tensor("vfm", [128, 16 * R], BF16))
            bqv = PE(nc.sbuf_tensor("bqv", [128, 64], F32))
            on128 = PE(nc.sbuf_tensor("on128", [128, 128], BF16))
            qnp = [PE(nc.sbuf_tensor(f"qnp{i}", [128, 1], F32)) for i in range(2)]
            qntmp = PE(nc.sbuf_tensor("qntmp", [128, 1], F32))
            qnbf = PE(nc.sbuf_tensor("qnbf", [1, 1], BF16))
            pp = PE(nc.psum_tensor("pp", [128, 1024], F32))
            npp = [PE(nc.psum_tensor(f"npp{i}", [128, 512], F32)) for i in range(2)]
            blk = PE(nc.Block())

            mm2, ve2b, gl2b, ac2 = s_mm.n, s_ve.n, s_gl.n, s_ac.n

            @blk.gpsimd
            def _(gpsimd):
                for j, src in enumerate((qb_in, kb_in, vb_in)):
                    s_gl.dma(gpsimd.dma_start(
                        out=bqv[:, j * 16:(j + 1) * 16],
                        in_=bass.AP(src, 0, [[1, 128], [128, 16]]),
                        allow_slow_non_contiguous=True))
                s_gl.one(gpsimd.memset(on128[:, :], 1.0))
                for j, src in enumerate((qT_in, kT_in, vT_in)):
                    if j == 2:
                        gpsimd.wait_ge(s_ve.h, ve2b + 33)
                        for ft in range(16):
                            s_gl.dma(gpsimd.dma_start(
                                out=bass.AP(xqv_i, ft * 128 * R, [[R, 128], [1, R]]),
                                in_=qfm[:, ft * R:(ft + 1) * R]))
                    for dc in range(16):
                        if j >= 1:
                            gpsimd.wait_ge(s_mm.h,
                                           mm2 + (j - 1) * 512 + 15 * 32 + 16 + dc + 1)
                        s_gl.dma(gpsimd.dma_start(
                            out=wful[:, dc * D:(dc + 1) * D],
                            in_=src.ap()[dc * 128:(dc + 1) * 128, :]))

            @blk.tensor
            def _(tensor):
                for j in range(3):
                    for ft in range(16):
                        idx = j * 16 + ft
                        if j <= 1 and idx >= 1:
                            tensor.wait_ge(s_ve.h, ve2b + idx * 2)
                        elif j == 2 and ft >= 1:
                            tensor.wait_ge(s_ve.h, ve2b + 130 + ft * 2)
                        for rc in range(2):
                            for dc in range(16):
                                if ft == 0 and rc == 0:
                                    tensor.wait_ge(s_gl.h,
                                                   gl2b + 49 + (j * 16 + dc + 1) * 16)
                                s_mm.one(tensor.matmul(
                                    pp[:, rc * 512:(rc + 1) * 512],
                                    wful[:, dc * D + ft * 128: dc * D + (ft + 1) * 128],
                                    h1fm[:, dc * R + rc * 512: dc * R + (rc + 1) * 512],
                                    start=(dc == 0), stop=(dc == 15)))
                    if j == 1:
                        tensor.wait_ge(s_ve.h, ve2b + 64 + 1)
                        for ch in range(32):
                            if ch >= 2:
                                tensor.wait_ge(s_ve.h, ve2b + 65 + (ch - 1) * 2)
                            s_mm.one(tensor.matmul(npp[ch % 2][:, :], on128[:, :],
                                                   vfm[:, ch * 512:(ch + 1) * 512],
                                                   start=True, stop=True))

            @blk.vector
            def _(vector):
                vector.wait_ge(s_gl.h, gl2b + 48)
                vector.tensor_scalar(bqv[:, 48:64], bqv[:, 16:32], SCALE, None,
                                     ALU.mult)

                def mm_of(j, ft):
                    return mm2 + (j * 512 if j <= 1 else 1056) + (ft + 1) * 32

                def evacs(j, dst, bofs, scl):
                    for ft in range(16):
                        vector.wait_ge(s_mm.h, mm_of(j, ft))
                        for rc in range(2):
                            sl = dst[:, ft * R + rc * 512: ft * R + (rc + 1) * 512]
                            if scl is None:
                                inst = vector.tensor_scalar(
                                    sl, pp[:, rc * 512:(rc + 1) * 512],
                                    bqv[:, bofs + ft:bofs + ft + 1], None, ALU.add)
                            else:
                                inst = vector.tensor_scalar(
                                    sl, pp[:, rc * 512:(rc + 1) * 512],
                                    scl, bqv[:, 32 + bofs + ft:33 + bofs + ft],
                                    ALU.mult, ALU.add)
                            s_ve.one(inst)

                evacs(0, qfm, 0, None)
                evacs(1, kfm, 16, SCALE)
                s_ve.one(vector.tensor_tensor(vfm[:, :], qfm[:, :], qfm[:, :], ALU.mult))
                for ch in range(32):
                    vector.wait_ge(s_mm.h, mm2 + 1024 + ch + 1)
                    s_ve.one(vector.tensor_reduce(qntmp[:, :], npp[ch % 2][:, :],
                                                  AX.X, ALU.max))
                    vector.wait_ge(s_ve.h, s_ve.n)
                    if ch == 0:
                        s_ve.one(vector.tensor_copy(qnp[0][:, :], qntmp[:, :]))
                    else:
                        s_ve.one(vector.tensor_tensor(
                            qnp[ch % 2][:, :], qnp[(ch + 1) % 2][:, :],
                            qntmp[:, :], ALU.max))
                        vector.wait_ge(s_ve.h, s_ve.n)
                s_ve.one(vector.tensor_copy(qnbf[:, :], qnp[1][0:1, 0:1]))
                vector.wait_ge(s_mm.h, mm2 + 1056)
                evacs(2, vfm, 32, None)

            @blk.gpsimd
            def _(gpsimd):
                gpsimd.wait_ge(s_ve.h, s_ve.n)
                for ft in range(16):
                    s_gl.dma(gpsimd.dma_start(
                        out=bass.AP(xqv_i, QOFF + ft * 128 * R, [[R, 128], [1, R]]),
                        in_=vfm[:, ft * R:(ft + 1) * R]))
                s_gl.dma(gpsimd.dma_start(
                    out=bass.AP(xqv_i, 2 * QOFF, [[1, 1], [1, 1]]),
                    in_=qnbf[:, :]))
                gpsimd.wait_ge(s_gl.h, s_gl.n)
                s_cc.one(gpsimd.collective_compute(
                    "AllGather", ALU.bypass,
                    replica_groups=[[0, 1, 2, 3], [4, 5, 6, 7]],
                    ins=[xqv_i.ap().opt()], outs=[xqv_o.ap().opt()]))
                s_cc.one(gpsimd.collective_compute(
                    "AllReduce", ALU.add,
                    replica_groups=[[0, 1, 2, 3], [4, 5, 6, 7]],
                    ins=[bass.AP(xqv_i, 0, [[1, 1], [1, 16]]).opt()],
                    outs=[xsy2_o.ap().opt()]))
                if "d_qfm" in dump_t:
                    s_gl.dma(gpsimd.dma_start(out=dump_t["d_qfm"].ap()[:, :], in_=qfm[:, :]))
                if "d_kfm" in dump_t:
                    s_gl.dma(gpsimd.dma_start(out=dump_t["d_kfm"].ap()[:, :], in_=kfm[:, :]))
                if "d_vfm" in dump_t:
                    s_gl.dma(gpsimd.dma_start(out=dump_t["d_vfm"].ap()[:, :], in_=vfm[:, :]))
                gpsimd.wait_ge(s_gl.h, s_gl.n)

        st_h1.close()

        st_afm = contextlib.ExitStack()
        afm = st_afm.enter_context(nc.sbuf_tensor("afm", [128, 16 * R], BF16, side="left"))
        st_att = contextlib.ExitStack()
        AE = st_att.enter_context
        bounds = AE(nc.sbuf_tensor("bounds", [128, 128], F32, side="left"))
        boundsr = AE(nc.sbuf_tensor("boundsr", [128, 128], F32, side="left"))
        qh = [AE(nc.sbuf_tensor(f"qh{i}", [128, T], BF16, side="left")) for i in range(2)]
        vhfm = [AE(nc.sbuf_tensor(f"vhfm{i}", [128, T], BF16, side="left")) for i in range(2)]
        vrm = [AE(nc.sbuf_tensor(f"vrm{i}", [128, 32 * 129], BF16, side="left")) for i in range(2)]

        with contextlib.ExitStack() as P:
            PE = P.enter_context
            sqk = PE(nc.sbuf_tensor("sqk", [128, 16 * R], BF16))
            on128b = PE(nc.sbuf_tensor("on128b", [128, 128], BF16))
            normb = PE(nc.sbuf_tensor("normb", [128, 512], F32))
            qmt = PE(nc.sbuf_tensor("qmt", [1, 4], BF16))
            qmf = PE(nc.sbuf_tensor("qmf", [1, 4], F32))
            qms = PE(nc.sbuf_tensor("qms", [1, 1], F32))
            qms2 = PE(nc.sbuf_tensor("qms2", [1, 1], F32))
            qmaxv = PE(nc.sbuf_tensor("qmaxv", [128, 1], F32))
            nppb = [PE(nc.psum_tensor(f"nppb{i}", [128, 512], F32)) for i in range(2)]
            blk = PE(nc.Block())

            mm3, ve3b, ac3, gl3b, ld3 = s_mm.n, s_ve.n, s_ac.n, s_gl.n, s_ld.n

            @blk.gpsimd
            def _(gpsimd):
                s_gl.one(gpsimd.memset(on128b[:, :], 1.0))
                for h in range(H):
                    for ch in range(2):
                        i2 = h * 2 + ch
                        gpsimd.wait_ge(s_ac.h, ac3 + i2 + 1)
                        s_gl.dma(gpsimd.dma_start(
                            out=nrm_s.ap()[h:h + 1, ch * 512:(ch + 1) * 512],
                            in_=normb[0:1, 0:512]))
                gpsimd.wait_ge(s_cc.h, s_cc.n)
                s_gl.dma(gpsimd.dma_start(
                    out=qmt[:, :],
                    in_=bass.AP(xqv_o, 2 * QOFF, [[1, 1], [XFLAT, 4]]),
                    allow_slow_non_contiguous=True))

            @blk.vector
            def _(vector):
                vector.wait_ge(s_gl.h, gl3b + 1)
                s_ve.one(vector.tensor_tensor(sqk[:, :], kfm[:, :], kfm[:, :], ALU.mult))
                vector.wait_ge(s_gl.h, gl3b + 1 + 32 * 16 + 16)
                s_ve.one(vector.tensor_copy(qmf[:, :], qmt[:, :]))
                vector.wait_ge(s_ve.h, s_ve.n)
                s_ve.one(vector.tensor_reduce(qms[:, :], qmf[:, :], AX.X, ALU.max))

            @blk.tensor
            def _(tensor):
                tensor.wait_ge(s_ve.h, ve3b + 1)
                for h in range(H):
                    for ch in range(2):
                        i2 = h * 2 + ch
                        if i2 >= 2:
                            tensor.wait_ge(s_ac.h, ac3 + i2 - 1)
                        s_mm.one(tensor.matmul(
                            nppb[i2 % 2][:, :], on128b[:, :],
                            sqk[:, h * R + ch * 512: h * R + (ch + 1) * 512],
                            start=True, stop=True))

            @blk.scalar
            def _(scalar):
                for h in range(H):
                    for ch in range(2):
                        i2 = h * 2 + ch
                        scalar.wait_ge(s_mm.h, mm3 + i2 + 1)
                        if i2 >= 1:
                            scalar.wait_ge(s_gl.h, gl3b + 1 + i2 * 16)
                        s_ac.one(scalar.activation(normb[:, :], nppb[i2 % 2][:, :],
                                                   AF.Sqrt))
                scalar.wait_ge(s_ve.h, ve3b + 3)
                s_ac.one(scalar.activation(qms2[:, :], qms[:, :], AF.Sqrt))

            ac3e = s_ac.n

            @blk.gpsimd
            def _(gpsimd):
                gpsimd.wait_ge(s_ac.h, ac3e)
                s_gl.dma(gpsimd.dma_start(
                    out=bass.AP(vec_s, 9 * D, [[1, 1], [1, 1]]), in_=qms2[:, :]))

            gl3e = s_gl.n

            @blk.sync
            def _(sync):
                for h in range(H):
                    sync.wait_ge(s_gl.h, gl3b + 1 + (h + 1) * 32)
                    s_ld.dma(sync.dma_start(
                        out=boundsr[:, h * 8:(h + 1) * 8],
                        in_=bass.AP(nrm_s, h * R, [[1, 128], [128, 8]]),
                        allow_slow_non_contiguous=True))
                sync.wait_ge(s_gl.h, gl3e)
                s_ld.dma(sync.dma_start(out=qmaxv[:, :],
                                        in_=bass.AP(vec_s, 9 * D, [[0, 128], [1, 1]])))

            ld3e = s_ld.n

            @blk.vector
            def _(vector):
                vector.wait_ge(s_ld.h, ld3e)
                s_ve.one(vector.tensor_scalar(bounds[:, :], boundsr[:, :],
                                              qmaxv[:, :], None, ALU.mult))

            @blk.gpsimd
            def _(gpsimd):
                if "d_bounds" in dump_t:
                    gpsimd.wait_ge(s_ve.h, s_ve.n)
                    s_gl.dma(gpsimd.dma_start(out=dump_t["d_bounds"].ap()[:, :],
                                              in_=bounds[:, :]))
                    gpsimd.wait_ge(s_gl.h, s_gl.n)

        with contextlib.ExitStack() as P:
            PE = P.enter_context
            Tt = [PE(nc.sbuf_tensor(f"Tt{i}", [128, 512], F32)) for i in range(4)]
            Pt = [PE(nc.sbuf_tensor(f"Pt{i}", [128, 512], BF16)) for i in range(4)]
            PTs = [PE(nc.sbuf_tensor(f"PTs{i}", [128, 512], BF16)) for i in range(4)]
            pat = PE(nc.sbuf_tensor("pat", [128, 2 * 512], F16))
            bvs = [PE(nc.sbuf_tensor(f"bvs{i}", [128, 8], F32)) for i in range(2)]
            sums = [PE(nc.sbuf_tensor(f"sums{i}", [128, 8], F32)) for i in range(2)]
            sumt = [PE(nc.sbuf_tensor(f"sumt{i}", [128, 1], F32)) for i in range(2)]
            sumr = [PE(nc.sbuf_tensor(f"sumr{i}", [128, 1], F32)) for i in range(2)]
            arm = [PE(nc.sbuf_tensor(f"arm{i}", [128, 128], BF16)) for i in range(2)]
            S_ps = [PE(nc.psum_tensor(f"S_ps{i}", [128, 512], F32)) for i in range(2)]
            PT_ps = [PE(nc.psum_tensor(f"PT_ps{i}", [128, 1024], BF16)) for i in range(2)]
            O_ps = [PE(nc.psum_tensor(f"O_ps{i}", [128, 512], F32)) for i in range(2)]
            vtp = PE(nc.psum_tensor("vtp", [128, 1024], BF16))
            otp = PE(nc.psum_tensor("otp", [128, 1024], BF16))
            blk = PE(nc.Block())

            pld = s_ld.n + 16

            @blk.gpsimd
            def _(gpsimd):
                for i in range(2):
                    s_gl.one(gpsimd.memset(
                        bass.AP(vrm[i], 128, [[32 * 129, 128], [129, 32], [1, 1]]), 1.0))

            gl3bb = s_gl.n

            @blk.sync
            def _(sync):
                s_ld.dma(sync.dma_start(
                    out=pat[:, :],
                    in_=bass.AP(pat_in, 0, [[512, 128], [128 * 512, 2], [1, 512]])))
                for hh in range(H):
                    if hh >= 2:
                        sync.wait_ge(a_s.h, G0[hh - 1])
                        sync.wait_ge(a_av.h, G0[hh - 1])
                    for cp in range(4):
                        a_qv.dma(sync.dma_start(
                            out=bass.AP(qh[hh % 2], cp * 128,
                                        [[T, 128], [512, 8], [1, 128]]),
                            in_=bass.AP(xqv_o, cp * XFLAT + hh * 128 * R,
                                        [[R, 128], [128, 8], [1, 128]])))
                    for cp in range(4):
                        a_qv.dma(sync.dma_start(
                            out=bass.AP(vhfm[hh % 2], cp * 128,
                                        [[T, 128], [512, 8], [1, 128]]),
                            in_=bass.AP(xqv_o, cp * XFLAT + QOFF + hh * 128 * R,
                                        [[R, 128], [128, 8], [1, 128]])))

            vprep_T = {}
            vprep_V = {}
            for h in range(1, H):
                for grp in range(4):
                    vprep_T.setdefault(G0[h - 1] + 8 + 2 * grp, []).append((h, grp))
                    vprep_V.setdefault(G0[h - 1] + 9 + 2 * grp, []).append((h, grp))

            def v_transposes(tensor, h, grp):
                vt_idx = h * 4 + grp
                tensor.wait_ge(a_qv.h, 128 * (h + 1))
                if vt_idx >= 1:
                    tensor.wait_ge(a_ve.h, vt_idx)
                for s8 in range(8):
                    jt = grp * 8 + s8
                    inst = tensor.transpose(
                        vtp[:, s8 * 128:(s8 + 1) * 128],
                        vhfm[h % 2][:, jt * 128:(jt + 1) * 128],
                        ident[:, :])
                inst.then_inc(a_vt.h, 1)
                a_vt.n += 1

            def v_evac(vector, h, grp):
                vt_idx = h * 4 + grp
                vector.wait_ge(a_vt.h, vt_idx + 1)
                if h >= 2:
                    vector.wait_ge(a_av.h, G0[h - 1])
                vector.tensor_copy(
                    bass.AP(vrm[h % 2], grp * 8 * 129, [[32 * 129, 128], [129, 8], [1, 128]]),
                    vtp[:, :]).then_inc(a_ve.h, 1)
                a_ve.n += 1

            @blk.tensor
            def _(tensor):
                tensor.wait_ge(s_ld.h, pld)
                for grp in range(4):
                    v_transposes(tensor, 0, grp)

                def emit_S(g):
                    (gg, h, k, c, last) = CHUNKS[g]
                    if g >= 2:
                        tensor.wait_ge(a_t.h, g - 1)
                    tensor.matmul(S_ps[g % 2][:, :],
                                  kfm[:, h * R + k * 128: h * R + (k + 1) * 128],
                                  qh[h % 2][:, c * 512:(c + 1) * 512],
                                  start=True, stop=True).then_inc(a_s.h, 1)
                    a_s.n += 1

                def emit_TR(g):
                    tensor.wait_ge(a_p.h, g + 1)
                    if g >= 2:
                        tensor.wait_ge(a_e.h, g - 1)
                    for t4 in range(4):
                        inst = tensor.transpose(
                            PT_ps[g % 2][:, t4 * 128:(t4 + 1) * 128],
                            Pt[g % 4][:, t4 * 128:(t4 + 1) * 128],
                            ident[:, :])
                    inst.then_inc(a_tr.h, 1)
                    a_tr.n += 1

                def emit_AV(g):
                    (gg, h, k, c, last) = CHUNKS[g]
                    m = M_OF_G[g]
                    tensor.wait_ge(a_e.h, g + 1)
                    if c == 0 and m >= 2:
                        tensor.wait_ge(a_oe.h, m - 1)
                    for t4 in range(4):
                        jt = c * 4 + t4
                        inst = tensor.matmul(
                            O_ps[m % 2][:, 0:129],
                            PTs[g % 4][:, t4 * 128:(t4 + 1) * 128],
                            vrm[h % 2][:, jt * 129:jt * 129 + 129],
                            start=(c == 0 and t4 == 0), stop=(last and t4 == 3))
                    inst.then_inc(a_av.h, 1)
                    a_av.n += 1

                def emit_OT(m):
                    tensor.wait_ge(a_oe.h, m + 1)
                    if m >= 1:
                        tensor.wait_ge(a_of.h, m)
                    tensor.transpose(otp[:, 0:128], arm[m % 2][:, :],
                                     ident[:, :]).then_inc(a_ot.h, 1)
                    a_ot.n += 1

                tensor.wait_ge(s_gl.h, gl3bb)
                for g in range(G_TOT + 4):
                    if g < G_TOT:
                        emit_S(g)
                    for (h, grp) in vprep_T.get(g, ()):
                        v_transposes(tensor, h, grp)
                    if 1 <= g <= G_TOT:
                        emit_TR(g - 1)
                    if 2 <= g <= G_TOT + 1:
                        emit_AV(g - 2)
                    if 4 <= g <= G_TOT + 3 and CHUNKS[g - 4][4]:
                        emit_OT(M_OF_G[g - 4])

            @blk.vector
            def _(vector):
                vector.wait_ge(s_ld.h, pld)
                for grp in range(4):
                    v_evac(vector, 0, grp)

                def emit_stt(g):
                    (gg, h, k, c, last) = CHUNKS[g]
                    m = M_OF_G[g]
                    if c == 0:
                        if m >= 2:
                            vector.wait_ge(a_p.h, ITILES[m - 2][4] + 1)
                        vector.tensor_scalar(bvs[m % 2][:, :], t512[:, :],
                                             bounds[:, h * 8 + k: h * 8 + k + 1],
                                             i0bc[:, k:k + 1],
                                             ALU.subtract, ALU.subtract)
                    vector.wait_ge(a_s.h, g + 1)
                    if g >= 4:
                        vector.wait_ge(a_p.h, g - 3)
                    kp = 1 if last else 0
                    vector.tensor_tensor(Tt[g % 4][:, :], S_ps[g % 2][:, :],
                                         pat[:, kp * 512:(kp + 1) * 512],
                                         ALU.add).then_inc(a_t.h, 1)
                    a_t.n += 1

                def emit_evac_v(g):
                    vector.wait_ge(a_tr.h, g + 1)
                    if g >= 4:
                        vector.wait_ge(a_av.h, g - 3)
                    vector.tensor_copy(PTs[g % 4][:, :],
                                       PT_ps[g % 2][:, 0:512]).then_inc(a_e.h, 1)
                    a_e.n += 1

                def emit_finish1(m):
                    (mm_, h, k, nch, g_last) = ITILES[m]
                    vector.wait_ge(a_av.h, g_last + 1)
                    inst = vector.reciprocal(sumt[m % 2][:, :], O_ps[m % 2][:, 128:129])
                    if g_last >= G_TOT - 3:
                        fence(vector, inst)

                def emit_finish2(m):
                    (mm_, h, k, nch, g_last) = ITILES[m]
                    if m >= 2:
                        vector.wait_ge(a_ot.h, m - 1)
                    vector.tensor_scalar(arm[m % 2][:, :], O_ps[m % 2][:, 0:128],
                                         sumt[m % 2][:, :], None,
                                         ALU.mult).then_inc(a_oe.h, 1)
                    a_oe.n += 1

                for g in range(G_TOT + 2):
                    if g < G_TOT:
                        emit_stt(g)
                    for (h, grp) in vprep_V.get(g, ()):
                        v_evac(vector, h, grp)
                    if 1 <= g <= G_TOT and (g - 1) % 2 == 0:
                        emit_evac_v(g - 1)
                    if 1 <= g <= G_TOT and CHUNKS[g - 1][4]:
                        emit_finish1(M_OF_G[g - 1])
                    if 2 <= g <= G_TOT + 1 and CHUNKS[g - 2][4]:
                        emit_finish2(M_OF_G[g - 2])

            @blk.scalar
            def _(scalar):
                scalar.wait_ge(s_ld.h, pld)

                def emit_exp(g):
                    (gg, h, k, c, last) = CHUNKS[g]
                    m = M_OF_G[g]
                    scalar.wait_ge(a_t.h, g + 1)
                    if g >= 4:
                        scalar.wait_ge(a_tr.h, g - 3)
                    scalar.activation(Pt[g % 4][:, :], Tt[g % 4][:, :], AF.Exp,
                                      bias=bvs[m % 2][:, c:c + 1],
                                      scale=1.0).then_inc(a_p.h, 1)
                    a_p.n += 1

                def emit_evac_a(g):
                    scalar.wait_ge(a_tr.h, g + 1)
                    if g >= 4:
                        scalar.wait_ge(a_av.h, g - 3)
                    scalar.copy(PTs[g % 4][:, :],
                                PT_ps[g % 2][:, 0:512]).then_inc(a_e.h, 1)
                    a_e.n += 1

                def emit_afm(m):
                    (mm_, h, k, nch, g_last) = ITILES[m]
                    scalar.wait_ge(a_ot.h, m + 1)
                    scalar.copy(afm[:, h * R + k * 128: h * R + (k + 1) * 128],
                                otp[:, 0:128]).then_inc(a_of.h, 1)
                    a_of.n += 1

                for g in range(G_TOT + 1):
                    if g < G_TOT:
                        emit_exp(g)
                    if 1 <= g <= G_TOT and (g - 1) % 2 == 1:
                        emit_evac_a(g - 1)
                    if g >= 3 and CHUNKS[g - 3][4]:
                        emit_afm(M_OF_G[g - 3])
                for gx in (G_TOT - 2, G_TOT - 1):
                    if CHUNKS[gx][4]:
                        emit_afm(M_OF_G[gx])

            @blk.gpsimd
            def _(gpsimd):
                if "d_afm" in dump_t:
                    gpsimd.wait_ge(a_of.h, M_TOT)
                    s_gl.dma(gpsimd.dma_start(out=dump_t["d_afm"].ap()[:, :],
                                              in_=afm[:, :]))
                    gpsimd.wait_ge(s_gl.h, s_gl.n)

        st_att.close()
        st_kfm.close()

        with contextlib.ExitStack() as P:
            PE = P.enter_context
            x2 = PE(nc.sbuf_tensor("x2", [128, 8 * D], F32))
            ow = PE(nc.sbuf_tensor("ow", [128, 16 * D], BF16))
            bga = PE(nc.sbuf_tensor("bga", [128, D], F32))
            bob = PE(nc.sbuf_tensor("bob", [128, D], F32))
            t1 = PE(nc.sbuf_tensor("t1", [128, 512], F32))
            t2 = PE(nc.sbuf_tensor("t2", [128, 512], F32))
            po = PE(nc.psum_tensor("po", [128, 8 * 512], F32))
            blk = PE(nc.Block())

            mm4, ve4, gl4, ld4 = s_mm.n, s_ve.n, s_gl.n, s_ld.n

            @blk.gpsimd
            def _(gpsimd):
                for dc in range(16):
                    s_gl.dma(gpsimd.dma_start(
                        out=ow[:, dc * D:(dc + 1) * D],
                        in_=oT_in.ap()[dc * 128:(dc + 1) * 128, :]))

            @blk.sync
            def _(sync):
                s_ld.dma(sync.dma_start(out=bga[:, :],
                                        in_=bass.AP(vec_s, 2 * D, [[0, 128], [1, D]])))
                s_ld.dma(sync.dma_start(out=bob[:, :],
                                        in_=bass.AP(ob_in, 0, [[0, 128], [1, D]])))
                for rt in range(8):
                    s_ld.dma(sync.dma_start(
                        out=x2[:, rt * D:(rt + 1) * D],
                        in_=x_in.ap()[rt * 128:(rt + 1) * 128, :]))

            @blk.tensor
            def _(tensor):
                tensor.wait_ge(s_gl.h, gl4 + 256)
                for fg in range(4):
                    if fg >= 1:
                        tensor.wait_ge(s_ve.h, ve4 + fg * 8)
                    for dc in range(16):
                        for rt in range(8):
                            s_mm.one(tensor.matmul(
                                po[:, rt * 512:(rt + 1) * 512],
                                afm[:, dc * R + rt * 128: dc * R + (rt + 1) * 128],
                                ow[:, dc * D + fg * 512: dc * D + (fg + 1) * 512],
                                start=(dc == 0), stop=(dc == 15)))

            @blk.vector
            def _(vector):
                vector.wait_ge(s_ld.h, ld4 + 16 * 10)
                for fg in range(4):
                    vector.wait_ge(s_mm.h, mm4 + (fg + 1) * 128)
                    for rt in range(8):
                        vector.tensor_tensor(t1[:, :], po[:, rt * 512:(rt + 1) * 512],
                                             bob[:, fg * 512:(fg + 1) * 512], ALU.add)
                        vector.tensor_tensor(t2[:, :], t1[:, :],
                                             bga[:, fg * 512:(fg + 1) * 512], ALU.mult)
                        xsl = x2[:, rt * D + fg * 512: rt * D + (fg + 1) * 512]
                        s_ve.one(vector.tensor_tensor(xsl, xsl, t2[:, :], ALU.add))

            ve4e = s_ve.n

            @blk.gpsimd
            def _(gpsimd):
                gpsimd.wait_ge(s_ve.h, ve4e)
                for rt in range(8):
                    s_gl.dma(gpsimd.dma_start(out=x2_d.ap()[rt * 128:(rt + 1) * 128, :],
                                              in_=x2[:, rt * D:(rt + 1) * D]))
                gpsimd.wait_ge(s_gl.h, s_gl.n)
                if "d_x2" in dump_t:
                    s_gl.dma(gpsimd.dma_start(out=dump_t["d_x2"].ap()[:, :], in_=x2[:, :]))
                    gpsimd.wait_ge(s_gl.h, s_gl.n)

        st_afm.close()

        st_qx = contextlib.ExitStack()
        qxfm = st_qx.enter_context(nc.sbuf_tensor("qxfm", [128, 16 * R], BF16, side="left"))

        with contextlib.ExitStack() as P:
            PE = P.enter_context
            bcb1 = PE(nc.sbuf_tensor("bcb1", [128, D], F32))
            bsb = PE(nc.sbuf_tensor("bsb", [128, D], F32))
            lnt2 = PE(nc.sbuf_tensor("lnt2", [128, D], F32))
            h2t = PE(nc.sbuf_tensor("h2t", [128, D], F32))
            q_rm = [PE(nc.sbuf_tensor(f"q_rm{i}", [128, D], BF16)) for i in range(2)]
            qi = PE(nc.sbuf_tensor("qi", [128, D], I32))
            bnb2 = PE(nc.sbuf_tensor("bnb2", [128, 24], F32))
            c6a = PE(nc.sbuf_tensor("c6a", [128, 6], F32))
            c6b = PE(nc.sbuf_tensor("c6b", [128, 6], F32))
            c6c = PE(nc.sbuf_tensor("c6c", [128, 6], F32))
            sq24b = PE(nc.sbuf_tensor("sq24b", [128, 24], F32))
            d6a = PE(nc.sbuf_tensor("d6a", [128, 6], F32))
            d6b = PE(nc.sbuf_tensor("d6b", [128, 6], F32))
            d6c = PE(nc.sbuf_tensor("d6c", [128, 6], F32))
            vvs2 = PE(nc.sbuf_tensor("vvs2", [128, 12], F32))
            lnu2 = PE(nc.sbuf_tensor("lnu2", [128, D], F32))
            amx2 = PE(nc.sbuf_tensor("amx2", [128, 1], F32))
            mv2 = PE(nc.sbuf_tensor("mv2", [128, 1], F32))
            vva = [PE(nc.sbuf_tensor(f"vva{i}", [128, 1], F32)) for i in range(2)]
            sqa = [PE(nc.sbuf_tensor(f"sqa{i}", [128, 1], F32)) for i in range(2)]
            rst2 = PE(nc.sbuf_tensor("rst2", [128, 1], F32))
            amx = PE(nc.sbuf_tensor("amx", [128, 1], F32))
            r1v = PE(nc.sbuf_tensor("r1v", [128, 1], F32))
            htq = [PE(nc.psum_tensor(f"htq{i}", [128, 1024], BF16)) for i in range(2)]
            x2t = [PE(nc.sbuf_tensor(f"x2t{i}", [128, D], F32)) for i in range(2)]
            blk = PE(nc.Block())

            ve5, ac5, mm5, ld5 = s_ve.n, s_ac.n, s_mm.n, s_ld.n

            @blk.sync
            def _(sync):
                s_ld.dma(sync.dma_start(out=bcb1[:, :],
                                        in_=bass.AP(vec_s, 4 * D, [[0, 128], [1, D]])))
                s_ld.dma(sync.dma_start(out=bsb[:, :],
                                        in_=bass.AP(vec_s, 3 * D, [[0, 128], [1, D]])))
                for rt in range(8):
                    if rt >= 2:
                        sync.wait_ge(s_ve.h, ve5 + (rt - 1) * 2)
                    s_ld.dma(sync.dma_start(out=x2t[rt % 2][:, :],
                                            in_=x2_d.ap()[rt * 128:(rt + 1) * 128, :]))

            @blk.vector
            def _(vector):
                for rt in range(8):
                    xx = x2t[rt % 2][:, :]
                    vector.wait_ge(s_ld.h, ld5 + 32 + (rt + 1) * 16)
                    if rt >= 2:
                        vector.wait_ge(s_mm.h, mm5 + (rt - 1) * 16)
                    for q in range(4):
                        vector.bn_stats(bnb2[:, q * 6:(q + 1) * 6],
                                        xx[:, q * 512:(q + 1) * 512])
                    vector.tensor_tensor(c6a[:, :], bnb2[:, 0:6], bnb2[:, 6:12], ALU.add)
                    vector.tensor_tensor(c6b[:, :], bnb2[:, 12:18], bnb2[:, 18:24], ALU.add)
                    vector.tensor_tensor(sq24b[:, :], bnb2[:, :], bnb2[:, :], ALU.mult)
                    vector.tensor_tensor(c6c[:, :], c6a[:, :], c6b[:, :], ALU.add)
                    vector.tensor_tensor(d6a[:, :], sq24b[:, 0:6], sq24b[:, 6:12], ALU.add)
                    vector.tensor_tensor(d6b[:, :], sq24b[:, 12:18], sq24b[:, 18:24], ALU.add)
                    vector.tensor_tensor(vvs2[:, 0:1], c6c[:, 1:2], c6c[:, 4:5], ALU.add)
                    vector.tensor_tensor(d6c[:, :], d6a[:, :], d6b[:, :], ALU.add)
                    vector.tensor_tensor(vvs2[:, 1:2], c6c[:, 2:3], c6c[:, 5:6], ALU.add)
                    vector.tensor_scalar(mv2[:, :], vvs2[:, 0:1], 1.0 / 8.0, None, ALU.mult)
                    vector.tensor_tensor(vvs2[:, 3:4], d6c[:, 1:2], d6c[:, 4:5], ALU.add)
                    vector.tensor_scalar(vvs2[:, 2:3], vvs2[:, 1:2], 1.0 / D, None, ALU.mult)
                    vector.tensor_scalar(vvs2[:, 4:5], vvs2[:, 3:4], 1.0 / 8.0, None, ALU.mult)
                    vector.tensor_tensor(vvs2[:, 6:7], mv2[:, :], mv2[:, :], ALU.mult)
                    fence(vector, vector.tensor_tensor(vvs2[:, 5:6], vvs2[:, 2:3],
                                                       vvs2[:, 4:5], ALU.add))
                    fence(vector, vector.tensor_tensor(vvs2[:, 7:8], vvs2[:, 5:6],
                                                       vvs2[:, 6:7], ALU.subtract))
                    s_ve.one(vector.tensor_scalar(vva[rt % 2][:, :], vvs2[:, 7:8],
                                                  1.0, EPS, ALU.mult, ALU.add))
                    vector.wait_ge(s_ac.h, ac5 + (rt + 1 if rt <= 1 else 5 * rt - 3))
                    fence(vector, vector.reciprocal(rst2[:, :], sqa[rt % 2][:, :]))
                    vector.tensor_scalar(lnt2[:, :], xx[:, :], mv2[:, :], rst2[:, :],
                                         ALU.subtract, ALU.mult)
                    vector.tensor_tensor(lnu2[:, :], lnt2[:, :], bcb1[:, :], ALU.mult)
                    vector.tensor_tensor(h2t[:, :], lnu2[:, :], bsb[:, :], ALU.add)
                    fence(vector, vector.tensor_reduce(amx[:, :], h2t[:, :], AX.X,
                                                       ALU.max,
                                                       apply_absolute_value=True))
                    fence(vector, vector.tensor_scalar(amx2[:, :], amx[:, :], TINY,
                                                       None, ALU.max))
                    fence(vector, vector.reciprocal(r1v[:, :], amx2[:, :]))
                    vector.tensor_scalar(sxv[:, rt:rt + 1], amx2[:, :], 1.0 / QMAX,
                                         None, ALU.mult)
                    vector.tensor_scalar(lnu2[:, :], h2t[:, :], r1v[:, :], QMAX,
                                         ALU.mult, ALU.mult)
                    vector.tensor_copy(qi[:, :], lnu2[:, :])
                    s_ve.one(vector.tensor_scalar(q_rm[rt % 2][:, :], qi[:, :],
                                                  QMAX, -QMAX, ALU.min, ALU.max))

            evac_cnt5 = {}

            @blk.scalar
            def _(scalar):
                def evacs(rt):
                    for grp in range(4):
                        scalar.wait_ge(s_mm.h, mm5 + rt * 16 + (grp + 1) * 4)
                        s_ac.one(scalar.copy(
                            bass.AP(qxfm, (4 * grp) * R + rt * 128,
                                    [[16 * R, 128], [R, 4], [1, 128]]),
                            htq[grp % 2][:, 0:512]))
                        evac_cnt5[(rt, grp)] = s_ac.n
                for rt in range(8):
                    scalar.wait_ge(s_ve.h, ve5 + rt * 2 + 1)
                    s_ac.one(scalar.activation(sqa[rt % 2][:, :], vva[rt % 2][:, :],
                                               AF.Sqrt))
                    if rt >= 1:
                        evacs(rt - 1)
                evacs(7)

            @blk.tensor
            def _(tensor):
                for rt in range(8):
                    tensor.wait_ge(s_ve.h, ve5 + rt * 2 + 2)
                    for dc in range(16):
                        grp = dc // 4
                        gidx = rt * 4 + grp
                        if gidx >= 2 and dc % 4 == 0:
                            prt, pgrp = (gidx - 2) // 4, (gidx - 2) % 4
                            tensor.wait_ge(s_ac.h, evac_cnt5[(prt, pgrp)])
                        s_mm.one(tensor.transpose(
                            htq[grp % 2][:, (dc % 4) * 128:((dc % 4) + 1) * 128],
                            q_rm[rt % 2][:, dc * 128:(dc + 1) * 128],
                            ident[:, :]))

            @blk.gpsimd
            def _(gpsimd):
                if "d_qx" in dump_t:
                    gpsimd.wait_ge(s_ac.h, s_ac.n)
                    s_gl.dma(gpsimd.dma_start(out=dump_t["d_qx"].ap()[:, :], in_=qxfm[:, :]))
                    gpsimd.wait_ge(s_gl.h, s_gl.n)

        with contextlib.ExitStack() as P:
            PE = P.enter_context
            akv = PE(nc.sbuf_tensor("akv", [128, 16], F32))
            rkv = PE(nc.sbuf_tensor("rkv", [128, 16], F32))
            skv = PE(nc.sbuf_tensor("skv", [128, 16], F32))
            brk = PE(nc.sbuf_tensor("brk", [128, D], F32))
            qk = PE(nc.sbuf_tensor("qk", [128, 16 * D], BF16))
            ki32 = PE(nc.sbuf_tensor("ki32", [128, D], I32))
            p6 = PE(nc.psum_tensor("p6", [128, 8 * 512], F32))
            P6i = PE(contextlib.ExitStack())
            kt = [P6i.enter_context(nc.sbuf_tensor(f"kt{i}", [128, D], F32))
                  for i in range(2)]
            blk = P6i.enter_context(nc.Block())

            ve6, gl6, ld6, mm6, st6 = s_ve.n, s_gl.n, s_ld.n, s_mm.n, s_st.n

            @blk.sync
            def _(sync):
                for ft in range(16):
                    if ft >= 2:
                        sync.wait_ge(s_ve.h, ve6 + ft - 1)
                    s_ld.dma(sync.dma_start(out=kt[ft % 2][:, :],
                                            in_=fkT_in.ap()[ft * 128:(ft + 1) * 128, :]))

            @blk.vector
            def _(vector):
                for ft in range(16):
                    vector.wait_ge(s_ld.h, ld6 + (ft + 1) * 16)
                    s_ve.one(vector.tensor_reduce(
                        akv[:, ft:ft + 1], kt[ft % 2][:, :], AX.X, ALU.max,
                        apply_absolute_value=True))
                    if ft == 15:
                        vector.wait_ge(s_ve.h, s_ve.n)
                s_ve.one(vector.tensor_scalar(skv[:, :], akv[:, :], TINY, 1.0 / QMAX,
                                              ALU.max, ALU.mult))
                vector.wait_ge(s_ve.h, s_ve.n)
                s_ve.one(vector.reciprocal(rkv[:, :], skv[:, :]))

            @blk.gpsimd
            def _(gpsimd):
                gpsimd.wait_ge(s_ve.h, ve6 + 18)
                s_gl.dma(gpsimd.dma_start(
                    out=bass.AP(vec_s, 7 * D, [[1, 128], [128, 16]]), in_=skv[:, :],
                    allow_slow_non_contiguous=True))
                s_gl.dma(gpsimd.dma_start(
                    out=bass.AP(vec_s, 8 * D, [[1, 128], [128, 16]]), in_=rkv[:, :],
                    allow_slow_non_contiguous=True))

            @blk.sync
            def _(sync):
                sync.wait_ge(s_gl.h, gl6 + 32)
                s_ld.dma(sync.dma_start(out=brk[:, :],
                                        in_=bass.AP(vec_s, 8 * D, [[0, 128], [1, D]])))
                for dc in range(16):
                    if dc >= 2:
                        sync.wait_ge(s_ve.h, ve6 + 18 + dc - 1)
                    s_ld.dma(sync.dma_start(out=kt[dc % 2][:, :],
                                            in_=fk_in.ap()[dc * 128:(dc + 1) * 128, :]))

            @blk.vector
            def _(vector):
                for dc in range(16):
                    vector.wait_ge(s_ld.h, ld6 + 16 * 16 + 16 + (dc + 1) * 16)
                    vector.tensor_tensor(ki32[:, :], kt[dc % 2][:, :], brk[:, :], ALU.mult)
                    s_ve.one(vector.tensor_scalar(qk[:, dc * D:(dc + 1) * D], ki32[:, :],
                                                  QMAX, -QMAX, ALU.min, ALU.max))

            P6i.close()
            bsk = PE(nc.sbuf_tensor("bsk", [128, D], F32))
            bbq = PE(nc.sbuf_tensor("bbq", [128, D], F32))
            bgb = PE(nc.sbuf_tensor("bgb", [128, D], F32))
            t3 = PE(nc.sbuf_tensor("t3", [128, D], F32))
            t4 = PE(nc.sbuf_tensor("t4", [128, D], F32))
            t5 = PE(nc.sbuf_tensor("t5", [128, D], F32))
            ay2 = PE(nc.sbuf_tensor("ay2", [128, 1], F32))
            x2u = [PE(nc.sbuf_tensor(f"x2u{i}", [128, D], F32)) for i in range(2)]
            outt = [PE(nc.sbuf_tensor(f"outt{i}", [128, D], F32)) for i in range(2)]
            ay = PE(nc.sbuf_tensor("ay", [128, 1], F32))
            syv = PE(nc.sbuf_tensor("syv", [128, 1], F32))
            ryv = PE(nc.sbuf_tensor("ryv", [128, 1], F32))
            blk2 = PE(nc.Block())
            ld6b = s_ld.n

            @blk2.sync
            def _(sync):
                s_ld.dma(sync.dma_start(out=bsk[:, :],
                                        in_=bass.AP(vec_s, 7 * D, [[0, 128], [1, D]])))
                s_ld.dma(sync.dma_start(out=bbq[:, :],
                                        in_=bass.AP(vec_s, 6 * D, [[0, 128], [1, D]])))
                s_ld.dma(sync.dma_start(out=bgb[:, :],
                                        in_=bass.AP(vec_s, 5 * D, [[0, 128], [1, D]])))
                for rt in range(8):
                    if rt >= 2:
                        sync.wait_ge(s_ve.h, ve6 + 34 + rt - 1)
                    s_ld.dma(sync.dma_start(out=x2u[rt % 2][:, :],
                                            in_=x2_d.ap()[rt * 128:(rt + 1) * 128, :]))

            @blk2.tensor
            def _(tensor):
                for rt in range(8):
                    half = rt % 2
                    if rt >= 2:
                        tensor.wait_ge(s_ve.h, ve6 + 34 + (rt - 1))
                    for dc in range(16):
                        for fg in range(4):
                            s_mm.one(tensor.matmul(
                                p6[:, (half * 4 + fg) * 512:(half * 4 + fg + 1) * 512],
                                qxfm[:, dc * R + rt * 128: dc * R + (rt + 1) * 128],
                                qk[:, dc * D + fg * 512: dc * D + (fg + 1) * 512],
                                start=(dc == 0), stop=(dc == 15)))

            @blk2.vector
            def _(vector):
                for rt in range(8):
                    half = rt % 2
                    vector.wait_ge(s_mm.h, mm6 + (rt + 1) * 64)
                    vector.wait_ge(s_ld.h, ld6b + 48 + (rt + 1) * 16)
                    if rt >= 2:
                        vector.wait_ge(s_st.h, st6 + (rt - 1) * 16)
                    psl = p6[:, half * 2048:(half + 1) * 2048]
                    vector.scalar_tensor_tensor(t3[:, :], psl, sxv[:, rt:rt + 1],
                                                bsk[:, :], ALU.mult, ALU.mult)
                    vector.tensor_tensor(t4[:, :], t3[:, :], bbq[:, :], ALU.add)
                    vector.tensor_scalar(t5[:, :], t4[:, :], 0.0, None, ALU.max)
                    fence(vector, vector.tensor_reduce(ay[:, :], t5[:, :], AX.X,
                                                       ALU.max,
                                                       apply_absolute_value=True))
                    fence(vector, vector.tensor_scalar(ay2[:, :], ay[:, :], TINY,
                                                       None, ALU.max))
                    fence(vector, vector.reciprocal(ryv[:, :], ay2[:, :]))
                    vector.tensor_scalar(syv[:, :], ay2[:, :], 1.0 / QMAX, None, ALU.mult)
                    vector.tensor_scalar(t3[:, :], t5[:, :], ryv[:, :], QMAX,
                                         ALU.mult, ALU.mult)
                    vector.tensor_copy(ki32[:, :], t3[:, :])
                    vector.tensor_scalar(t4[:, :], ki32[:, :], QMAX, -QMAX,
                                         ALU.min, ALU.max)
                    vector.scalar_tensor_tensor(t3[:, :], t4[:, :], syv[:, :],
                                                bgb[:, :], ALU.mult, ALU.mult)
                    s_ve.one(vector.tensor_tensor(outt[rt % 2][:, :],
                                                  x2u[rt % 2][:, :],
                                                  t3[:, :], ALU.add))

            @blk2.gpsimd
            def _(gpsimd):
                for rt in range(8):
                    gpsimd.wait_ge(s_ve.h, ve6 + 34 + rt + 1)
                    s_st.dma(gpsimd.dma_start(out=out_t.ap()[rt * 128:(rt + 1) * 128, :],
                                              in_=outt[rt % 2][:, :]))
                gpsimd.wait_ge(s_st.h, s_st.n)

        st_qx.close()

    return nc


def _host_consts():
    ident = np.eye(128, dtype=ml_dtypes.bfloat16)
    jj = np.arange(512, dtype=np.float32)[None, :]
    ip = np.arange(128, dtype=np.float32)[:, None]
    pat_off = (jj - ip).astype(np.float32)
    t512 = np.broadcast_to(512.0 * np.arange(8, dtype=np.float32)[None, :],
                           (128, 8)).copy()
    return ident, pat_off, t512


def _run(dump, L0_attn_x, L0_attn_q_w, L0_attn_q_b, L0_attn_k_w, L0_attn_k_b,
         L0_attn_v_w, L0_attn_v_b, L0_attn_o_w, L0_attn_o_b,
         L0_ffn_kernel, L0_ffn_bias, L0_c, L0_wmod):
    ident, pat_off, t512 = _host_consts()
    x = np.ascontiguousarray(np.asarray(L0_attn_x, dtype=np.float32).reshape(B * T, D))
    qT = np.ascontiguousarray(np.asarray(L0_attn_q_w, dtype=np.float32).T)
    kT = np.ascontiguousarray(np.asarray(L0_attn_k_w, dtype=np.float32).T)
    vT = np.ascontiguousarray(np.asarray(L0_attn_v_w, dtype=np.float32).T)
    oT = np.ascontiguousarray(np.asarray(L0_attn_o_w, dtype=np.float32).T)
    fk = np.ascontiguousarray(np.asarray(L0_ffn_kernel, dtype=np.float32))
    fkT = np.ascontiguousarray(fk.T)
    qb = np.asarray(L0_attn_q_b, dtype=np.float32).reshape(1, D)
    kb = np.asarray(L0_attn_k_b, dtype=np.float32).reshape(1, D)
    vb = np.asarray(L0_attn_v_b, dtype=np.float32).reshape(1, D)
    ob = np.asarray(L0_attn_o_b, dtype=np.float32).reshape(1, D)
    fb = np.asarray(L0_ffn_bias, dtype=np.float32).reshape(1, D)
    cc = np.asarray(L0_c, dtype=np.float32)
    wm = np.ascontiguousarray(np.asarray(L0_wmod, dtype=np.float32))

    nc = build_nc(dump)
    in_maps = []
    for core in range(NC):
        b, cp = core // 4, core % 4
        gtiles = cp + 4 * np.arange(8)
        rows = (b * T + (gtiles[:, None] * 128 + np.arange(128)[None, :])).reshape(-1)
        x_own = np.ascontiguousarray(x[rows])
        jj = np.arange(512, dtype=np.float32)[None, :]
        ip = np.arange(128, dtype=np.float32)[:, None]
        pat_diag = np.where(jj > 128.0 * cp + ip, np.float32(NEG), jj - ip)
        pats = np.stack([pat_off, pat_diag]).astype(np.float16)
        i0 = (512.0 * np.arange(8) + 128.0 * cp).astype(np.float32)
        i0bc = np.broadcast_to(i0[None, :], (128, 8)).copy()
        in_maps.append({
            "x_own": x_own, "qT": qT, "kT": kT, "vT": vT,
            "qb": qb, "kb": kb, "vb": vb, "oT": oT, "ob": ob,
            "fk": fk, "fkT": fkT, "fb": fb,
            "c_own": np.ascontiguousarray(cc[b].reshape(1, D)),
            "wm_own": np.ascontiguousarray(wm[:, cp * 3072:(cp + 1) * 3072]),
            "ident": ident, "pats": pats,
            "t512c": t512, "i0bc": i0bc,
        })
    res = run_bass_kernel_spmd(nc, in_maps, core_ids=list(range(NC)))
    return res


def core_rows(core):
    b, cp = core // 4, core % 4
    gtiles = cp + 4 * np.arange(8)
    return (b * T + (gtiles[:, None] * 128 + np.arange(128)[None, :])).reshape(-1)


def kernel(**inputs):
    res = _run((), **inputs)
    out = np.empty((B * T, D), dtype=np.float32)
    for core in range(NC):
        out[core_rows(core)] = res.results[core]["out"]
    return out.reshape(B, T, D)


# version 06
# speedup vs baseline: 1.2316x; 1.1349x over previous
import contextlib
import numpy as np
import ml_dtypes
import concourse.bass as bass
import concourse.mybir as mybir
from concourse.bass_utils import run_bass_kernel_spmd

F32 = mybir.dt.float32
F16 = mybir.dt.float16
BF16 = mybir.dt.bfloat16
I32 = mybir.dt.int32
AF = mybir.ActivationFunctionType
ALU = mybir.AluOpType
AX = mybir.AxisListType

B, T, D, H, HD = 2, 4096, 2048, 16, 128
R = 1024
NC = 8
SCALE = HD ** -0.5
EPS = 1e-5
NEG = -60000.0
TINY = 1e-30
QMAX = 127.0
QOFF = D * R
XFLAT = 2 * QOFF + 128

WIN = 3
CHUNKS = []
ITILES = []
for _h in range(H):
    for _k in range(8):
        _c0 = max(0, _k + 1 - WIN)
        for _c in range(_c0, _k + 1):
            CHUNKS.append((len(CHUNKS), _h, _k, _c, _c == _k))
        ITILES.append((len(ITILES), _h, _k, _k + 1 - _c0, len(CHUNKS) - 1))
G_TOT = len(CHUNKS)
M_TOT = len(ITILES)
M_OF_G = {}
for (_m, _h, _k, _nch, _gl) in ITILES:
    for _c in range(_nch):
        M_OF_G[_gl - _nch + 1 + _c] = _m
G0 = {h: (G_TOT // H) * h for h in range(H)}


class Sem:

    def __init__(self, handle):
        self.h = handle
        self.n = 0

    def dma(self, inst):
        inst.then_inc(self.h, 16)
        self.n += 16
        return inst

    def one(self, inst):
        inst.then_inc(self.h, 1)
        self.n += 1
        return inst


def build_nc(dump=()):
    nc = bass.Bass("TRN2", target_bir_lowering=False)

    x_in = nc.dram_tensor("x_own", [R, D], F32, kind="ExternalInput")
    qT_in = nc.dram_tensor("qT", [D, D], F32, kind="ExternalInput")
    kT_in = nc.dram_tensor("kT", [D, D], F32, kind="ExternalInput")
    vT_in = nc.dram_tensor("vT", [D, D], F32, kind="ExternalInput")
    qb_in = nc.dram_tensor("qb", [1, D], F32, kind="ExternalInput")
    kb_in = nc.dram_tensor("kb", [1, D], F32, kind="ExternalInput")
    vb_in = nc.dram_tensor("vb", [1, D], F32, kind="ExternalInput")
    oT_in = nc.dram_tensor("oT", [D, D], F32, kind="ExternalInput")
    ob_in = nc.dram_tensor("ob", [1, D], F32, kind="ExternalInput")
    fk_in = nc.dram_tensor("fk", [D, D], F32, kind="ExternalInput")
    fkT_in = nc.dram_tensor("fkT", [D, D], F32, kind="ExternalInput")
    fb_in = nc.dram_tensor("fb", [1, D], F32, kind="ExternalInput")
    c_in = nc.dram_tensor("c_own", [1, D], F32, kind="ExternalInput")
    wm_in = nc.dram_tensor("wm_own", [D, 3072], F32, kind="ExternalInput")
    ident_in = nc.dram_tensor("ident", [128, 128], BF16, kind="ExternalInput")
    pat_in = nc.dram_tensor("pats", [2, 128, 512], F16, kind="ExternalInput")
    t512_in = nc.dram_tensor("t512c", [128, 8], F32, kind="ExternalInput")
    i0bc_in = nc.dram_tensor("i0bc", [128, 8], F32, kind="ExternalInput")
    out_t = nc.dram_tensor("out", [R, D], F32, kind="ExternalOutput")

    xmod_i = nc.dram_tensor("xmod_i", [1, 3072], F32)
    xmod_o = nc.dram_tensor("xmod_o", [4, 3072], F32)
    xqv_i = nc.dram_tensor("xqv_i", [1, XFLAT], BF16)
    xqv_o = nc.dram_tensor("xqv_o", [4, XFLAT], BF16)
    xsy_o = nc.dram_tensor("xsy_o", [1, 16], F32)
    xsy2_o = nc.dram_tensor("xsy2_o", [1, 16], BF16)
    vec_s = nc.dram_tensor("vec_s", [16, D], F32)
    nrm_s = nc.dram_tensor("nrm_s", [H, R], F32)
    x2_d = nc.dram_tensor("x2_d", [R, D], F32)

    dump_t = {}
    for name, shp, dt in (("d_h1", [128, 16 * R], BF16), ("d_qfm", [128, 16 * R], BF16),
                          ("d_kfm", [128, 16 * R], BF16), ("d_vfm", [128, 16 * R], BF16),
                          ("d_bounds", [128, 128], F32), ("d_afm", [128, 16 * R], BF16),
                          ("d_x2", [128, 8 * D], F32), ("d_qx", [128, 16 * R], BF16),
                          ("d_mod", [4, 3072], F32), ("d_vecs", [16, D], F32),
                          ("d_bca", [128, 2 * D], F32), ("d_xqo", [4, QOFF], BF16)):
        if name in dump:
            dump_t[name] = nc.dram_tensor(name, shp, dt, kind="ExternalOutput")

    ES = contextlib.ExitStack()
    with ES:
        E = ES.enter_context

        vf = Sem(E(nc.semaphore("vf")))

        def fence(eng, inst):
            vf.n += 1
            inst.then_inc(vf.h, 1)
            eng.wait_ge(vf.h, vf.n)
            return inst

        s_ld = Sem(E(nc.semaphore("s_ld")))
        s_gl = Sem(E(nc.semaphore("s_gl")))
        s_cc = Sem(E(nc.semaphore("s_cc")))
        s_mm = Sem(E(nc.semaphore("s_mm")))
        s_ve = Sem(E(nc.semaphore("s_ve")))
        s_ac = Sem(E(nc.semaphore("s_ac")))
        s_st = Sem(E(nc.semaphore("s_st")))
        a_s = Sem(E(nc.semaphore("a_s")))
        a_t = Sem(E(nc.semaphore("a_t")))
        a_p = Sem(E(nc.semaphore("a_p")))
        a_tr = Sem(E(nc.semaphore("a_tr")))
        a_e = Sem(E(nc.semaphore("a_e")))
        a_av = Sem(E(nc.semaphore("a_av")))
        a_oe = Sem(E(nc.semaphore("a_oe")))
        a_ot = Sem(E(nc.semaphore("a_ot")))
        a_of = Sem(E(nc.semaphore("a_of")))
        a_qv = Sem(E(nc.semaphore("a_qv")))
        a_vq = Sem(E(nc.semaphore("a_vq")))
        a_vt = Sem(E(nc.semaphore("a_vt")))
        a_ve = Sem(E(nc.semaphore("a_ve")))

        ident = E(nc.sbuf_tensor("identt", [128, 128], BF16))
        t512 = E(nc.sbuf_tensor("t512t", [128, 8], F32))
        i0bc = E(nc.sbuf_tensor("i0bct", [128, 8], F32))
        sxv = E(nc.sbuf_tensor("sxv", [128, 8], F32))

        st_h1 = contextlib.ExitStack()
        h1fm = st_h1.enter_context(nc.sbuf_tensor("h1fm", [128, 16 * R], BF16, side="left"))
        st_kfm = contextlib.ExitStack()
        kfm = st_kfm.enter_context(nc.sbuf_tensor("kfm", [128, 16 * R], BF16, side="right"))

        with contextlib.ExitStack() as P:
            PE = P.enter_context
            cvec = PE(nc.sbuf_tensor("cvec", [128, 16], F32))
            csil = PE(nc.sbuf_tensor("csil", [128, 16], BF16))
            wmt = [PE(nc.sbuf_tensor(f"wmt{i}", [128, 3072], BF16)) for i in range(2)]
            modv = PE(nc.sbuf_tensor("modv", [1, 3072], F32))
            mps = [PE(nc.psum_tensor(f"mps{j}", [1, 512], F32)) for j in range(6)]
            blk = PE(nc.Block())

            @blk.sync
            def _(sync):
                s_ld.dma(sync.dma_start(out=ident[:, :], in_=ident_in.ap()[:, :]))
                s_ld.dma(sync.dma_start(out=t512[:, :], in_=t512_in.ap()[:, :]))
                s_ld.dma(sync.dma_start(out=i0bc[:, :], in_=i0bc_in.ap()[:, :]))
                s_ld.dma(sync.dma_start(out=cvec[:, :],
                                        in_=bass.AP(c_in, 0, [[1, 128], [128, 16]]), allow_slow_non_contiguous=True))

            ld0, mm0, gl0 = s_ld.n, s_mm.n, s_gl.n

            @blk.gpsimd
            def _(gpsimd):
                for dc in range(16):
                    if dc >= 2:
                        gpsimd.wait_ge(s_mm.h, mm0 + (dc - 1) * 6)
                    s_gl.dma(gpsimd.dma_start(
                        out=wmt[dc % 2][:, :],
                        in_=wm_in.ap()[dc * 128:(dc + 1) * 128, :]))

            @blk.scalar
            def _(scalar):
                scalar.wait_ge(s_ld.h, ld0)
                s_ac.one(scalar.activation(csil[:, :], cvec[:, :], AF.Silu))

            ac0 = s_ac.n

            @blk.tensor
            def _(tensor):
                tensor.wait_ge(s_ac.h, ac0)
                for dc in range(16):
                    tensor.wait_ge(s_gl.h, gl0 + (dc + 1) * 16)
                    for j in range(6):
                        s_mm.one(tensor.matmul(
                            mps[j][:, :],
                            csil[:, dc:dc + 1],
                            wmt[dc % 2][:, j * 512:(j + 1) * 512],
                            start=(dc == 0), stop=(dc == 15)))

            mm0e = s_mm.n

            @blk.vector
            def _(vector):
                vector.wait_ge(s_mm.h, mm0e)
                for j in range(6):
                    s_ve.one(vector.tensor_copy(modv[:, j * 512:(j + 1) * 512],
                                                mps[j][:, :]))

            ve0 = s_ve.n

            @blk.gpsimd
            def _(gpsimd):
                gpsimd.wait_ge(s_ve.h, ve0)
                s_gl.dma(gpsimd.dma_start(out=xmod_i.ap()[:, :], in_=modv[:, :]))
                gpsimd.wait_ge(s_gl.h, s_gl.n)
                s_cc.one(gpsimd.collective_compute(
                    "AllGather", ALU.bypass,
                    replica_groups=[[0, 1, 2, 3], [4, 5, 6, 7]],
                    ins=[xmod_i.ap().opt()], outs=[xmod_o.ap().opt()]))
                s_cc.one(gpsimd.collective_compute(
                    "AllReduce", ALU.add,
                    replica_groups=[[0, 1, 2, 3], [4, 5, 6, 7]],
                    ins=[bass.AP(xmod_i, 0, [[1, 1], [1, 16]]).opt()],
                    outs=[xsy_o.ap().opt()]))
                gpsimd.wait_ge(s_cc.h, s_cc.n)
                if "d_mod" in dump_t:
                    s_gl.dma(gpsimd.dma_start(out=dump_t["d_mod"].ap()[:, :],
                                              in_=xmod_o.ap()[:, :]))
                    gpsimd.wait_ge(s_gl.h, s_gl.n)

        with contextlib.ExitStack() as P:
            PE = P.enter_context
            vrow = PE(nc.sbuf_tensor("vrow", [1, D], F32))
            vrow2 = PE(nc.sbuf_tensor("vrow2", [1, D], F32))
            vrow3 = PE(nc.sbuf_tensor("vrow3", [1, D], F32))
            msk = PE(nc.sbuf_tensor("msk", [1, D], mybir.dt.uint8))
            mtmp = PE(nc.sbuf_tensor("mtmp", [1, D], F32))
            mtmp2 = PE(nc.sbuf_tensor("mtmp2", [1, D], F32))
            mi32 = PE(nc.sbuf_tensor("mi32", [1, D], I32))
            sc1 = PE(nc.sbuf_tensor("sc1", [1, 1], F32))
            sc2 = PE(nc.sbuf_tensor("sc2", [1, 1], F32))
            blk = PE(nc.Block())

            def mod_sl(i):
                return bass.AP(xmod_o, i * D, [[1, 1], [1, D]])

            @blk.gpsimd
            def _(gpsimd):
                s_gl.dma(gpsimd.dma_start(out=vec_s.ap()[0:1, :], in_=mod_sl(0)))
                s_gl.dma(gpsimd.dma_start(out=vec_s.ap()[3:4, :], in_=mod_sl(3)))
                s_gl.dma(gpsimd.dma_start(out=vrow[:, :], in_=mod_sl(1)))
                s_gl.dma(gpsimd.dma_start(out=vrow2[:, :], in_=mod_sl(4)))
                s_gl.dma(gpsimd.dma_start(out=vrow3[:, :], in_=mod_sl(2)))

            gl1 = s_gl.n

            @blk.vector
            def _(vector):
                vector.wait_ge(s_gl.h, gl1)
                s_ve.one(vector.tensor_scalar_add(mtmp[:, :], vrow[:, :], 1.0))
                s_ve.one(vector.tensor_copy(vrow[:, :], mtmp[:, :]))
                s_ve.one(vector.tensor_scalar_add(mtmp[:, :], vrow2[:, :], 1.0))
                s_ve.one(vector.tensor_copy(vrow2[:, :], mtmp[:, :]))
                s_ve.one(vector.tensor_scalar(msk[:, :], vrow3[:, :], 0.0, None, ALU.is_ge))
                s_ve.one(vector.tensor_scalar_max(mtmp[:, :], vrow3[:, :], 0.25))
                s_ve.one(vector.tensor_scalar_min(mtmp2[:, :], vrow3[:, :], -0.25))
                s_ve.one(vector.select(vrow3[:, :], msk[:, :], mtmp[:, :], mtmp2[:, :]))

            ve1 = s_ve.n

            @blk.gpsimd
            def _(gpsimd):
                gpsimd.wait_ge(s_ve.h, ve1)
                s_gl.dma(gpsimd.dma_start(out=vec_s.ap()[1:2, :], in_=vrow[:, :]))
                s_gl.dma(gpsimd.dma_start(out=vec_s.ap()[4:5, :], in_=vrow2[:, :]))
                s_gl.dma(gpsimd.dma_start(out=vec_s.ap()[2:3, :], in_=vrow3[:, :]))
                s_gl.dma(gpsimd.dma_start(out=vrow3[:, :], in_=mod_sl(5)))

            gl2 = s_gl.n

            @blk.vector
            def _(vector):
                vector.wait_ge(s_gl.h, gl2)
                s_ve.one(vector.tensor_scalar(msk[:, :], vrow3[:, :], 0.0, None, ALU.is_ge))
                s_ve.one(vector.tensor_scalar_max(mtmp[:, :], vrow3[:, :], 0.25))
                s_ve.one(vector.tensor_scalar_min(mtmp2[:, :], vrow3[:, :], -0.25))
                s_ve.one(vector.select(vrow3[:, :], msk[:, :], mtmp[:, :], mtmp2[:, :]))

            ve2 = s_ve.n

            @blk.gpsimd
            def _(gpsimd):
                gpsimd.wait_ge(s_ve.h, ve2)
                s_gl.dma(gpsimd.dma_start(out=vec_s.ap()[5:6, :], in_=vrow3[:, :]))
                s_gl.dma(gpsimd.dma_start(out=vrow[:, :], in_=fb_in.ap()[0:1, :]))

            gl3 = s_gl.n

            @blk.vector
            def _(vector):
                vector.wait_ge(s_gl.h, gl3)
                s_ve.one(vector.tensor_reduce(sc1[:, :], vrow[:, :], AX.X, ALU.max,
                                              apply_absolute_value=True))
                vector.wait_ge(s_ve.h, s_ve.n)
                s_ve.one(vector.tensor_scalar(sc2[:, :], sc1[:, :], TINY, 1.0 / QMAX,
                                              ALU.max, ALU.mult))
                vector.wait_ge(s_ve.h, s_ve.n)
                s_ve.one(vector.reciprocal(mtmp[0:1, 0:1], sc2[:, :]))
                vector.wait_ge(s_ve.h, s_ve.n)
                s_ve.one(vector.tensor_scalar(mtmp2[:, :], vrow[:, :], mtmp[0:1, 0:1],
                                              None, ALU.mult))
                s_ve.one(vector.tensor_copy(mi32[:, :], mtmp2[:, :]))
                s_ve.one(vector.tensor_scalar(mtmp[:, :], mi32[:, :], QMAX, -QMAX,
                                              ALU.min, ALU.max))
                s_ve.one(vector.tensor_scalar(mtmp2[:, :], mtmp[:, :], sc2[0:1, 0:1],
                                              None, ALU.mult))

            ve3 = s_ve.n

            @blk.gpsimd
            def _(gpsimd):
                gpsimd.wait_ge(s_ve.h, ve3)
                s_gl.dma(gpsimd.dma_start(out=vec_s.ap()[6:7, :], in_=mtmp2[:, :]))
                gpsimd.wait_ge(s_gl.h, s_gl.n)

        with contextlib.ExitStack() as P:
            PE = P.enter_context
            bca = PE(nc.sbuf_tensor("bca", [128, D], F32))
            bcb = PE(nc.sbuf_tensor("bcb", [128, D], F32))
            xt = [PE(nc.sbuf_tensor(f"xt{i}", [128, D], F32)) for i in range(2)]
            lnt = PE(nc.sbuf_tensor("lnt", [128, D], F32))
            h1t = [PE(nc.sbuf_tensor(f"h1t{i}", [128, D], BF16)) for i in range(2)]
            bnb = PE(nc.sbuf_tensor("bnb", [128, 24], F32))
            b6a = PE(nc.sbuf_tensor("b6a", [128, 6], F32))
            b6b = PE(nc.sbuf_tensor("b6b", [128, 6], F32))
            b6c = PE(nc.sbuf_tensor("b6c", [128, 6], F32))
            sq24 = PE(nc.sbuf_tensor("sq24", [128, 24], F32))
            s6a = PE(nc.sbuf_tensor("s6a", [128, 6], F32))
            s6b = PE(nc.sbuf_tensor("s6b", [128, 6], F32))
            s6c = PE(nc.sbuf_tensor("s6c", [128, 6], F32))
            vvs = PE(nc.sbuf_tensor("vvs", [128, 12], F32))
            lnu = PE(nc.sbuf_tensor("lnu", [128, D], F32))
            mvec = PE(nc.sbuf_tensor("mvec", [128, 1], F32))
            vvar = [PE(nc.sbuf_tensor(f"vvar{i}", [128, 1], F32)) for i in range(2)]
            sqv = [PE(nc.sbuf_tensor(f"sqv{i}", [128, 1], F32)) for i in range(2)]
            rstd = PE(nc.sbuf_tensor("rstd", [128, 1], F32))
            htp = [PE(nc.psum_tensor(f"htp{i}", [128, 1024], BF16)) for i in range(2)]
            blk = PE(nc.Block())

            ld1, ve1b, ac1, mm1 = s_ld.n, s_ve.n, s_ac.n, s_mm.n

            @blk.sync
            def _(sync):
                s_ld.dma(sync.dma_start(out=bca[:, :],
                                        in_=bass.AP(vec_s, 1 * D, [[0, 128], [1, D]])))
                s_ld.dma(sync.dma_start(out=bcb[:, :],
                                        in_=bass.AP(vec_s, 0 * D, [[0, 128], [1, D]])))
                for rt in range(8):
                    if rt >= 2:
                        sync.wait_ge(s_ve.h, ve1b + (rt - 1) * 2)
                    s_ld.dma(sync.dma_start(out=xt[rt % 2][:, :],
                                            in_=x_in.ap()[rt * 128:(rt + 1) * 128, :]))

            @blk.vector
            def _(vector):
                for rt in range(8):
                    xx = xt[rt % 2]
                    vector.wait_ge(s_ld.h, ld1 + 32 + (rt + 1) * 16)
                    if rt >= 2:
                        vector.wait_ge(s_mm.h, mm1 + (rt - 1) * 16)
                    for q in range(4):
                        vector.bn_stats(bnb[:, q * 6:(q + 1) * 6],
                                        xx[:, q * 512:(q + 1) * 512])
                    vector.tensor_tensor(b6a[:, :], bnb[:, 0:6], bnb[:, 6:12], ALU.add)
                    vector.tensor_tensor(b6b[:, :], bnb[:, 12:18], bnb[:, 18:24], ALU.add)
                    vector.tensor_tensor(sq24[:, :], bnb[:, :], bnb[:, :], ALU.mult)
                    vector.tensor_tensor(b6c[:, :], b6a[:, :], b6b[:, :], ALU.add)
                    vector.tensor_tensor(s6a[:, :], sq24[:, 0:6], sq24[:, 6:12], ALU.add)
                    vector.tensor_tensor(s6b[:, :], sq24[:, 12:18], sq24[:, 18:24], ALU.add)
                    vector.tensor_tensor(vvs[:, 0:1], b6c[:, 1:2], b6c[:, 4:5], ALU.add)
                    vector.tensor_tensor(s6c[:, :], s6a[:, :], s6b[:, :], ALU.add)
                    vector.tensor_tensor(vvs[:, 1:2], b6c[:, 2:3], b6c[:, 5:6], ALU.add)
                    vector.tensor_scalar(mvec[:, :], vvs[:, 0:1], 1.0 / 8.0, None, ALU.mult)
                    vector.tensor_tensor(vvs[:, 3:4], s6c[:, 1:2], s6c[:, 4:5], ALU.add)
                    vector.tensor_scalar(vvs[:, 2:3], vvs[:, 1:2], 1.0 / D, None, ALU.mult)
                    vector.tensor_scalar(vvs[:, 4:5], vvs[:, 3:4], 1.0 / 8.0, None, ALU.mult)
                    vector.tensor_tensor(vvs[:, 6:7], mvec[:, :], mvec[:, :], ALU.mult)
                    fence(vector, vector.tensor_tensor(vvs[:, 5:6], vvs[:, 2:3],
                                                       vvs[:, 4:5], ALU.add))
                    fence(vector, vector.tensor_tensor(vvs[:, 7:8], vvs[:, 5:6],
                                                       vvs[:, 6:7], ALU.subtract))
                    s_ve.one(vector.tensor_scalar(vvar[rt % 2][:, :], vvs[:, 7:8],
                                                  1.0, EPS, ALU.mult, ALU.add))
                    vector.wait_ge(s_ac.h, ac1 + (rt + 1 if rt <= 1 else 5 * rt - 3))
                    fence(vector, vector.reciprocal(rstd[:, :], sqv[rt % 2][:, :]))
                    vector.tensor_scalar(lnt[:, :], xx[:, :], mvec[:, :], rstd[:, :],
                                         ALU.subtract, ALU.mult)
                    vector.tensor_tensor(lnu[:, :], lnt[:, :], bca[:, :], ALU.mult)
                    s_ve.one(vector.tensor_tensor(h1t[rt % 2][:, :], lnu[:, :],
                                                  bcb[:, :], ALU.add))

            evac_cnt1 = {}

            @blk.scalar
            def _(scalar):
                def evacs(rt):
                    for grp in range(4):
                        scalar.wait_ge(s_mm.h, mm1 + rt * 16 + (grp + 1) * 4)
                        s_ac.one(scalar.copy(
                            bass.AP(h1fm, (4 * grp) * R + rt * 128,
                                    [[16 * R, 128], [R, 4], [1, 128]]),
                            htp[grp % 2][:, 0:512]))
                        evac_cnt1[(rt, grp)] = s_ac.n
                for rt in range(8):
                    scalar.wait_ge(s_ve.h, ve1b + rt * 2 + 1)
                    s_ac.one(scalar.activation(sqv[rt % 2][:, :], vvar[rt % 2][:, :],
                                               AF.Sqrt))
                    if rt >= 1:
                        evacs(rt - 1)
                evacs(7)

            @blk.tensor
            def _(tensor):
                for rt in range(8):
                    tensor.wait_ge(s_ve.h, ve1b + rt * 2 + 2)
                    for dc in range(16):
                        grp = dc // 4
                        gidx = rt * 4 + grp
                        if gidx >= 2 and dc % 4 == 0:
                            prt, pgrp = (gidx - 2) // 4, (gidx - 2) % 4
                            tensor.wait_ge(s_ac.h, evac_cnt1[(prt, pgrp)])
                        s_mm.one(tensor.transpose(
                            htp[grp % 2][:, (dc % 4) * 128:((dc % 4) + 1) * 128],
                            h1t[rt % 2][:, dc * 128:(dc + 1) * 128],
                            ident[:, :]))

            @blk.gpsimd
            def _(gpsimd):
                if "d_vecs" in dump_t:
                    gpsimd.wait_ge(s_ld.h, ld1 + 32)
                    s_gl.dma(gpsimd.dma_start(out=dump_t["d_vecs"].ap()[:, :],
                                              in_=vec_s.ap()[:, :]))
                    if "d_bca" in dump_t:
                        s_gl.dma(gpsimd.dma_start(out=dump_t["d_bca"].ap()[:, 0:D],
                                                  in_=bca[:, :]))
                        s_gl.dma(gpsimd.dma_start(out=dump_t["d_bca"].ap()[:, D:2 * D],
                                                  in_=bcb[:, :]))
                    gpsimd.wait_ge(s_gl.h, s_gl.n)
                if "d_h1" in dump_t:
                    gpsimd.wait_ge(s_ac.h, s_ac.n)
                    s_gl.dma(gpsimd.dma_start(out=dump_t["d_h1"].ap()[:, :],
                                              in_=h1fm[:, :]))
                    gpsimd.wait_ge(s_gl.h, s_gl.n)

        with contextlib.ExitStack() as P:
            PE = P.enter_context
            wful = PE(nc.sbuf_tensor("wful", [128, 16 * D], BF16))
            qfm = PE(nc.sbuf_tensor("qfm", [128, 16 * R], BF16))
            vfm = PE(nc.sbuf_tensor("vfm", [128, 16 * R], BF16))
            bqv = PE(nc.sbuf_tensor("bqv", [128, 64], F32))
            on128 = PE(nc.sbuf_tensor("on128", [128, 128], BF16))
            qnp = [PE(nc.sbuf_tensor(f"qnp{i}", [128, 1], F32)) for i in range(2)]
            qntmp = PE(nc.sbuf_tensor("qntmp", [128, 1], F32))
            qnbf = PE(nc.sbuf_tensor("qnbf", [1, 1], BF16))
            pp = PE(nc.psum_tensor("pp", [128, 1024], F32))
            npp = [PE(nc.psum_tensor(f"npp{i}", [128, 512], F32)) for i in range(2)]
            blk = PE(nc.Block())

            mm2, ve2b, gl2b, ac2 = s_mm.n, s_ve.n, s_gl.n, s_ac.n

            @blk.gpsimd
            def _(gpsimd):
                for j, src in enumerate((qb_in, kb_in, vb_in)):
                    s_gl.dma(gpsimd.dma_start(
                        out=bqv[:, j * 16:(j + 1) * 16],
                        in_=bass.AP(src, 0, [[1, 128], [128, 16]]),
                        allow_slow_non_contiguous=True))
                s_gl.one(gpsimd.memset(on128[:, :], 1.0))
                for j, src in enumerate((qT_in, kT_in, vT_in)):
                    if j == 2:
                        gpsimd.wait_ge(s_ve.h, ve2b + 33)
                        for ft in range(16):
                            s_gl.dma(gpsimd.dma_start(
                                out=bass.AP(xqv_i, ft * 128 * R, [[R, 128], [1, R]]),
                                in_=qfm[:, ft * R:(ft + 1) * R]))
                    for dc in range(16):
                        if j >= 1:
                            gpsimd.wait_ge(s_mm.h,
                                           mm2 + (j - 1) * 512 + 15 * 32 + 16 + dc + 1)
                        s_gl.dma(gpsimd.dma_start(
                            out=wful[:, dc * D:(dc + 1) * D],
                            in_=src.ap()[dc * 128:(dc + 1) * 128, :]))

            @blk.tensor
            def _(tensor):
                for j in range(3):
                    for ft in range(16):
                        idx = j * 16 + ft
                        if j <= 1 and idx >= 1:
                            tensor.wait_ge(s_ve.h, ve2b + idx * 2)
                        elif j == 2 and ft >= 1:
                            tensor.wait_ge(s_ve.h, ve2b + 130 + ft * 2)
                        for rc in range(2):
                            for dc in range(16):
                                if ft == 0 and rc == 0:
                                    tensor.wait_ge(s_gl.h,
                                                   gl2b + 49 + (j * 16 + dc + 1) * 16)
                                s_mm.one(tensor.matmul(
                                    pp[:, rc * 512:(rc + 1) * 512],
                                    wful[:, dc * D + ft * 128: dc * D + (ft + 1) * 128],
                                    h1fm[:, dc * R + rc * 512: dc * R + (rc + 1) * 512],
                                    start=(dc == 0), stop=(dc == 15)))
                    if j == 1:
                        tensor.wait_ge(s_ve.h, ve2b + 64 + 1)
                        for ch in range(32):
                            if ch >= 2:
                                tensor.wait_ge(s_ve.h, ve2b + 65 + (ch - 1) * 2)
                            s_mm.one(tensor.matmul(npp[ch % 2][:, :], on128[:, :],
                                                   vfm[:, ch * 512:(ch + 1) * 512],
                                                   start=True, stop=True))

            @blk.vector
            def _(vector):
                vector.wait_ge(s_gl.h, gl2b + 48)
                vector.tensor_scalar(bqv[:, 48:64], bqv[:, 16:32], SCALE, None,
                                     ALU.mult)

                def mm_of(j, ft):
                    return mm2 + (j * 512 if j <= 1 else 1056) + (ft + 1) * 32

                def evacs(j, dst, bofs, scl):
                    for ft in range(16):
                        vector.wait_ge(s_mm.h, mm_of(j, ft))
                        for rc in range(2):
                            sl = dst[:, ft * R + rc * 512: ft * R + (rc + 1) * 512]
                            if scl is None:
                                inst = vector.tensor_scalar(
                                    sl, pp[:, rc * 512:(rc + 1) * 512],
                                    bqv[:, bofs + ft:bofs + ft + 1], None, ALU.add)
                            else:
                                inst = vector.tensor_scalar(
                                    sl, pp[:, rc * 512:(rc + 1) * 512],
                                    scl, bqv[:, 32 + bofs + ft:33 + bofs + ft],
                                    ALU.mult, ALU.add)
                            s_ve.one(inst)

                evacs(0, qfm, 0, None)
                evacs(1, kfm, 16, SCALE)
                s_ve.one(vector.tensor_tensor(vfm[:, :], qfm[:, :], qfm[:, :], ALU.mult))
                for ch in range(32):
                    vector.wait_ge(s_mm.h, mm2 + 1024 + ch + 1)
                    s_ve.one(vector.tensor_reduce(qntmp[:, :], npp[ch % 2][:, :],
                                                  AX.X, ALU.max))
                    vector.wait_ge(s_ve.h, s_ve.n)
                    if ch == 0:
                        s_ve.one(vector.tensor_copy(qnp[0][:, :], qntmp[:, :]))
                    else:
                        s_ve.one(vector.tensor_tensor(
                            qnp[ch % 2][:, :], qnp[(ch + 1) % 2][:, :],
                            qntmp[:, :], ALU.max))
                        vector.wait_ge(s_ve.h, s_ve.n)
                s_ve.one(vector.tensor_copy(qnbf[:, :], qnp[1][0:1, 0:1]))
                vector.wait_ge(s_mm.h, mm2 + 1056)
                evacs(2, vfm, 32, None)

            @blk.gpsimd
            def _(gpsimd):
                gpsimd.wait_ge(s_ve.h, s_ve.n)
                for ft in range(16):
                    s_gl.dma(gpsimd.dma_start(
                        out=bass.AP(xqv_i, QOFF + ft * 128 * R, [[R, 128], [1, R]]),
                        in_=vfm[:, ft * R:(ft + 1) * R]))
                s_gl.dma(gpsimd.dma_start(
                    out=bass.AP(xqv_i, 2 * QOFF, [[1, 1], [1, 1]]),
                    in_=qnbf[:, :]))
                gpsimd.wait_ge(s_gl.h, s_gl.n)
                s_cc.one(gpsimd.collective_compute(
                    "AllGather", ALU.bypass,
                    replica_groups=[[0, 1, 2, 3], [4, 5, 6, 7]],
                    ins=[xqv_i.ap().opt()], outs=[xqv_o.ap().opt()]))
                s_cc.one(gpsimd.collective_compute(
                    "AllReduce", ALU.add,
                    replica_groups=[[0, 1, 2, 3], [4, 5, 6, 7]],
                    ins=[bass.AP(xqv_i, 0, [[1, 1], [1, 16]]).opt()],
                    outs=[xsy2_o.ap().opt()]))
                if "d_qfm" in dump_t:
                    s_gl.dma(gpsimd.dma_start(out=dump_t["d_qfm"].ap()[:, :], in_=qfm[:, :]))
                if "d_kfm" in dump_t:
                    s_gl.dma(gpsimd.dma_start(out=dump_t["d_kfm"].ap()[:, :], in_=kfm[:, :]))
                if "d_vfm" in dump_t:
                    s_gl.dma(gpsimd.dma_start(out=dump_t["d_vfm"].ap()[:, :], in_=vfm[:, :]))
                gpsimd.wait_ge(s_gl.h, s_gl.n)

        st_h1.close()

        st_afm = contextlib.ExitStack()
        afm = st_afm.enter_context(nc.sbuf_tensor("afm", [128, 16 * R], BF16, side="left"))
        st_att = contextlib.ExitStack()
        AE = st_att.enter_context
        bounds = AE(nc.sbuf_tensor("bounds", [128, 128], F32, side="left"))
        boundsr = AE(nc.sbuf_tensor("boundsr", [128, 128], F32, side="left"))
        qh = [AE(nc.sbuf_tensor(f"qh{i}", [128, T], BF16, side="left")) for i in range(2)]
        vhfm = [AE(nc.sbuf_tensor(f"vhfm{i}", [128, T], BF16, side="left")) for i in range(2)]
        vrm = [AE(nc.sbuf_tensor(f"vrm{i}", [128, 32 * 129], BF16, side="left")) for i in range(2)]

        with contextlib.ExitStack() as P:
            PE = P.enter_context
            sqk = PE(nc.sbuf_tensor("sqk", [128, 16 * R], BF16))
            on128b = PE(nc.sbuf_tensor("on128b", [128, 128], BF16))
            normb = PE(nc.sbuf_tensor("normb", [128, 512], F32))
            qmt = PE(nc.sbuf_tensor("qmt", [1, 4], BF16))
            qmf = PE(nc.sbuf_tensor("qmf", [1, 4], F32))
            qms = PE(nc.sbuf_tensor("qms", [1, 1], F32))
            qms2 = PE(nc.sbuf_tensor("qms2", [1, 1], F32))
            qmaxv = PE(nc.sbuf_tensor("qmaxv", [128, 1], F32))
            nppb = [PE(nc.psum_tensor(f"nppb{i}", [128, 512], F32)) for i in range(2)]
            blk = PE(nc.Block())

            mm3, ve3b, ac3, gl3b, ld3 = s_mm.n, s_ve.n, s_ac.n, s_gl.n, s_ld.n

            @blk.gpsimd
            def _(gpsimd):
                s_gl.one(gpsimd.memset(on128b[:, :], 1.0))
                for h in range(H):
                    for ch in range(2):
                        i2 = h * 2 + ch
                        gpsimd.wait_ge(s_ac.h, ac3 + i2 + 1)
                        s_gl.dma(gpsimd.dma_start(
                            out=nrm_s.ap()[h:h + 1, ch * 512:(ch + 1) * 512],
                            in_=normb[0:1, 0:512]))
                gpsimd.wait_ge(s_cc.h, s_cc.n)
                s_gl.dma(gpsimd.dma_start(
                    out=qmt[:, :],
                    in_=bass.AP(xqv_o, 2 * QOFF, [[1, 1], [XFLAT, 4]]),
                    allow_slow_non_contiguous=True))

            @blk.vector
            def _(vector):
                vector.wait_ge(s_gl.h, gl3b + 1)
                s_ve.one(vector.tensor_tensor(sqk[:, :], kfm[:, :], kfm[:, :], ALU.mult))
                vector.wait_ge(s_gl.h, gl3b + 1 + 32 * 16 + 16)
                s_ve.one(vector.tensor_copy(qmf[:, :], qmt[:, :]))
                vector.wait_ge(s_ve.h, s_ve.n)
                s_ve.one(vector.tensor_reduce(qms[:, :], qmf[:, :], AX.X, ALU.max))

            @blk.tensor
            def _(tensor):
                tensor.wait_ge(s_ve.h, ve3b + 1)
                for h in range(H):
                    for ch in range(2):
                        i2 = h * 2 + ch
                        if i2 >= 2:
                            tensor.wait_ge(s_ac.h, ac3 + i2 - 1)
                        s_mm.one(tensor.matmul(
                            nppb[i2 % 2][:, :], on128b[:, :],
                            sqk[:, h * R + ch * 512: h * R + (ch + 1) * 512],
                            start=True, stop=True))

            @blk.scalar
            def _(scalar):
                for h in range(H):
                    for ch in range(2):
                        i2 = h * 2 + ch
                        scalar.wait_ge(s_mm.h, mm3 + i2 + 1)
                        if i2 >= 1:
                            scalar.wait_ge(s_gl.h, gl3b + 1 + i2 * 16)
                        s_ac.one(scalar.activation(normb[:, :], nppb[i2 % 2][:, :],
                                                   AF.Sqrt))
                scalar.wait_ge(s_ve.h, ve3b + 3)
                s_ac.one(scalar.activation(qms2[:, :], qms[:, :], AF.Sqrt))

            ac3e = s_ac.n

            @blk.gpsimd
            def _(gpsimd):
                gpsimd.wait_ge(s_ac.h, ac3e)
                s_gl.dma(gpsimd.dma_start(
                    out=bass.AP(vec_s, 9 * D, [[1, 1], [1, 1]]), in_=qms2[:, :]))

            gl3e = s_gl.n

            @blk.sync
            def _(sync):
                for h in range(H):
                    sync.wait_ge(s_gl.h, gl3b + 1 + (h + 1) * 32)
                    s_ld.dma(sync.dma_start(
                        out=boundsr[:, h * 8:(h + 1) * 8],
                        in_=bass.AP(nrm_s, h * R, [[1, 128], [128, 8]]),
                        allow_slow_non_contiguous=True))
                sync.wait_ge(s_gl.h, gl3e)
                s_ld.dma(sync.dma_start(out=qmaxv[:, :],
                                        in_=bass.AP(vec_s, 9 * D, [[0, 128], [1, 1]])))

            ld3e = s_ld.n

            @blk.vector
            def _(vector):
                vector.wait_ge(s_ld.h, ld3e)
                s_ve.one(vector.tensor_scalar(bounds[:, :], boundsr[:, :],
                                              qmaxv[:, :], None, ALU.mult))

            @blk.gpsimd
            def _(gpsimd):
                if "d_bounds" in dump_t:
                    gpsimd.wait_ge(s_ve.h, s_ve.n)
                    s_gl.dma(gpsimd.dma_start(out=dump_t["d_bounds"].ap()[:, :],
                                              in_=bounds[:, :]))
                    gpsimd.wait_ge(s_gl.h, s_gl.n)

        with contextlib.ExitStack() as P:
            PE = P.enter_context
            Tt = [PE(nc.sbuf_tensor(f"Tt{i}", [128, 512], F32)) for i in range(4)]
            Pt = [PE(nc.sbuf_tensor(f"Pt{i}", [128, 512], BF16)) for i in range(4)]
            PTs = [PE(nc.sbuf_tensor(f"PTs{i}", [128, 512], BF16)) for i in range(4)]
            pat = PE(nc.sbuf_tensor("pat", [128, 2 * 512], F16))
            bvs = [PE(nc.sbuf_tensor(f"bvs{i}", [128, 8], F32)) for i in range(2)]
            sums = [PE(nc.sbuf_tensor(f"sums{i}", [128, 8], F32)) for i in range(2)]
            sumt = [PE(nc.sbuf_tensor(f"sumt{i}", [128, 1], F32)) for i in range(2)]
            sumr = [PE(nc.sbuf_tensor(f"sumr{i}", [128, 1], F32)) for i in range(2)]
            arm = [PE(nc.sbuf_tensor(f"arm{i}", [128, 128], BF16)) for i in range(2)]
            S_ps = [PE(nc.psum_tensor(f"S_ps{i}", [128, 512], F32)) for i in range(2)]
            PT_ps = [PE(nc.psum_tensor(f"PT_ps{i}", [128, 1024], BF16)) for i in range(2)]
            O_ps = [PE(nc.psum_tensor(f"O_ps{i}", [128, 512], F32)) for i in range(2)]
            vtp = PE(nc.psum_tensor("vtp", [128, 1024], BF16))
            otp = PE(nc.psum_tensor("otp", [128, 1024], BF16))
            blk = PE(nc.Block())

            pld = s_ld.n + 16

            @blk.gpsimd
            def _(gpsimd):
                for i in range(2):
                    s_gl.one(gpsimd.memset(
                        bass.AP(vrm[i], 128, [[32 * 129, 128], [129, 32], [1, 1]]), 1.0))

            gl3bb = s_gl.n

            @blk.sync
            def _(sync):
                s_ld.dma(sync.dma_start(
                    out=pat[:, :],
                    in_=bass.AP(pat_in, 0, [[512, 128], [128 * 512, 2], [1, 512]])))
                for hh in range(H):
                    if hh >= 2:
                        sync.wait_ge(a_s.h, G0[hh - 1])
                        sync.wait_ge(a_av.h, G0[hh - 1])
                    for cp in range(4):
                        a_qv.dma(sync.dma_start(
                            out=bass.AP(qh[hh % 2], cp * 128,
                                        [[T, 128], [512, 8], [1, 128]]),
                            in_=bass.AP(xqv_o, cp * XFLAT + hh * 128 * R,
                                        [[R, 128], [128, 8], [1, 128]])))

            @blk.gpsimd
            def _(gpsimd):
                for hh in range(H):
                    if hh >= 2:
                        gpsimd.wait_ge(a_s.h, G0[hh - 1])
                        gpsimd.wait_ge(a_av.h, G0[hh - 1])
                    for cp in range(4):
                        a_vq.dma(gpsimd.dma_start(
                            out=bass.AP(vhfm[hh % 2], cp * 128,
                                        [[T, 128], [512, 8], [1, 128]]),
                            in_=bass.AP(xqv_o, cp * XFLAT + QOFF + hh * 128 * R,
                                        [[R, 128], [128, 8], [1, 128]])))

            vprep_T = {}
            vprep_V = {}
            for h in range(1, H):
                for grp in range(4):
                    vprep_T.setdefault(G0[h - 1] + 8 + 2 * grp, []).append((h, grp))
                    vprep_V.setdefault(G0[h - 1] + 9 + 2 * grp, []).append((h, grp))

            def v_transposes(tensor, h, grp):
                vt_idx = h * 4 + grp
                tensor.wait_ge(a_qv.h, 64 * (h + 1))
                tensor.wait_ge(a_vq.h, 64 * (h + 1))
                if vt_idx >= 1:
                    tensor.wait_ge(a_ve.h, vt_idx)
                for s8 in range(8):
                    jt = grp * 8 + s8
                    inst = tensor.transpose(
                        vtp[:, s8 * 128:(s8 + 1) * 128],
                        vhfm[h % 2][:, jt * 128:(jt + 1) * 128],
                        ident[:, :])
                inst.then_inc(a_vt.h, 1)
                a_vt.n += 1

            def v_evac(vector, h, grp):
                vt_idx = h * 4 + grp
                vector.wait_ge(a_vt.h, vt_idx + 1)
                if h >= 2:
                    vector.wait_ge(a_av.h, G0[h - 1])
                vector.tensor_copy(
                    bass.AP(vrm[h % 2], grp * 8 * 129, [[32 * 129, 128], [129, 8], [1, 128]]),
                    vtp[:, :]).then_inc(a_ve.h, 1)
                a_ve.n += 1

            @blk.tensor
            def _(tensor):
                tensor.wait_ge(s_ld.h, pld)
                for grp in range(4):
                    v_transposes(tensor, 0, grp)

                def emit_S(g):
                    (gg, h, k, c, last) = CHUNKS[g]
                    if g >= 2:
                        tensor.wait_ge(a_t.h, g - 1)
                    tensor.matmul(S_ps[g % 2][:, :],
                                  kfm[:, h * R + k * 128: h * R + (k + 1) * 128],
                                  qh[h % 2][:, c * 512:(c + 1) * 512],
                                  start=True, stop=True).then_inc(a_s.h, 1)
                    a_s.n += 1

                def emit_TR(g):
                    tensor.wait_ge(a_p.h, g + 1)
                    if g >= 2:
                        tensor.wait_ge(a_e.h, g - 1)
                    for t4 in range(4):
                        inst = tensor.transpose(
                            PT_ps[g % 2][:, t4 * 128:(t4 + 1) * 128],
                            Pt[g % 4][:, t4 * 128:(t4 + 1) * 128],
                            ident[:, :])
                    inst.then_inc(a_tr.h, 1)
                    a_tr.n += 1

                def emit_AV(g):
                    (gg, h, k, c, last) = CHUNKS[g]
                    m = M_OF_G[g]
                    first = c == max(0, k + 1 - WIN)
                    tensor.wait_ge(a_e.h, g + 1)
                    if first and m >= 2:
                        tensor.wait_ge(a_oe.h, m - 1)
                    for t4 in range(4):
                        jt = c * 4 + t4
                        inst = tensor.matmul(
                            O_ps[m % 2][:, 0:129],
                            PTs[g % 4][:, t4 * 128:(t4 + 1) * 128],
                            vrm[h % 2][:, jt * 129:jt * 129 + 129],
                            start=(first and t4 == 0), stop=(last and t4 == 3))
                    inst.then_inc(a_av.h, 1)
                    a_av.n += 1

                def emit_OT(m):
                    tensor.wait_ge(a_oe.h, m + 1)
                    if m >= 1:
                        tensor.wait_ge(a_of.h, m)
                    tensor.transpose(otp[:, 0:128], arm[m % 2][:, :],
                                     ident[:, :]).then_inc(a_ot.h, 1)
                    a_ot.n += 1

                tensor.wait_ge(s_gl.h, gl3bb)
                for g in range(G_TOT + 4):
                    if g < G_TOT:
                        emit_S(g)
                    for (h, grp) in vprep_T.get(g, ()):
                        v_transposes(tensor, h, grp)
                    if 1 <= g <= G_TOT:
                        emit_TR(g - 1)
                    if 2 <= g <= G_TOT + 1:
                        emit_AV(g - 2)
                    if 4 <= g <= G_TOT + 3 and CHUNKS[g - 4][4]:
                        emit_OT(M_OF_G[g - 4])

            @blk.vector
            def _(vector):
                vector.wait_ge(s_ld.h, pld)
                for grp in range(4):
                    v_evac(vector, 0, grp)

                def emit_stt(g):
                    (gg, h, k, c, last) = CHUNKS[g]
                    m = M_OF_G[g]
                    if c == max(0, k + 1 - WIN):
                        if m >= 2:
                            vector.wait_ge(a_p.h, ITILES[m - 2][4] + 1)
                        vector.tensor_scalar(bvs[m % 2][:, :], t512[:, :],
                                             bounds[:, h * 8 + k: h * 8 + k + 1],
                                             i0bc[:, k:k + 1],
                                             ALU.subtract, ALU.subtract)
                    vector.wait_ge(a_s.h, g + 1)
                    if g >= 4:
                        vector.wait_ge(a_p.h, g - 3)
                    kp = 1 if last else 0
                    vector.tensor_tensor(Tt[g % 4][:, :], S_ps[g % 2][:, :],
                                         pat[:, kp * 512:(kp + 1) * 512],
                                         ALU.add).then_inc(a_t.h, 1)
                    a_t.n += 1

                def emit_evac_v(g):
                    vector.wait_ge(a_tr.h, g + 1)
                    if g >= 4:
                        vector.wait_ge(a_av.h, g - 3)
                    vector.tensor_copy(PTs[g % 4][:, :],
                                       PT_ps[g % 2][:, 0:512]).then_inc(a_e.h, 1)
                    a_e.n += 1

                def emit_finish1(m):
                    (mm_, h, k, nch, g_last) = ITILES[m]
                    vector.wait_ge(a_av.h, g_last + 1)
                    inst = vector.reciprocal(sumt[m % 2][:, :], O_ps[m % 2][:, 128:129])
                    if g_last >= G_TOT - 3:
                        fence(vector, inst)

                def emit_finish2(m):
                    (mm_, h, k, nch, g_last) = ITILES[m]
                    if m >= 2:
                        vector.wait_ge(a_ot.h, m - 1)
                    vector.tensor_scalar(arm[m % 2][:, :], O_ps[m % 2][:, 0:128],
                                         sumt[m % 2][:, :], None,
                                         ALU.mult).then_inc(a_oe.h, 1)
                    a_oe.n += 1

                for g in range(G_TOT + 2):
                    if g < G_TOT:
                        emit_stt(g)
                    for (h, grp) in vprep_V.get(g, ()):
                        v_evac(vector, h, grp)
                    if 1 <= g <= G_TOT and (g - 1) % 2 == 0:
                        emit_evac_v(g - 1)
                    if 1 <= g <= G_TOT and CHUNKS[g - 1][4]:
                        emit_finish1(M_OF_G[g - 1])
                    if 2 <= g <= G_TOT + 1 and CHUNKS[g - 2][4]:
                        emit_finish2(M_OF_G[g - 2])

            @blk.scalar
            def _(scalar):
                scalar.wait_ge(s_ld.h, pld)

                def emit_exp(g):
                    (gg, h, k, c, last) = CHUNKS[g]
                    m = M_OF_G[g]
                    scalar.wait_ge(a_t.h, g + 1)
                    if g >= 4:
                        scalar.wait_ge(a_tr.h, g - 3)
                    scalar.activation(Pt[g % 4][:, :], Tt[g % 4][:, :], AF.Exp,
                                      bias=bvs[m % 2][:, c:c + 1],
                                      scale=1.0).then_inc(a_p.h, 1)
                    a_p.n += 1

                def emit_evac_a(g):
                    scalar.wait_ge(a_tr.h, g + 1)
                    if g >= 4:
                        scalar.wait_ge(a_av.h, g - 3)
                    scalar.copy(PTs[g % 4][:, :],
                                PT_ps[g % 2][:, 0:512]).then_inc(a_e.h, 1)
                    a_e.n += 1

                def emit_afm(m):
                    (mm_, h, k, nch, g_last) = ITILES[m]
                    scalar.wait_ge(a_ot.h, m + 1)
                    scalar.copy(afm[:, h * R + k * 128: h * R + (k + 1) * 128],
                                otp[:, 0:128]).then_inc(a_of.h, 1)
                    a_of.n += 1

                for g in range(G_TOT + 1):
                    if g < G_TOT:
                        emit_exp(g)
                    if 1 <= g <= G_TOT and (g - 1) % 2 == 1:
                        emit_evac_a(g - 1)
                    if g >= 3 and CHUNKS[g - 3][4]:
                        emit_afm(M_OF_G[g - 3])
                for gx in (G_TOT - 2, G_TOT - 1):
                    if CHUNKS[gx][4]:
                        emit_afm(M_OF_G[gx])

            @blk.gpsimd
            def _(gpsimd):
                if "d_afm" in dump_t:
                    gpsimd.wait_ge(a_of.h, M_TOT)
                    s_gl.dma(gpsimd.dma_start(out=dump_t["d_afm"].ap()[:, :],
                                              in_=afm[:, :]))
                    gpsimd.wait_ge(s_gl.h, s_gl.n)

        st_att.close()
        st_kfm.close()

        with contextlib.ExitStack() as P:
            PE = P.enter_context
            x2 = PE(nc.sbuf_tensor("x2", [128, 8 * D], F32))
            ow = PE(nc.sbuf_tensor("ow", [128, 16 * D], BF16))
            bga = PE(nc.sbuf_tensor("bga", [128, D], F32))
            bob = PE(nc.sbuf_tensor("bob", [128, D], F32))
            t1 = PE(nc.sbuf_tensor("t1", [128, 512], F32))
            t2 = PE(nc.sbuf_tensor("t2", [128, 512], F32))
            po = PE(nc.psum_tensor("po", [128, 8 * 512], F32))
            blk = PE(nc.Block())

            mm4, ve4, gl4, ld4 = s_mm.n, s_ve.n, s_gl.n, s_ld.n

            @blk.gpsimd
            def _(gpsimd):
                for dc in range(16):
                    s_gl.dma(gpsimd.dma_start(
                        out=ow[:, dc * D:(dc + 1) * D],
                        in_=oT_in.ap()[dc * 128:(dc + 1) * 128, :]))

            @blk.sync
            def _(sync):
                s_ld.dma(sync.dma_start(out=bga[:, :],
                                        in_=bass.AP(vec_s, 2 * D, [[0, 128], [1, D]])))
                s_ld.dma(sync.dma_start(out=bob[:, :],
                                        in_=bass.AP(ob_in, 0, [[0, 128], [1, D]])))
                for rt in range(8):
                    s_ld.dma(sync.dma_start(
                        out=x2[:, rt * D:(rt + 1) * D],
                        in_=x_in.ap()[rt * 128:(rt + 1) * 128, :]))

            @blk.tensor
            def _(tensor):
                for fg in range(4):
                    for dc in range(16):
                        if fg == 0:
                            tensor.wait_ge(s_gl.h, gl4 + (dc + 1) * 16)
                        for rt in range(8):
                            if fg >= 1 and dc == 0:
                                tensor.wait_ge(s_ve.h, ve4 + (fg - 1) * 8 + rt + 1)
                            s_mm.one(tensor.matmul(
                                po[:, rt * 512:(rt + 1) * 512],
                                afm[:, dc * R + rt * 128: dc * R + (rt + 1) * 128],
                                ow[:, dc * D + fg * 512: dc * D + (fg + 1) * 512],
                                start=(dc == 0), stop=(dc == 15)))

            @blk.vector
            def _(vector):
                vector.wait_ge(s_ld.h, ld4 + 16 * 10)
                for fg in range(4):
                    for rt in range(8):
                        vector.wait_ge(s_mm.h, mm4 + fg * 128 + 15 * 8 + rt + 1)
                        vector.tensor_tensor(t1[:, :], po[:, rt * 512:(rt + 1) * 512],
                                             bob[:, fg * 512:(fg + 1) * 512], ALU.add)
                        vector.tensor_tensor(t2[:, :], t1[:, :],
                                             bga[:, fg * 512:(fg + 1) * 512], ALU.mult)
                        xsl = x2[:, rt * D + fg * 512: rt * D + (fg + 1) * 512]
                        s_ve.one(vector.tensor_tensor(xsl, xsl, t2[:, :], ALU.add))

            ve4e = s_ve.n

            @blk.gpsimd
            def _(gpsimd):
                gpsimd.wait_ge(s_ve.h, ve4e)
                for rt in range(8):
                    s_gl.dma(gpsimd.dma_start(out=x2_d.ap()[rt * 128:(rt + 1) * 128, :],
                                              in_=x2[:, rt * D:(rt + 1) * D]))
                gpsimd.wait_ge(s_gl.h, s_gl.n)
                if "d_x2" in dump_t:
                    s_gl.dma(gpsimd.dma_start(out=dump_t["d_x2"].ap()[:, :], in_=x2[:, :]))
                    gpsimd.wait_ge(s_gl.h, s_gl.n)

        st_afm.close()

        st_qx = contextlib.ExitStack()
        qxfm = st_qx.enter_context(nc.sbuf_tensor("qxfm", [128, 16 * R], BF16, side="left"))

        with contextlib.ExitStack() as P:
            PE = P.enter_context
            bcb1 = PE(nc.sbuf_tensor("bcb1", [128, D], F32))
            bsb = PE(nc.sbuf_tensor("bsb", [128, D], F32))
            lnt2 = PE(nc.sbuf_tensor("lnt2", [128, D], F32))
            h2t = PE(nc.sbuf_tensor("h2t", [128, D], F32))
            q_rm = [PE(nc.sbuf_tensor(f"q_rm{i}", [128, D], BF16)) for i in range(2)]
            qi = PE(nc.sbuf_tensor("qi", [128, D], I32))
            bnb2 = PE(nc.sbuf_tensor("bnb2", [128, 24], F32))
            c6a = PE(nc.sbuf_tensor("c6a", [128, 6], F32))
            c6b = PE(nc.sbuf_tensor("c6b", [128, 6], F32))
            c6c = PE(nc.sbuf_tensor("c6c", [128, 6], F32))
            sq24b = PE(nc.sbuf_tensor("sq24b", [128, 24], F32))
            d6a = PE(nc.sbuf_tensor("d6a", [128, 6], F32))
            d6b = PE(nc.sbuf_tensor("d6b", [128, 6], F32))
            d6c = PE(nc.sbuf_tensor("d6c", [128, 6], F32))
            vvs2 = PE(nc.sbuf_tensor("vvs2", [128, 12], F32))
            lnu2 = PE(nc.sbuf_tensor("lnu2", [128, D], F32))
            amx2 = PE(nc.sbuf_tensor("amx2", [128, 1], F32))
            mv2 = PE(nc.sbuf_tensor("mv2", [128, 1], F32))
            vva = [PE(nc.sbuf_tensor(f"vva{i}", [128, 1], F32)) for i in range(2)]
            sqa = [PE(nc.sbuf_tensor(f"sqa{i}", [128, 1], F32)) for i in range(2)]
            rst2 = PE(nc.sbuf_tensor("rst2", [128, 1], F32))
            amx = PE(nc.sbuf_tensor("amx", [128, 1], F32))
            r1v = PE(nc.sbuf_tensor("r1v", [128, 1], F32))
            htq = [PE(nc.psum_tensor(f"htq{i}", [128, 1024], BF16)) for i in range(2)]
            x2t = [PE(nc.sbuf_tensor(f"x2t{i}", [128, D], F32)) for i in range(2)]
            blk = PE(nc.Block())

            ve5, ac5, mm5, ld5 = s_ve.n, s_ac.n, s_mm.n, s_ld.n

            @blk.sync
            def _(sync):
                s_ld.dma(sync.dma_start(out=bcb1[:, :],
                                        in_=bass.AP(vec_s, 4 * D, [[0, 128], [1, D]])))
                s_ld.dma(sync.dma_start(out=bsb[:, :],
                                        in_=bass.AP(vec_s, 3 * D, [[0, 128], [1, D]])))
                for rt in range(8):
                    if rt >= 2:
                        sync.wait_ge(s_ve.h, ve5 + (rt - 1) * 2)
                    s_ld.dma(sync.dma_start(out=x2t[rt % 2][:, :],
                                            in_=x2_d.ap()[rt * 128:(rt + 1) * 128, :]))

            @blk.vector
            def _(vector):
                for rt in range(8):
                    xx = x2t[rt % 2][:, :]
                    vector.wait_ge(s_ld.h, ld5 + 32 + (rt + 1) * 16)
                    if rt >= 2:
                        vector.wait_ge(s_mm.h, mm5 + (rt - 1) * 16)
                    for q in range(4):
                        vector.bn_stats(bnb2[:, q * 6:(q + 1) * 6],
                                        xx[:, q * 512:(q + 1) * 512])
                    vector.tensor_tensor(c6a[:, :], bnb2[:, 0:6], bnb2[:, 6:12], ALU.add)
                    vector.tensor_tensor(c6b[:, :], bnb2[:, 12:18], bnb2[:, 18:24], ALU.add)
                    vector.tensor_tensor(sq24b[:, :], bnb2[:, :], bnb2[:, :], ALU.mult)
                    vector.tensor_tensor(c6c[:, :], c6a[:, :], c6b[:, :], ALU.add)
                    vector.tensor_tensor(d6a[:, :], sq24b[:, 0:6], sq24b[:, 6:12], ALU.add)
                    vector.tensor_tensor(d6b[:, :], sq24b[:, 12:18], sq24b[:, 18:24], ALU.add)
                    vector.tensor_tensor(vvs2[:, 0:1], c6c[:, 1:2], c6c[:, 4:5], ALU.add)
                    vector.tensor_tensor(d6c[:, :], d6a[:, :], d6b[:, :], ALU.add)
                    vector.tensor_tensor(vvs2[:, 1:2], c6c[:, 2:3], c6c[:, 5:6], ALU.add)
                    vector.tensor_scalar(mv2[:, :], vvs2[:, 0:1], 1.0 / 8.0, None, ALU.mult)
                    vector.tensor_tensor(vvs2[:, 3:4], d6c[:, 1:2], d6c[:, 4:5], ALU.add)
                    vector.tensor_scalar(vvs2[:, 2:3], vvs2[:, 1:2], 1.0 / D, None, ALU.mult)
                    vector.tensor_scalar(vvs2[:, 4:5], vvs2[:, 3:4], 1.0 / 8.0, None, ALU.mult)
                    vector.tensor_tensor(vvs2[:, 6:7], mv2[:, :], mv2[:, :], ALU.mult)
                    fence(vector, vector.tensor_tensor(vvs2[:, 5:6], vvs2[:, 2:3],
                                                       vvs2[:, 4:5], ALU.add))
                    fence(vector, vector.tensor_tensor(vvs2[:, 7:8], vvs2[:, 5:6],
                                                       vvs2[:, 6:7], ALU.subtract))
                    s_ve.one(vector.tensor_scalar(vva[rt % 2][:, :], vvs2[:, 7:8],
                                                  1.0, EPS, ALU.mult, ALU.add))
                    vector.wait_ge(s_ac.h, ac5 + (rt + 1 if rt <= 1 else 5 * rt - 3))
                    fence(vector, vector.reciprocal(rst2[:, :], sqa[rt % 2][:, :]))
                    vector.tensor_scalar(lnt2[:, :], xx[:, :], mv2[:, :], rst2[:, :],
                                         ALU.subtract, ALU.mult)
                    vector.tensor_tensor(lnu2[:, :], lnt2[:, :], bcb1[:, :], ALU.mult)
                    vector.tensor_tensor(h2t[:, :], lnu2[:, :], bsb[:, :], ALU.add)
                    fence(vector, vector.tensor_reduce(amx[:, :], h2t[:, :], AX.X,
                                                       ALU.max,
                                                       apply_absolute_value=True))
                    fence(vector, vector.tensor_scalar(amx2[:, :], amx[:, :], TINY,
                                                       None, ALU.max))
                    fence(vector, vector.reciprocal(r1v[:, :], amx2[:, :]))
                    vector.tensor_scalar(sxv[:, rt:rt + 1], amx2[:, :], 1.0 / QMAX,
                                         None, ALU.mult)
                    vector.tensor_scalar(lnu2[:, :], h2t[:, :], r1v[:, :], QMAX,
                                         ALU.mult, ALU.mult)
                    vector.tensor_copy(qi[:, :], lnu2[:, :])
                    s_ve.one(vector.tensor_scalar(q_rm[rt % 2][:, :], qi[:, :],
                                                  QMAX, -QMAX, ALU.min, ALU.max))

            evac_cnt5 = {}

            @blk.scalar
            def _(scalar):
                def evacs(rt):
                    for grp in range(4):
                        scalar.wait_ge(s_mm.h, mm5 + rt * 16 + (grp + 1) * 4)
                        s_ac.one(scalar.copy(
                            bass.AP(qxfm, (4 * grp) * R + rt * 128,
                                    [[16 * R, 128], [R, 4], [1, 128]]),
                            htq[grp % 2][:, 0:512]))
                        evac_cnt5[(rt, grp)] = s_ac.n
                for rt in range(8):
                    scalar.wait_ge(s_ve.h, ve5 + rt * 2 + 1)
                    s_ac.one(scalar.activation(sqa[rt % 2][:, :], vva[rt % 2][:, :],
                                               AF.Sqrt))
                    if rt >= 1:
                        evacs(rt - 1)
                evacs(7)

            @blk.tensor
            def _(tensor):
                for rt in range(8):
                    tensor.wait_ge(s_ve.h, ve5 + rt * 2 + 2)
                    for dc in range(16):
                        grp = dc // 4
                        gidx = rt * 4 + grp
                        if gidx >= 2 and dc % 4 == 0:
                            prt, pgrp = (gidx - 2) // 4, (gidx - 2) % 4
                            tensor.wait_ge(s_ac.h, evac_cnt5[(prt, pgrp)])
                        s_mm.one(tensor.transpose(
                            htq[grp % 2][:, (dc % 4) * 128:((dc % 4) + 1) * 128],
                            q_rm[rt % 2][:, dc * 128:(dc + 1) * 128],
                            ident[:, :]))

            @blk.gpsimd
            def _(gpsimd):
                if "d_qx" in dump_t:
                    gpsimd.wait_ge(s_ac.h, s_ac.n)
                    s_gl.dma(gpsimd.dma_start(out=dump_t["d_qx"].ap()[:, :], in_=qxfm[:, :]))
                    gpsimd.wait_ge(s_gl.h, s_gl.n)

        with contextlib.ExitStack() as P:
            PE = P.enter_context
            akv = PE(nc.sbuf_tensor("akv", [128, 16], F32))
            rkv = PE(nc.sbuf_tensor("rkv", [128, 16], F32))
            skv = PE(nc.sbuf_tensor("skv", [128, 16], F32))
            brk = PE(nc.sbuf_tensor("brk", [128, D], F32))
            qk = PE(nc.sbuf_tensor("qk", [128, 16 * D], BF16))
            ki32 = PE(nc.sbuf_tensor("ki32", [128, D], I32))
            p6 = PE(nc.psum_tensor("p6", [128, 8 * 512], F32))
            P6i = PE(contextlib.ExitStack())
            kt = [P6i.enter_context(nc.sbuf_tensor(f"kt{i}", [128, D], F32))
                  for i in range(2)]
            blk = P6i.enter_context(nc.Block())

            ve6, gl6, ld6, mm6, st6 = s_ve.n, s_gl.n, s_ld.n, s_mm.n, s_st.n

            @blk.sync
            def _(sync):
                for ft in range(16):
                    if ft >= 2:
                        sync.wait_ge(s_ve.h, ve6 + ft - 1)
                    s_ld.dma(sync.dma_start(out=kt[ft % 2][:, :],
                                            in_=fkT_in.ap()[ft * 128:(ft + 1) * 128, :]))

            @blk.vector
            def _(vector):
                for ft in range(16):
                    vector.wait_ge(s_ld.h, ld6 + (ft + 1) * 16)
                    s_ve.one(vector.tensor_reduce(
                        akv[:, ft:ft + 1], kt[ft % 2][:, :], AX.X, ALU.max,
                        apply_absolute_value=True))
                    if ft == 15:
                        vector.wait_ge(s_ve.h, s_ve.n)
                s_ve.one(vector.tensor_scalar(skv[:, :], akv[:, :], TINY, 1.0 / QMAX,
                                              ALU.max, ALU.mult))
                vector.wait_ge(s_ve.h, s_ve.n)
                s_ve.one(vector.reciprocal(rkv[:, :], skv[:, :]))

            @blk.gpsimd
            def _(gpsimd):
                gpsimd.wait_ge(s_ve.h, ve6 + 18)
                s_gl.dma(gpsimd.dma_start(
                    out=bass.AP(vec_s, 7 * D, [[1, 128], [128, 16]]), in_=skv[:, :],
                    allow_slow_non_contiguous=True))
                s_gl.dma(gpsimd.dma_start(
                    out=bass.AP(vec_s, 8 * D, [[1, 128], [128, 16]]), in_=rkv[:, :],
                    allow_slow_non_contiguous=True))

            @blk.sync
            def _(sync):
                sync.wait_ge(s_gl.h, gl6 + 32)
                s_ld.dma(sync.dma_start(out=brk[:, :],
                                        in_=bass.AP(vec_s, 8 * D, [[0, 128], [1, D]])))
                for dc in range(16):
                    if dc >= 2:
                        sync.wait_ge(s_ve.h, ve6 + 18 + dc - 1)
                    s_ld.dma(sync.dma_start(out=kt[dc % 2][:, :],
                                            in_=fk_in.ap()[dc * 128:(dc + 1) * 128, :]))

            @blk.vector
            def _(vector):
                for dc in range(16):
                    vector.wait_ge(s_ld.h, ld6 + 16 * 16 + 16 + (dc + 1) * 16)
                    vector.tensor_tensor(ki32[:, :], kt[dc % 2][:, :], brk[:, :], ALU.mult)
                    s_ve.one(vector.tensor_scalar(qk[:, dc * D:(dc + 1) * D], ki32[:, :],
                                                  QMAX, -QMAX, ALU.min, ALU.max))

            P6i.close()
            bsk = PE(nc.sbuf_tensor("bsk", [128, D], F32))
            bbq = PE(nc.sbuf_tensor("bbq", [128, D], F32))
            bgb = PE(nc.sbuf_tensor("bgb", [128, D], F32))
            t3 = PE(nc.sbuf_tensor("t3", [128, D], F32))
            t4 = PE(nc.sbuf_tensor("t4", [128, D], F32))
            t5 = PE(nc.sbuf_tensor("t5", [128, D], F32))
            ay2 = PE(nc.sbuf_tensor("ay2", [128, 1], F32))
            x2u = [PE(nc.sbuf_tensor(f"x2u{i}", [128, D], F32)) for i in range(2)]
            outt = [PE(nc.sbuf_tensor(f"outt{i}", [128, D], F32)) for i in range(2)]
            ay = PE(nc.sbuf_tensor("ay", [128, 1], F32))
            syv = PE(nc.sbuf_tensor("syv", [128, 1], F32))
            ryv = PE(nc.sbuf_tensor("ryv", [128, 1], F32))
            blk2 = PE(nc.Block())
            ld6b = s_ld.n

            @blk2.sync
            def _(sync):
                s_ld.dma(sync.dma_start(out=bsk[:, :],
                                        in_=bass.AP(vec_s, 7 * D, [[0, 128], [1, D]])))
                s_ld.dma(sync.dma_start(out=bbq[:, :],
                                        in_=bass.AP(vec_s, 6 * D, [[0, 128], [1, D]])))
                s_ld.dma(sync.dma_start(out=bgb[:, :],
                                        in_=bass.AP(vec_s, 5 * D, [[0, 128], [1, D]])))
                for rt in range(8):
                    if rt >= 2:
                        sync.wait_ge(s_ve.h, ve6 + 34 + rt - 1)
                    s_ld.dma(sync.dma_start(out=x2u[rt % 2][:, :],
                                            in_=x2_d.ap()[rt * 128:(rt + 1) * 128, :]))

            @blk2.tensor
            def _(tensor):
                for rt in range(8):
                    half = rt % 2
                    if rt >= 2:
                        tensor.wait_ge(s_ve.h, ve6 + 34 + (rt - 1))
                    for dc in range(16):
                        for fg in range(4):
                            s_mm.one(tensor.matmul(
                                p6[:, (half * 4 + fg) * 512:(half * 4 + fg + 1) * 512],
                                qxfm[:, dc * R + rt * 128: dc * R + (rt + 1) * 128],
                                qk[:, dc * D + fg * 512: dc * D + (fg + 1) * 512],
                                start=(dc == 0), stop=(dc == 15)))

            @blk2.vector
            def _(vector):
                for rt in range(8):
                    half = rt % 2
                    vector.wait_ge(s_mm.h, mm6 + (rt + 1) * 64)
                    vector.wait_ge(s_ld.h, ld6b + 48 + (rt + 1) * 16)
                    if rt >= 2:
                        vector.wait_ge(s_st.h, st6 + (rt - 1) * 16)
                    psl = p6[:, half * 2048:(half + 1) * 2048]
                    vector.scalar_tensor_tensor(t3[:, :], psl, sxv[:, rt:rt + 1],
                                                bsk[:, :], ALU.mult, ALU.mult)
                    vector.tensor_tensor(t4[:, :], t3[:, :], bbq[:, :], ALU.add)
                    vector.tensor_scalar(t5[:, :], t4[:, :], 0.0, None, ALU.max)
                    fence(vector, vector.tensor_reduce(ay[:, :], t5[:, :], AX.X,
                                                       ALU.max,
                                                       apply_absolute_value=True))
                    fence(vector, vector.tensor_scalar(ay2[:, :], ay[:, :], TINY,
                                                       None, ALU.max))
                    fence(vector, vector.reciprocal(ryv[:, :], ay2[:, :]))
                    vector.tensor_scalar(syv[:, :], ay2[:, :], 1.0 / QMAX, None, ALU.mult)
                    vector.tensor_scalar(t3[:, :], t5[:, :], ryv[:, :], QMAX,
                                         ALU.mult, ALU.mult)
                    vector.tensor_copy(ki32[:, :], t3[:, :])
                    vector.tensor_scalar(t4[:, :], ki32[:, :], QMAX, -QMAX,
                                         ALU.min, ALU.max)
                    vector.scalar_tensor_tensor(t3[:, :], t4[:, :], syv[:, :],
                                                bgb[:, :], ALU.mult, ALU.mult)
                    s_ve.one(vector.tensor_tensor(outt[rt % 2][:, :],
                                                  x2u[rt % 2][:, :],
                                                  t3[:, :], ALU.add))

            @blk2.gpsimd
            def _(gpsimd):
                for rt in range(8):
                    gpsimd.wait_ge(s_ve.h, ve6 + 34 + rt + 1)
                    s_st.dma(gpsimd.dma_start(out=out_t.ap()[rt * 128:(rt + 1) * 128, :],
                                              in_=outt[rt % 2][:, :]))
                gpsimd.wait_ge(s_st.h, s_st.n)

        st_qx.close()

    return nc


def _host_consts():
    ident = np.eye(128, dtype=ml_dtypes.bfloat16)
    jj = np.arange(512, dtype=np.float32)[None, :]
    ip = np.arange(128, dtype=np.float32)[:, None]
    pat_off = (jj - ip).astype(np.float32)
    t512 = np.broadcast_to(512.0 * np.arange(8, dtype=np.float32)[None, :],
                           (128, 8)).copy()
    return ident, pat_off, t512


def _run(dump, L0_attn_x, L0_attn_q_w, L0_attn_q_b, L0_attn_k_w, L0_attn_k_b,
         L0_attn_v_w, L0_attn_v_b, L0_attn_o_w, L0_attn_o_b,
         L0_ffn_kernel, L0_ffn_bias, L0_c, L0_wmod):
    ident, pat_off, t512 = _host_consts()
    x = np.ascontiguousarray(np.asarray(L0_attn_x, dtype=np.float32).reshape(B * T, D))
    qT = np.ascontiguousarray(np.asarray(L0_attn_q_w, dtype=np.float32).T)
    kT = np.ascontiguousarray(np.asarray(L0_attn_k_w, dtype=np.float32).T)
    vT = np.ascontiguousarray(np.asarray(L0_attn_v_w, dtype=np.float32).T)
    oT = np.ascontiguousarray(np.asarray(L0_attn_o_w, dtype=np.float32).T)
    fk = np.ascontiguousarray(np.asarray(L0_ffn_kernel, dtype=np.float32))
    fkT = np.ascontiguousarray(fk.T)
    qb = np.asarray(L0_attn_q_b, dtype=np.float32).reshape(1, D)
    kb = np.asarray(L0_attn_k_b, dtype=np.float32).reshape(1, D)
    vb = np.asarray(L0_attn_v_b, dtype=np.float32).reshape(1, D)
    ob = np.asarray(L0_attn_o_b, dtype=np.float32).reshape(1, D)
    fb = np.asarray(L0_ffn_bias, dtype=np.float32).reshape(1, D)
    cc = np.asarray(L0_c, dtype=np.float32)
    wm = np.ascontiguousarray(np.asarray(L0_wmod, dtype=np.float32))

    nc = build_nc(dump)
    in_maps = []
    for core in range(NC):
        b, cp = core // 4, core % 4
        gtiles = cp + 4 * np.arange(8)
        rows = (b * T + (gtiles[:, None] * 128 + np.arange(128)[None, :])).reshape(-1)
        x_own = np.ascontiguousarray(x[rows])
        jj = np.arange(512, dtype=np.float32)[None, :]
        ip = np.arange(128, dtype=np.float32)[:, None]
        pat_diag = np.where(jj > 128.0 * cp + ip, np.float32(NEG), jj - ip)
        pats = np.stack([pat_off, pat_diag]).astype(np.float16)
        i0 = (512.0 * np.arange(8) + 128.0 * cp).astype(np.float32)
        i0bc = np.broadcast_to(i0[None, :], (128, 8)).copy()
        in_maps.append({
            "x_own": x_own, "qT": qT, "kT": kT, "vT": vT,
            "qb": qb, "kb": kb, "vb": vb, "oT": oT, "ob": ob,
            "fk": fk, "fkT": fkT, "fb": fb,
            "c_own": np.ascontiguousarray(cc[b].reshape(1, D)),
            "wm_own": np.ascontiguousarray(wm[:, cp * 3072:(cp + 1) * 3072]),
            "ident": ident, "pats": pats,
            "t512c": t512, "i0bc": i0bc,
        })
    res = run_bass_kernel_spmd(nc, in_maps, core_ids=list(range(NC)))
    return res


def core_rows(core):
    b, cp = core // 4, core % 4
    gtiles = cp + 4 * np.arange(8)
    return (b * T + (gtiles[:, None] * 128 + np.arange(128)[None, :])).reshape(-1)


def kernel(**inputs):
    res = _run((), **inputs)
    out = np.empty((B * T, D), dtype=np.float32)
    for core in range(NC):
        out[core_rows(core)] = res.results[core]["out"]
    return out.reshape(B, T, D)


# version 07
# speedup vs baseline: 1.3070x; 1.0612x over previous
import contextlib
import numpy as np
import ml_dtypes
import concourse.bass as bass
import concourse.mybir as mybir
from concourse.bass_utils import run_bass_kernel_spmd

F32 = mybir.dt.float32
F16 = mybir.dt.float16
BF16 = mybir.dt.bfloat16
I32 = mybir.dt.int32
AF = mybir.ActivationFunctionType
ALU = mybir.AluOpType
AX = mybir.AxisListType

B, T, D, H, HD = 2, 4096, 2048, 16, 128
R = 1024
NC = 8
SCALE = HD ** -0.5
EPS = 1e-5
NEG = -60000.0
TINY = 1e-30
QMAX = 127.0
QOFF = D * R
XFLAT = 2 * QOFF + 128

WIN = 3
CHUNKS = []
ITILES = []
for _h in range(H):
    for _k in range(8):
        _c0 = max(0, _k + 1 - WIN)
        for _c in range(_c0, _k + 1):
            CHUNKS.append((len(CHUNKS), _h, _k, _c, _c == _k))
        ITILES.append((len(ITILES), _h, _k, _k + 1 - _c0, len(CHUNKS) - 1))
G_TOT = len(CHUNKS)
M_TOT = len(ITILES)
M_OF_G = {}
for (_m, _h, _k, _nch, _gl) in ITILES:
    for _c in range(_nch):
        M_OF_G[_gl - _nch + 1 + _c] = _m
G0 = {h: (G_TOT // H) * h for h in range(H)}


class Sem:

    def __init__(self, handle):
        self.h = handle
        self.n = 0

    def dma(self, inst):
        inst.then_inc(self.h, 16)
        self.n += 16
        return inst

    def one(self, inst):
        inst.then_inc(self.h, 1)
        self.n += 1
        return inst


def build_nc(dump=()):
    nc = bass.Bass("TRN2", target_bir_lowering=False)

    x_in = nc.dram_tensor("x_own", [R, D], F32, kind="ExternalInput")
    qT_in = nc.dram_tensor("qT", [D, D], F32, kind="ExternalInput")
    kT_in = nc.dram_tensor("kT", [D, D], F32, kind="ExternalInput")
    vT_in = nc.dram_tensor("vT", [D, D], F32, kind="ExternalInput")
    qb_in = nc.dram_tensor("qb", [1, D], F32, kind="ExternalInput")
    kb_in = nc.dram_tensor("kb", [1, D], F32, kind="ExternalInput")
    vb_in = nc.dram_tensor("vb", [1, D], F32, kind="ExternalInput")
    oT_in = nc.dram_tensor("oT", [D, D], F32, kind="ExternalInput")
    ob_in = nc.dram_tensor("ob", [1, D], F32, kind="ExternalInput")
    fk_in = nc.dram_tensor("fk", [D, D], F32, kind="ExternalInput")
    fkT_in = nc.dram_tensor("fkT", [D, D], F32, kind="ExternalInput")
    fb_in = nc.dram_tensor("fb", [1, D], F32, kind="ExternalInput")
    c_in = nc.dram_tensor("c_own", [1, D], F32, kind="ExternalInput")
    wm_in = nc.dram_tensor("wm_own", [D, 3072], F32, kind="ExternalInput")
    ident_in = nc.dram_tensor("ident", [128, 128], BF16, kind="ExternalInput")
    pat_in = nc.dram_tensor("pats", [2, 128, 512], F16, kind="ExternalInput")
    t512_in = nc.dram_tensor("t512c", [128, 8], F32, kind="ExternalInput")
    i0bc_in = nc.dram_tensor("i0bc", [128, 8], F32, kind="ExternalInput")
    out_t = nc.dram_tensor("out", [R, D], F32, kind="ExternalOutput")

    xmod_i = nc.dram_tensor("xmod_i", [1, 3072], F32)
    xmod_o = nc.dram_tensor("xmod_o", [4, 3072], F32)
    P0LEN = 128 + QOFF
    PVLEN = QOFF // 2
    xqv_i0 = nc.dram_tensor("xqv_i0", [1, P0LEN], BF16)
    xqv_o0 = nc.dram_tensor("xqv_o0", [4, P0LEN], BF16)
    xqv_i1 = nc.dram_tensor("xqv_i1", [1, PVLEN], BF16)
    xqv_o1 = nc.dram_tensor("xqv_o1", [4, PVLEN], BF16)
    xqv_i2 = nc.dram_tensor("xqv_i2", [1, PVLEN], BF16)
    xqv_o2 = nc.dram_tensor("xqv_o2", [4, PVLEN], BF16)
    xsy_o = nc.dram_tensor("xsy_o", [1, 16], F32)
    xsy2_o = nc.dram_tensor("xsy2_o", [1, 16], BF16)
    xsy3_o = nc.dram_tensor("xsy3_o", [1, 16], BF16)
    xsy4_o = nc.dram_tensor("xsy4_o", [1, 16], BF16)
    CCM = {}
    vec_s = nc.dram_tensor("vec_s", [16, D], F32)
    nrm_s = nc.dram_tensor("nrm_s", [H, R], F32)
    x2_d = nc.dram_tensor("x2_d", [R, D], F32)

    dump_t = {}
    for name, shp, dt in (("d_h1", [128, 16 * R], BF16), ("d_qfm", [128, 16 * R], BF16),
                          ("d_kfm", [128, 16 * R], BF16), ("d_vfm", [128, 16 * R], BF16),
                          ("d_bounds", [128, 128], F32), ("d_afm", [128, 16 * R], BF16),
                          ("d_x2", [128, 8 * D], F32), ("d_qx", [128, 16 * R], BF16),
                          ("d_mod", [4, 3072], F32), ("d_vecs", [16, D], F32),
                          ("d_bca", [128, 2 * D], F32), ("d_xqo", [4, QOFF], BF16)):
        if name in dump:
            dump_t[name] = nc.dram_tensor(name, shp, dt, kind="ExternalOutput")

    ES = contextlib.ExitStack()
    with ES:
        E = ES.enter_context

        vf = Sem(E(nc.semaphore("vf")))

        def fence(eng, inst):
            vf.n += 1
            inst.then_inc(vf.h, 1)
            eng.wait_ge(vf.h, vf.n)
            return inst

        s_ld = Sem(E(nc.semaphore("s_ld")))
        s_gl = Sem(E(nc.semaphore("s_gl")))
        s_cc = Sem(E(nc.semaphore("s_cc")))
        s_mm = Sem(E(nc.semaphore("s_mm")))
        s_ve = Sem(E(nc.semaphore("s_ve")))
        s_ac = Sem(E(nc.semaphore("s_ac")))
        s_st = Sem(E(nc.semaphore("s_st")))
        a_s = Sem(E(nc.semaphore("a_s")))
        a_t = Sem(E(nc.semaphore("a_t")))
        a_p = Sem(E(nc.semaphore("a_p")))
        a_tr = Sem(E(nc.semaphore("a_tr")))
        a_e = Sem(E(nc.semaphore("a_e")))
        a_av = Sem(E(nc.semaphore("a_av")))
        a_oe = Sem(E(nc.semaphore("a_oe")))
        a_ot = Sem(E(nc.semaphore("a_ot")))
        a_of = Sem(E(nc.semaphore("a_of")))
        a_qv = Sem(E(nc.semaphore("a_qv")))
        a_vq = Sem(E(nc.semaphore("a_vq")))
        a_vt = Sem(E(nc.semaphore("a_vt")))
        a_ve = Sem(E(nc.semaphore("a_ve")))

        ident = E(nc.sbuf_tensor("identt", [128, 128], BF16))
        t512 = E(nc.sbuf_tensor("t512t", [128, 8], F32))
        i0bc = E(nc.sbuf_tensor("i0bct", [128, 8], F32))
        sxv = E(nc.sbuf_tensor("sxv", [128, 8], F32))

        st_h1 = contextlib.ExitStack()
        h1fm = st_h1.enter_context(nc.sbuf_tensor("h1fm", [128, 16 * R], BF16, side="left"))
        st_kfm = contextlib.ExitStack()
        kfm = st_kfm.enter_context(nc.sbuf_tensor("kfm", [128, 16 * R], BF16, side="right"))

        with contextlib.ExitStack() as P:
            PE = P.enter_context
            cvec = PE(nc.sbuf_tensor("cvec", [128, 16], F32))
            csil = PE(nc.sbuf_tensor("csil", [128, 16], BF16))
            wmt = [PE(nc.sbuf_tensor(f"wmt{i}", [128, 3072], BF16)) for i in range(2)]
            modv = PE(nc.sbuf_tensor("modv", [1, 3072], F32))
            mps = [PE(nc.psum_tensor(f"mps{j}", [1, 512], F32)) for j in range(6)]
            blk = PE(nc.Block())

            @blk.sync
            def _(sync):
                s_ld.dma(sync.dma_start(out=ident[:, :], in_=ident_in.ap()[:, :]))
                s_ld.dma(sync.dma_start(out=t512[:, :], in_=t512_in.ap()[:, :]))
                s_ld.dma(sync.dma_start(out=i0bc[:, :], in_=i0bc_in.ap()[:, :]))
                s_ld.dma(sync.dma_start(out=cvec[:, :],
                                        in_=bass.AP(c_in, 0, [[1, 128], [128, 16]]), allow_slow_non_contiguous=True))

            ld0, mm0, gl0 = s_ld.n, s_mm.n, s_gl.n

            @blk.gpsimd
            def _(gpsimd):
                for dc in range(16):
                    if dc >= 2:
                        gpsimd.wait_ge(s_mm.h, mm0 + (dc - 1) * 6)
                    s_gl.dma(gpsimd.dma_start(
                        out=wmt[dc % 2][:, :],
                        in_=wm_in.ap()[dc * 128:(dc + 1) * 128, :]))

            @blk.scalar
            def _(scalar):
                scalar.wait_ge(s_ld.h, ld0)
                s_ac.one(scalar.activation(csil[:, :], cvec[:, :], AF.Silu))

            ac0 = s_ac.n

            @blk.tensor
            def _(tensor):
                tensor.wait_ge(s_ac.h, ac0)
                for dc in range(16):
                    tensor.wait_ge(s_gl.h, gl0 + (dc + 1) * 16)
                    for j in range(6):
                        s_mm.one(tensor.matmul(
                            mps[j][:, :],
                            csil[:, dc:dc + 1],
                            wmt[dc % 2][:, j * 512:(j + 1) * 512],
                            start=(dc == 0), stop=(dc == 15)))

            mm0e = s_mm.n

            @blk.vector
            def _(vector):
                vector.wait_ge(s_mm.h, mm0e)
                for j in range(6):
                    s_ve.one(vector.tensor_copy(modv[:, j * 512:(j + 1) * 512],
                                                mps[j][:, :]))

            ve0 = s_ve.n

            @blk.gpsimd
            def _(gpsimd):
                gpsimd.wait_ge(s_ve.h, ve0)
                s_gl.dma(gpsimd.dma_start(out=xmod_i.ap()[:, :], in_=modv[:, :]))
                gpsimd.wait_ge(s_gl.h, s_gl.n)
                s_cc.one(gpsimd.collective_compute(
                    "AllGather", ALU.bypass,
                    replica_groups=[[0, 1, 2, 3], [4, 5, 6, 7]],
                    ins=[xmod_i.ap().opt()], outs=[xmod_o.ap().opt()]))
                s_cc.one(gpsimd.collective_compute(
                    "AllReduce", ALU.add,
                    replica_groups=[[0, 1, 2, 3], [4, 5, 6, 7]],
                    ins=[bass.AP(xmod_i, 0, [[1, 1], [1, 16]]).opt()],
                    outs=[xsy_o.ap().opt()]))
                gpsimd.wait_ge(s_cc.h, s_cc.n)
                if "d_mod" in dump_t:
                    s_gl.dma(gpsimd.dma_start(out=dump_t["d_mod"].ap()[:, :],
                                              in_=xmod_o.ap()[:, :]))
                    gpsimd.wait_ge(s_gl.h, s_gl.n)

        with contextlib.ExitStack() as P:
            PE = P.enter_context
            vrow = PE(nc.sbuf_tensor("vrow", [1, D], F32))
            vrow2 = PE(nc.sbuf_tensor("vrow2", [1, D], F32))
            vrow3 = PE(nc.sbuf_tensor("vrow3", [1, D], F32))
            msk = PE(nc.sbuf_tensor("msk", [1, D], mybir.dt.uint8))
            mtmp = PE(nc.sbuf_tensor("mtmp", [1, D], F32))
            mtmp2 = PE(nc.sbuf_tensor("mtmp2", [1, D], F32))
            mi32 = PE(nc.sbuf_tensor("mi32", [1, D], I32))
            sc1 = PE(nc.sbuf_tensor("sc1", [1, 1], F32))
            sc2 = PE(nc.sbuf_tensor("sc2", [1, 1], F32))
            blk = PE(nc.Block())

            def mod_sl(i):
                return bass.AP(xmod_o, i * D, [[1, 1], [1, D]])

            @blk.gpsimd
            def _(gpsimd):
                s_gl.dma(gpsimd.dma_start(out=vec_s.ap()[0:1, :], in_=mod_sl(0)))
                s_gl.dma(gpsimd.dma_start(out=vec_s.ap()[3:4, :], in_=mod_sl(3)))
                s_gl.dma(gpsimd.dma_start(out=vrow[:, :], in_=mod_sl(1)))
                s_gl.dma(gpsimd.dma_start(out=vrow2[:, :], in_=mod_sl(4)))
                s_gl.dma(gpsimd.dma_start(out=vrow3[:, :], in_=mod_sl(2)))

            gl1 = s_gl.n

            @blk.vector
            def _(vector):
                vector.wait_ge(s_gl.h, gl1)
                s_ve.one(vector.tensor_scalar_add(mtmp[:, :], vrow[:, :], 1.0))
                s_ve.one(vector.tensor_copy(vrow[:, :], mtmp[:, :]))
                s_ve.one(vector.tensor_scalar_add(mtmp[:, :], vrow2[:, :], 1.0))
                s_ve.one(vector.tensor_copy(vrow2[:, :], mtmp[:, :]))
                s_ve.one(vector.tensor_scalar(msk[:, :], vrow3[:, :], 0.0, None, ALU.is_ge))
                s_ve.one(vector.tensor_scalar_max(mtmp[:, :], vrow3[:, :], 0.25))
                s_ve.one(vector.tensor_scalar_min(mtmp2[:, :], vrow3[:, :], -0.25))
                s_ve.one(vector.select(vrow3[:, :], msk[:, :], mtmp[:, :], mtmp2[:, :]))

            ve1 = s_ve.n

            @blk.gpsimd
            def _(gpsimd):
                gpsimd.wait_ge(s_ve.h, ve1)
                s_gl.dma(gpsimd.dma_start(out=vec_s.ap()[1:2, :], in_=vrow[:, :]))
                s_gl.dma(gpsimd.dma_start(out=vec_s.ap()[4:5, :], in_=vrow2[:, :]))
                s_gl.dma(gpsimd.dma_start(out=vec_s.ap()[2:3, :], in_=vrow3[:, :]))
                s_gl.dma(gpsimd.dma_start(out=vrow3[:, :], in_=mod_sl(5)))

            gl2 = s_gl.n

            @blk.vector
            def _(vector):
                vector.wait_ge(s_gl.h, gl2)
                s_ve.one(vector.tensor_scalar(msk[:, :], vrow3[:, :], 0.0, None, ALU.is_ge))
                s_ve.one(vector.tensor_scalar_max(mtmp[:, :], vrow3[:, :], 0.25))
                s_ve.one(vector.tensor_scalar_min(mtmp2[:, :], vrow3[:, :], -0.25))
                s_ve.one(vector.select(vrow3[:, :], msk[:, :], mtmp[:, :], mtmp2[:, :]))

            ve2 = s_ve.n

            @blk.gpsimd
            def _(gpsimd):
                gpsimd.wait_ge(s_ve.h, ve2)
                s_gl.dma(gpsimd.dma_start(out=vec_s.ap()[5:6, :], in_=vrow3[:, :]))
                s_gl.dma(gpsimd.dma_start(out=vrow[:, :], in_=fb_in.ap()[0:1, :]))

            gl3 = s_gl.n

            @blk.vector
            def _(vector):
                vector.wait_ge(s_gl.h, gl3)
                s_ve.one(vector.tensor_reduce(sc1[:, :], vrow[:, :], AX.X, ALU.max,
                                              apply_absolute_value=True))
                vector.wait_ge(s_ve.h, s_ve.n)
                s_ve.one(vector.tensor_scalar(sc2[:, :], sc1[:, :], TINY, 1.0 / QMAX,
                                              ALU.max, ALU.mult))
                vector.wait_ge(s_ve.h, s_ve.n)
                s_ve.one(vector.reciprocal(mtmp[0:1, 0:1], sc2[:, :]))
                vector.wait_ge(s_ve.h, s_ve.n)
                s_ve.one(vector.tensor_scalar(mtmp2[:, :], vrow[:, :], mtmp[0:1, 0:1],
                                              None, ALU.mult))
                s_ve.one(vector.tensor_copy(mi32[:, :], mtmp2[:, :]))
                s_ve.one(vector.tensor_scalar(mtmp[:, :], mi32[:, :], QMAX, -QMAX,
                                              ALU.min, ALU.max))
                s_ve.one(vector.tensor_scalar(mtmp2[:, :], mtmp[:, :], sc2[0:1, 0:1],
                                              None, ALU.mult))

            ve3 = s_ve.n

            @blk.gpsimd
            def _(gpsimd):
                gpsimd.wait_ge(s_ve.h, ve3)
                s_gl.dma(gpsimd.dma_start(out=vec_s.ap()[6:7, :], in_=mtmp2[:, :]))
                gpsimd.wait_ge(s_gl.h, s_gl.n)

        with contextlib.ExitStack() as P:
            PE = P.enter_context
            bca = PE(nc.sbuf_tensor("bca", [128, D], F32))
            bcb = PE(nc.sbuf_tensor("bcb", [128, D], F32))
            xt = [PE(nc.sbuf_tensor(f"xt{i}", [128, D], F32)) for i in range(2)]
            lnt = PE(nc.sbuf_tensor("lnt", [128, D], F32))
            h1t = [PE(nc.sbuf_tensor(f"h1t{i}", [128, D], BF16)) for i in range(2)]
            bnb = PE(nc.sbuf_tensor("bnb", [128, 24], F32))
            b6a = PE(nc.sbuf_tensor("b6a", [128, 6], F32))
            b6b = PE(nc.sbuf_tensor("b6b", [128, 6], F32))
            b6c = PE(nc.sbuf_tensor("b6c", [128, 6], F32))
            sq24 = PE(nc.sbuf_tensor("sq24", [128, 24], F32))
            s6a = PE(nc.sbuf_tensor("s6a", [128, 6], F32))
            s6b = PE(nc.sbuf_tensor("s6b", [128, 6], F32))
            s6c = PE(nc.sbuf_tensor("s6c", [128, 6], F32))
            vvs = PE(nc.sbuf_tensor("vvs", [128, 12], F32))
            lnu = PE(nc.sbuf_tensor("lnu", [128, D], F32))
            mvec = PE(nc.sbuf_tensor("mvec", [128, 1], F32))
            vvar = [PE(nc.sbuf_tensor(f"vvar{i}", [128, 1], F32)) for i in range(2)]
            sqv = [PE(nc.sbuf_tensor(f"sqv{i}", [128, 1], F32)) for i in range(2)]
            rstd = PE(nc.sbuf_tensor("rstd", [128, 1], F32))
            htp = [PE(nc.psum_tensor(f"htp{i}", [128, 1024], BF16)) for i in range(2)]
            blk = PE(nc.Block())

            ld1, ve1b, ac1, mm1 = s_ld.n, s_ve.n, s_ac.n, s_mm.n

            @blk.sync
            def _(sync):
                s_ld.dma(sync.dma_start(out=bca[:, :],
                                        in_=bass.AP(vec_s, 1 * D, [[0, 128], [1, D]])))
                s_ld.dma(sync.dma_start(out=bcb[:, :],
                                        in_=bass.AP(vec_s, 0 * D, [[0, 128], [1, D]])))
                for rt in range(8):
                    if rt >= 2:
                        sync.wait_ge(s_ve.h, ve1b + (rt - 1) * 2)
                    s_ld.dma(sync.dma_start(out=xt[rt % 2][:, :],
                                            in_=x_in.ap()[rt * 128:(rt + 1) * 128, :]))

            @blk.vector
            def _(vector):
                for rt in range(8):
                    xx = xt[rt % 2]
                    vector.wait_ge(s_ld.h, ld1 + 32 + (rt + 1) * 16)
                    if rt >= 2:
                        vector.wait_ge(s_mm.h, mm1 + (rt - 1) * 16)
                    for q in range(4):
                        vector.bn_stats(bnb[:, q * 6:(q + 1) * 6],
                                        xx[:, q * 512:(q + 1) * 512])
                    vector.tensor_tensor(b6a[:, :], bnb[:, 0:6], bnb[:, 6:12], ALU.add)
                    vector.tensor_tensor(b6b[:, :], bnb[:, 12:18], bnb[:, 18:24], ALU.add)
                    vector.tensor_tensor(sq24[:, :], bnb[:, :], bnb[:, :], ALU.mult)
                    vector.tensor_tensor(b6c[:, :], b6a[:, :], b6b[:, :], ALU.add)
                    vector.tensor_tensor(s6a[:, :], sq24[:, 0:6], sq24[:, 6:12], ALU.add)
                    vector.tensor_tensor(s6b[:, :], sq24[:, 12:18], sq24[:, 18:24], ALU.add)
                    vector.tensor_tensor(vvs[:, 0:1], b6c[:, 1:2], b6c[:, 4:5], ALU.add)
                    vector.tensor_tensor(s6c[:, :], s6a[:, :], s6b[:, :], ALU.add)
                    vector.tensor_tensor(vvs[:, 1:2], b6c[:, 2:3], b6c[:, 5:6], ALU.add)
                    vector.tensor_scalar(mvec[:, :], vvs[:, 0:1], 1.0 / 8.0, None, ALU.mult)
                    vector.tensor_tensor(vvs[:, 3:4], s6c[:, 1:2], s6c[:, 4:5], ALU.add)
                    vector.tensor_scalar(vvs[:, 2:3], vvs[:, 1:2], 1.0 / D, None, ALU.mult)
                    vector.tensor_scalar(vvs[:, 4:5], vvs[:, 3:4], 1.0 / 8.0, None, ALU.mult)
                    vector.tensor_tensor(vvs[:, 6:7], mvec[:, :], mvec[:, :], ALU.mult)
                    fence(vector, vector.tensor_tensor(vvs[:, 5:6], vvs[:, 2:3],
                                                       vvs[:, 4:5], ALU.add))
                    fence(vector, vector.tensor_tensor(vvs[:, 7:8], vvs[:, 5:6],
                                                       vvs[:, 6:7], ALU.subtract))
                    s_ve.one(vector.tensor_scalar(vvar[rt % 2][:, :], vvs[:, 7:8],
                                                  1.0, EPS, ALU.mult, ALU.add))
                    vector.wait_ge(s_ac.h, ac1 + (rt + 1 if rt <= 1 else 5 * rt - 3))
                    fence(vector, vector.reciprocal(rstd[:, :], sqv[rt % 2][:, :]))
                    vector.tensor_scalar(lnt[:, :], xx[:, :], mvec[:, :], rstd[:, :],
                                         ALU.subtract, ALU.mult)
                    vector.tensor_tensor(lnu[:, :], lnt[:, :], bca[:, :], ALU.mult)
                    s_ve.one(vector.tensor_tensor(h1t[rt % 2][:, :], lnu[:, :],
                                                  bcb[:, :], ALU.add))

            evac_cnt1 = {}

            @blk.scalar
            def _(scalar):
                def evacs(rt):
                    for grp in range(4):
                        scalar.wait_ge(s_mm.h, mm1 + rt * 16 + (grp + 1) * 4)
                        s_ac.one(scalar.copy(
                            bass.AP(h1fm, (4 * grp) * R + rt * 128,
                                    [[16 * R, 128], [R, 4], [1, 128]]),
                            htp[grp % 2][:, 0:512]))
                        evac_cnt1[(rt, grp)] = s_ac.n
                for rt in range(8):
                    scalar.wait_ge(s_ve.h, ve1b + rt * 2 + 1)
                    s_ac.one(scalar.activation(sqv[rt % 2][:, :], vvar[rt % 2][:, :],
                                               AF.Sqrt))
                    if rt >= 1:
                        evacs(rt - 1)
                evacs(7)

            @blk.tensor
            def _(tensor):
                for rt in range(8):
                    tensor.wait_ge(s_ve.h, ve1b + rt * 2 + 2)
                    for dc in range(16):
                        grp = dc // 4
                        gidx = rt * 4 + grp
                        if gidx >= 2 and dc % 4 == 0:
                            prt, pgrp = (gidx - 2) // 4, (gidx - 2) % 4
                            tensor.wait_ge(s_ac.h, evac_cnt1[(prt, pgrp)])
                        s_mm.one(tensor.transpose(
                            htp[grp % 2][:, (dc % 4) * 128:((dc % 4) + 1) * 128],
                            h1t[rt % 2][:, dc * 128:(dc + 1) * 128],
                            ident[:, :]))

            @blk.gpsimd
            def _(gpsimd):
                if "d_vecs" in dump_t:
                    gpsimd.wait_ge(s_ld.h, ld1 + 32)
                    s_gl.dma(gpsimd.dma_start(out=dump_t["d_vecs"].ap()[:, :],
                                              in_=vec_s.ap()[:, :]))
                    if "d_bca" in dump_t:
                        s_gl.dma(gpsimd.dma_start(out=dump_t["d_bca"].ap()[:, 0:D],
                                                  in_=bca[:, :]))
                        s_gl.dma(gpsimd.dma_start(out=dump_t["d_bca"].ap()[:, D:2 * D],
                                                  in_=bcb[:, :]))
                    gpsimd.wait_ge(s_gl.h, s_gl.n)
                if "d_h1" in dump_t:
                    gpsimd.wait_ge(s_ac.h, s_ac.n)
                    s_gl.dma(gpsimd.dma_start(out=dump_t["d_h1"].ap()[:, :],
                                              in_=h1fm[:, :]))
                    gpsimd.wait_ge(s_gl.h, s_gl.n)

        with contextlib.ExitStack() as P:
            PE = P.enter_context
            wful = PE(nc.sbuf_tensor("wful", [128, 16 * D], BF16))
            qfm = PE(nc.sbuf_tensor("qfm", [128, 16 * R], BF16))
            vfm = PE(nc.sbuf_tensor("vfm", [128, 16 * R], BF16))
            bqv = PE(nc.sbuf_tensor("bqv", [128, 64], F32))
            on128 = PE(nc.sbuf_tensor("on128", [128, 128], BF16))
            qnp = [PE(nc.sbuf_tensor(f"qnp{i}", [128, 1], F32)) for i in range(2)]
            qntmp = PE(nc.sbuf_tensor("qntmp", [128, 1], F32))
            qnbf = PE(nc.sbuf_tensor("qnbf", [1, 1], BF16))
            pp = PE(nc.psum_tensor("pp", [128, 1024], F32))
            npp = [PE(nc.psum_tensor(f"npp{i}", [128, 512], F32)) for i in range(2)]
            blk = PE(nc.Block())

            mm2, ve2b, gl2b, ac2 = s_mm.n, s_ve.n, s_gl.n, s_ac.n

            @blk.gpsimd
            def _(gpsimd):
                for j, src in enumerate((qb_in, kb_in, vb_in)):
                    s_gl.dma(gpsimd.dma_start(
                        out=bqv[:, j * 16:(j + 1) * 16],
                        in_=bass.AP(src, 0, [[1, 128], [128, 16]]),
                        allow_slow_non_contiguous=True))
                s_gl.one(gpsimd.memset(on128[:, :], 1.0))
                for j, src in enumerate((qT_in, kT_in, vT_in)):
                    if j == 2:
                        gpsimd.wait_ge(s_ve.h, ve2b + 33)
                        for ft in range(16):
                            s_gl.dma(gpsimd.dma_start(
                                out=bass.AP(xqv_i0, 128 + ft * 128 * R,
                                            [[R, 128], [1, R]]),
                                in_=qfm[:, ft * R:(ft + 1) * R]))
                    for dc in range(16):
                        if j >= 1:
                            gpsimd.wait_ge(s_mm.h,
                                           mm2 + (j - 1) * 512 + 15 * 32 + 16 + dc + 1)
                        s_gl.dma(gpsimd.dma_start(
                            out=wful[:, dc * D:(dc + 1) * D],
                            in_=src.ap()[dc * 128:(dc + 1) * 128, :]))
                gpsimd.wait_ge(s_ve.h, ve2b + 131)
                s_gl.dma(gpsimd.dma_start(
                    out=bass.AP(xqv_i0, 0, [[1, 1], [1, 1]]),
                    in_=qnbf[:, :]))
                gpsimd.wait_ge(s_gl.h, s_gl.n)
                s_cc.one(gpsimd.collective_compute(
                    "AllGather", ALU.bypass,
                    replica_groups=[[0, 1, 2, 3], [4, 5, 6, 7]],
                    ins=[xqv_i0.ap().opt()], outs=[xqv_o0.ap().opt()]))
                s_cc.one(gpsimd.collective_compute(
                    "AllReduce", ALU.add,
                    replica_groups=[[0, 1, 2, 3], [4, 5, 6, 7]],
                    ins=[bass.AP(xqv_i0, 128, [[1, 1], [1, 16]]).opt()],
                    outs=[xsy2_o.ap().opt()]))
                CCM["q"] = s_cc.n

            @blk.tensor
            def _(tensor):
                for j in range(3):
                    for ft in range(16):
                        idx = j * 16 + ft
                        if j <= 1 and idx >= 1:
                            tensor.wait_ge(s_ve.h, ve2b + idx * 2)
                        elif j == 2 and ft >= 1:
                            tensor.wait_ge(s_ve.h, ve2b + 130 + ft * 2)
                        for rc in range(2):
                            for dc in range(16):
                                if ft == 0 and rc == 0:
                                    tensor.wait_ge(s_gl.h,
                                                   gl2b + 49 + (j * 16 + dc + 1) * 16)
                                s_mm.one(tensor.matmul(
                                    pp[:, rc * 512:(rc + 1) * 512],
                                    wful[:, dc * D + ft * 128: dc * D + (ft + 1) * 128],
                                    h1fm[:, dc * R + rc * 512: dc * R + (rc + 1) * 512],
                                    start=(dc == 0), stop=(dc == 15)))
                    if j == 1:
                        tensor.wait_ge(s_ve.h, ve2b + 64 + 1)
                        for ch in range(32):
                            if ch >= 2:
                                tensor.wait_ge(s_ve.h, ve2b + 65 + (ch - 1) * 2)
                            s_mm.one(tensor.matmul(npp[ch % 2][:, :], on128[:, :],
                                                   vfm[:, ch * 512:(ch + 1) * 512],
                                                   start=True, stop=True))

            @blk.vector
            def _(vector):
                vector.wait_ge(s_gl.h, gl2b + 48)
                vector.tensor_scalar(bqv[:, 48:64], bqv[:, 16:32], SCALE, None,
                                     ALU.mult)

                def mm_of(j, ft):
                    return mm2 + (j * 512 if j <= 1 else 1056) + (ft + 1) * 32

                def evacs(j, dst, bofs, scl):
                    for ft in range(16):
                        vector.wait_ge(s_mm.h, mm_of(j, ft))
                        for rc in range(2):
                            sl = dst[:, ft * R + rc * 512: ft * R + (rc + 1) * 512]
                            if scl is None:
                                inst = vector.tensor_scalar(
                                    sl, pp[:, rc * 512:(rc + 1) * 512],
                                    bqv[:, bofs + ft:bofs + ft + 1], None, ALU.add)
                            else:
                                inst = vector.tensor_scalar(
                                    sl, pp[:, rc * 512:(rc + 1) * 512],
                                    scl, bqv[:, 32 + bofs + ft:33 + bofs + ft],
                                    ALU.mult, ALU.add)
                            s_ve.one(inst)

                evacs(0, qfm, 0, None)
                evacs(1, kfm, 16, SCALE)
                s_ve.one(vector.tensor_tensor(vfm[:, :], qfm[:, :], qfm[:, :], ALU.mult))
                for ch in range(32):
                    vector.wait_ge(s_mm.h, mm2 + 1024 + ch + 1)
                    s_ve.one(vector.tensor_reduce(qntmp[:, :], npp[ch % 2][:, :],
                                                  AX.X, ALU.max))
                    vector.wait_ge(s_ve.h, s_ve.n)
                    if ch == 0:
                        s_ve.one(vector.tensor_copy(qnp[0][:, :], qntmp[:, :]))
                    else:
                        s_ve.one(vector.tensor_tensor(
                            qnp[ch % 2][:, :], qnp[(ch + 1) % 2][:, :],
                            qntmp[:, :], ALU.max))
                        vector.wait_ge(s_ve.h, s_ve.n)
                s_ve.one(vector.tensor_copy(qnbf[:, :], qnp[1][0:1, 0:1]))
                vector.wait_ge(s_mm.h, mm2 + 1056)
                evacs(2, vfm, 32, None)

            @blk.gpsimd
            def _(gpsimd):
                gpsimd.wait_ge(s_ve.h, ve2b + 147)
                for ft in range(8):
                    s_gl.dma(gpsimd.dma_start(
                        out=bass.AP(xqv_i1, ft * 128 * R, [[R, 128], [1, R]]),
                        in_=vfm[:, ft * R:(ft + 1) * R]))
                gpsimd.wait_ge(s_gl.h, s_gl.n)
                s_cc.one(gpsimd.collective_compute(
                    "AllGather", ALU.bypass,
                    replica_groups=[[0, 1, 2, 3], [4, 5, 6, 7]],
                    ins=[xqv_i1.ap().opt()], outs=[xqv_o1.ap().opt()]))
                s_cc.one(gpsimd.collective_compute(
                    "AllReduce", ALU.add,
                    replica_groups=[[0, 1, 2, 3], [4, 5, 6, 7]],
                    ins=[bass.AP(xqv_i1, 0, [[1, 1], [1, 16]]).opt()],
                    outs=[xsy3_o.ap().opt()]))
                CCM["v1"] = s_cc.n
                gpsimd.wait_ge(s_ve.h, s_ve.n)
                for ft in range(8, 16):
                    s_gl.dma(gpsimd.dma_start(
                        out=bass.AP(xqv_i2, (ft - 8) * 128 * R, [[R, 128], [1, R]]),
                        in_=vfm[:, ft * R:(ft + 1) * R]))
                gpsimd.wait_ge(s_gl.h, s_gl.n)
                s_cc.one(gpsimd.collective_compute(
                    "AllGather", ALU.bypass,
                    replica_groups=[[0, 1, 2, 3], [4, 5, 6, 7]],
                    ins=[xqv_i2.ap().opt()], outs=[xqv_o2.ap().opt()]))
                s_cc.one(gpsimd.collective_compute(
                    "AllReduce", ALU.add,
                    replica_groups=[[0, 1, 2, 3], [4, 5, 6, 7]],
                    ins=[bass.AP(xqv_i2, 0, [[1, 1], [1, 16]]).opt()],
                    outs=[xsy4_o.ap().opt()]))
                CCM["v2"] = s_cc.n
                if "d_qfm" in dump_t:
                    s_gl.dma(gpsimd.dma_start(out=dump_t["d_qfm"].ap()[:, :], in_=qfm[:, :]))
                if "d_kfm" in dump_t:
                    s_gl.dma(gpsimd.dma_start(out=dump_t["d_kfm"].ap()[:, :], in_=kfm[:, :]))
                if "d_vfm" in dump_t:
                    s_gl.dma(gpsimd.dma_start(out=dump_t["d_vfm"].ap()[:, :], in_=vfm[:, :]))
                gpsimd.wait_ge(s_gl.h, s_gl.n)

        st_h1.close()

        st_afm = contextlib.ExitStack()
        afm = st_afm.enter_context(nc.sbuf_tensor("afm", [128, 16 * R], BF16, side="left"))
        st_att = contextlib.ExitStack()
        AE = st_att.enter_context
        bounds = AE(nc.sbuf_tensor("bounds", [128, 128], F32, side="left"))
        boundsr = AE(nc.sbuf_tensor("boundsr", [128, 128], F32, side="left"))
        qh = [AE(nc.sbuf_tensor(f"qh{i}", [128, T], BF16, side="left")) for i in range(2)]
        vhfm = [AE(nc.sbuf_tensor(f"vhfm{i}", [128, T], BF16, side="left")) for i in range(2)]
        vrm = [AE(nc.sbuf_tensor(f"vrm{i}", [128, 32 * 129], BF16, side="left")) for i in range(2)]

        with contextlib.ExitStack() as P:
            PE = P.enter_context
            sqk = PE(nc.sbuf_tensor("sqk", [128, 16 * R], BF16))
            on128b = PE(nc.sbuf_tensor("on128b", [128, 128], BF16))
            normb = PE(nc.sbuf_tensor("normb", [128, 512], F32))
            qmt = PE(nc.sbuf_tensor("qmt", [1, 4], BF16))
            qmf = PE(nc.sbuf_tensor("qmf", [1, 4], F32))
            qms = PE(nc.sbuf_tensor("qms", [1, 1], F32))
            qms2 = PE(nc.sbuf_tensor("qms2", [1, 1], F32))
            qmaxv = PE(nc.sbuf_tensor("qmaxv", [128, 1], F32))
            nppb = [PE(nc.psum_tensor(f"nppb{i}", [128, 512], F32)) for i in range(2)]
            blk = PE(nc.Block())

            mm3, ve3b, ac3, gl3b, ld3 = s_mm.n, s_ve.n, s_ac.n, s_gl.n, s_ld.n

            @blk.gpsimd
            def _(gpsimd):
                s_gl.one(gpsimd.memset(on128b[:, :], 1.0))
                for h in range(H):
                    for ch in range(2):
                        i2 = h * 2 + ch
                        gpsimd.wait_ge(s_ac.h, ac3 + i2 + 1)
                        s_gl.dma(gpsimd.dma_start(
                            out=nrm_s.ap()[h:h + 1, ch * 512:(ch + 1) * 512],
                            in_=normb[0:1, 0:512]))
                gpsimd.wait_ge(s_cc.h, CCM["q"])
                s_gl.dma(gpsimd.dma_start(
                    out=qmt[:, :],
                    in_=bass.AP(xqv_o0, 0, [[1, 1], [P0LEN, 4]]),
                    allow_slow_non_contiguous=True))

            @blk.vector
            def _(vector):
                vector.wait_ge(s_gl.h, gl3b + 1)
                s_ve.one(vector.tensor_tensor(sqk[:, :], kfm[:, :], kfm[:, :], ALU.mult))
                vector.wait_ge(s_gl.h, gl3b + 1 + 32 * 16 + 16)
                s_ve.one(vector.tensor_copy(qmf[:, :], qmt[:, :]))
                vector.wait_ge(s_ve.h, s_ve.n)
                s_ve.one(vector.tensor_reduce(qms[:, :], qmf[:, :], AX.X, ALU.max))

            @blk.tensor
            def _(tensor):
                tensor.wait_ge(s_ve.h, ve3b + 1)
                for h in range(H):
                    for ch in range(2):
                        i2 = h * 2 + ch
                        if i2 >= 2:
                            tensor.wait_ge(s_ac.h, ac3 + i2 - 1)
                        s_mm.one(tensor.matmul(
                            nppb[i2 % 2][:, :], on128b[:, :],
                            sqk[:, h * R + ch * 512: h * R + (ch + 1) * 512],
                            start=True, stop=True))

            @blk.scalar
            def _(scalar):
                for h in range(H):
                    for ch in range(2):
                        i2 = h * 2 + ch
                        scalar.wait_ge(s_mm.h, mm3 + i2 + 1)
                        if i2 >= 1:
                            scalar.wait_ge(s_gl.h, gl3b + 1 + i2 * 16)
                        s_ac.one(scalar.activation(normb[:, :], nppb[i2 % 2][:, :],
                                                   AF.Sqrt))
                scalar.wait_ge(s_ve.h, ve3b + 3)
                s_ac.one(scalar.activation(qms2[:, :], qms[:, :], AF.Sqrt))

            ac3e = s_ac.n

            @blk.gpsimd
            def _(gpsimd):
                gpsimd.wait_ge(s_ac.h, ac3e)
                s_gl.dma(gpsimd.dma_start(
                    out=bass.AP(vec_s, 9 * D, [[1, 1], [1, 1]]), in_=qms2[:, :]))

            gl3e = s_gl.n

            @blk.sync
            def _(sync):
                for h in range(H):
                    sync.wait_ge(s_gl.h, gl3b + 1 + (h + 1) * 32)
                    s_ld.dma(sync.dma_start(
                        out=boundsr[:, h * 8:(h + 1) * 8],
                        in_=bass.AP(nrm_s, h * R, [[1, 128], [128, 8]]),
                        allow_slow_non_contiguous=True))
                sync.wait_ge(s_gl.h, gl3e)
                s_ld.dma(sync.dma_start(out=qmaxv[:, :],
                                        in_=bass.AP(vec_s, 9 * D, [[0, 128], [1, 1]])))

            ld3e = s_ld.n

            @blk.vector
            def _(vector):
                vector.wait_ge(s_ld.h, ld3e)
                s_ve.one(vector.tensor_scalar(bounds[:, :], boundsr[:, :],
                                              qmaxv[:, :], None, ALU.mult))

            @blk.gpsimd
            def _(gpsimd):
                if "d_bounds" in dump_t:
                    gpsimd.wait_ge(s_ve.h, s_ve.n)
                    s_gl.dma(gpsimd.dma_start(out=dump_t["d_bounds"].ap()[:, :],
                                              in_=bounds[:, :]))
                    gpsimd.wait_ge(s_gl.h, s_gl.n)

        with contextlib.ExitStack() as P:
            PE = P.enter_context
            Tt = [PE(nc.sbuf_tensor(f"Tt{i}", [128, 512], F32)) for i in range(4)]
            Pt = [PE(nc.sbuf_tensor(f"Pt{i}", [128, 512], BF16)) for i in range(4)]
            PTs = [PE(nc.sbuf_tensor(f"PTs{i}", [128, 512], BF16)) for i in range(4)]
            pat = PE(nc.sbuf_tensor("pat", [128, 2 * 512], F16))
            bvs = [PE(nc.sbuf_tensor(f"bvs{i}", [128, 8], F32)) for i in range(2)]
            sums = [PE(nc.sbuf_tensor(f"sums{i}", [128, 8], F32)) for i in range(2)]
            sumt = [PE(nc.sbuf_tensor(f"sumt{i}", [128, 1], F32)) for i in range(2)]
            sumr = [PE(nc.sbuf_tensor(f"sumr{i}", [128, 1], F32)) for i in range(2)]
            arm = [PE(nc.sbuf_tensor(f"arm{i}", [128, 128], BF16)) for i in range(2)]
            S_ps = [PE(nc.psum_tensor(f"S_ps{i}", [128, 512], F32)) for i in range(2)]
            PT_ps = [PE(nc.psum_tensor(f"PT_ps{i}", [128, 1024], BF16)) for i in range(2)]
            O_ps = [PE(nc.psum_tensor(f"O_ps{i}", [128, 512], F32)) for i in range(2)]
            vtp = PE(nc.psum_tensor("vtp", [128, 1024], BF16))
            otp = PE(nc.psum_tensor("otp", [128, 1024], BF16))
            blk = PE(nc.Block())

            pld = s_ld.n + 16

            @blk.gpsimd
            def _(gpsimd):
                for i in range(2):
                    s_gl.one(gpsimd.memset(
                        bass.AP(vrm[i], 128, [[32 * 129, 128], [129, 32], [1, 1]]), 1.0))

            gl3bb = s_gl.n

            @blk.sync
            def _(sync):
                s_ld.dma(sync.dma_start(
                    out=pat[:, :],
                    in_=bass.AP(pat_in, 0, [[512, 128], [128 * 512, 2], [1, 512]])))
                sync.wait_ge(s_cc.h, CCM["q"])
                for hh in range(H):
                    if hh >= 2:
                        sync.wait_ge(a_s.h, G0[hh - 1])
                        sync.wait_ge(a_av.h, G0[hh - 1])
                    for cp in range(4):
                        a_qv.dma(sync.dma_start(
                            out=bass.AP(qh[hh % 2], cp * 128,
                                        [[T, 128], [512, 8], [1, 128]]),
                            in_=bass.AP(xqv_o0, cp * P0LEN + 128 + hh * 128 * R,
                                        [[R, 128], [128, 8], [1, 128]])))

            @blk.gpsimd
            def _(gpsimd):
                for hh in range(H):
                    if hh == 0:
                        gpsimd.wait_ge(s_cc.h, CCM["v1"])
                    if hh == 8:
                        gpsimd.wait_ge(s_cc.h, CCM["v2"])
                    if hh >= 2:
                        gpsimd.wait_ge(a_s.h, G0[hh - 1])
                        gpsimd.wait_ge(a_av.h, G0[hh - 1])
                    vsrc = xqv_o1 if hh < 8 else xqv_o2
                    for cp in range(4):
                        a_vq.dma(gpsimd.dma_start(
                            out=bass.AP(vhfm[hh % 2], cp * 128,
                                        [[T, 128], [512, 8], [1, 128]]),
                            in_=bass.AP(vsrc, cp * PVLEN + (hh % 8) * 128 * R,
                                        [[R, 128], [128, 8], [1, 128]])))

            vprep_T = {}
            vprep_V = {}
            for h in range(1, H):
                for grp in range(4):
                    vprep_T.setdefault(G0[h - 1] + 8 + 2 * grp, []).append((h, grp))
                    vprep_V.setdefault(G0[h - 1] + 9 + 2 * grp, []).append((h, grp))

            def v_transposes(tensor, h, grp):
                vt_idx = h * 4 + grp
                tensor.wait_ge(a_qv.h, 64 * (h + 1))
                tensor.wait_ge(a_vq.h, 64 * (h + 1))
                if vt_idx >= 1:
                    tensor.wait_ge(a_ve.h, vt_idx)
                for s8 in range(8):
                    jt = grp * 8 + s8
                    inst = tensor.transpose(
                        vtp[:, s8 * 128:(s8 + 1) * 128],
                        vhfm[h % 2][:, jt * 128:(jt + 1) * 128],
                        ident[:, :])
                inst.then_inc(a_vt.h, 1)
                a_vt.n += 1

            def v_evac(vector, h, grp):
                vt_idx = h * 4 + grp
                vector.wait_ge(a_vt.h, vt_idx + 1)
                if h >= 2:
                    vector.wait_ge(a_av.h, G0[h - 1])
                vector.tensor_copy(
                    bass.AP(vrm[h % 2], grp * 8 * 129, [[32 * 129, 128], [129, 8], [1, 128]]),
                    vtp[:, :]).then_inc(a_ve.h, 1)
                a_ve.n += 1

            @blk.tensor
            def _(tensor):
                tensor.wait_ge(s_ld.h, pld)
                for grp in range(4):
                    v_transposes(tensor, 0, grp)

                def emit_S(g):
                    (gg, h, k, c, last) = CHUNKS[g]
                    if g >= 2:
                        tensor.wait_ge(a_t.h, g - 1)
                    tensor.matmul(S_ps[g % 2][:, :],
                                  kfm[:, h * R + k * 128: h * R + (k + 1) * 128],
                                  qh[h % 2][:, c * 512:(c + 1) * 512],
                                  start=True, stop=True).then_inc(a_s.h, 1)
                    a_s.n += 1

                def emit_TR(g):
                    tensor.wait_ge(a_p.h, g + 1)
                    if g >= 2:
                        tensor.wait_ge(a_e.h, g - 1)
                    for t4 in range(4):
                        inst = tensor.transpose(
                            PT_ps[g % 2][:, t4 * 128:(t4 + 1) * 128],
                            Pt[g % 4][:, t4 * 128:(t4 + 1) * 128],
                            ident[:, :])
                    inst.then_inc(a_tr.h, 1)
                    a_tr.n += 1

                def emit_AV(g):
                    (gg, h, k, c, last) = CHUNKS[g]
                    m = M_OF_G[g]
                    first = c == max(0, k + 1 - WIN)
                    tensor.wait_ge(a_e.h, g + 1)
                    if first and m >= 2:
                        tensor.wait_ge(a_oe.h, m - 1)
                    for t4 in range(4):
                        jt = c * 4 + t4
                        inst = tensor.matmul(
                            O_ps[m % 2][:, 0:129],
                            PTs[g % 4][:, t4 * 128:(t4 + 1) * 128],
                            vrm[h % 2][:, jt * 129:jt * 129 + 129],
                            start=(first and t4 == 0), stop=(last and t4 == 3))
                    inst.then_inc(a_av.h, 1)
                    a_av.n += 1

                def emit_OT(m):
                    tensor.wait_ge(a_oe.h, m + 1)
                    if m >= 1:
                        tensor.wait_ge(a_of.h, m)
                    tensor.transpose(otp[:, 0:128], arm[m % 2][:, :],
                                     ident[:, :]).then_inc(a_ot.h, 1)
                    a_ot.n += 1

                tensor.wait_ge(s_gl.h, gl3bb)
                for g in range(G_TOT + 4):
                    if g < G_TOT:
                        emit_S(g)
                    for (h, grp) in vprep_T.get(g, ()):
                        v_transposes(tensor, h, grp)
                    if 1 <= g <= G_TOT:
                        emit_TR(g - 1)
                    if 2 <= g <= G_TOT + 1:
                        emit_AV(g - 2)
                    if 4 <= g <= G_TOT + 3 and CHUNKS[g - 4][4]:
                        emit_OT(M_OF_G[g - 4])

            @blk.vector
            def _(vector):
                vector.wait_ge(s_ld.h, pld)
                for grp in range(4):
                    v_evac(vector, 0, grp)

                def emit_stt(g):
                    (gg, h, k, c, last) = CHUNKS[g]
                    m = M_OF_G[g]
                    if c == max(0, k + 1 - WIN):
                        if m >= 2:
                            vector.wait_ge(a_p.h, ITILES[m - 2][4] + 1)
                        vector.tensor_scalar(bvs[m % 2][:, :], t512[:, :],
                                             bounds[:, h * 8 + k: h * 8 + k + 1],
                                             i0bc[:, k:k + 1],
                                             ALU.subtract, ALU.subtract)
                    vector.wait_ge(a_s.h, g + 1)
                    if g >= 4:
                        vector.wait_ge(a_p.h, g - 3)
                    kp = 1 if last else 0
                    vector.tensor_tensor(Tt[g % 4][:, :], S_ps[g % 2][:, :],
                                         pat[:, kp * 512:(kp + 1) * 512],
                                         ALU.add).then_inc(a_t.h, 1)
                    a_t.n += 1

                def emit_evac_v(g):
                    vector.wait_ge(a_tr.h, g + 1)
                    if g >= 4:
                        vector.wait_ge(a_av.h, g - 3)
                    vector.tensor_copy(PTs[g % 4][:, :],
                                       PT_ps[g % 2][:, 0:512]).then_inc(a_e.h, 1)
                    a_e.n += 1

                def emit_finish1(m):
                    (mm_, h, k, nch, g_last) = ITILES[m]
                    vector.wait_ge(a_av.h, g_last + 1)
                    inst = vector.reciprocal(sumt[m % 2][:, :], O_ps[m % 2][:, 128:129])
                    if g_last >= G_TOT - 3:
                        fence(vector, inst)

                def emit_finish2(m):
                    (mm_, h, k, nch, g_last) = ITILES[m]
                    if m >= 2:
                        vector.wait_ge(a_ot.h, m - 1)
                    vector.tensor_scalar(arm[m % 2][:, :], O_ps[m % 2][:, 0:128],
                                         sumt[m % 2][:, :], None,
                                         ALU.mult).then_inc(a_oe.h, 1)
                    a_oe.n += 1

                for g in range(G_TOT + 2):
                    if g < G_TOT:
                        emit_stt(g)
                    for (h, grp) in vprep_V.get(g, ()):
                        v_evac(vector, h, grp)
                    if 1 <= g <= G_TOT and (g - 1) % 2 == 0:
                        emit_evac_v(g - 1)
                    if 1 <= g <= G_TOT and CHUNKS[g - 1][4]:
                        emit_finish1(M_OF_G[g - 1])
                    if 2 <= g <= G_TOT + 1 and CHUNKS[g - 2][4]:
                        emit_finish2(M_OF_G[g - 2])

            @blk.scalar
            def _(scalar):
                scalar.wait_ge(s_ld.h, pld)

                def emit_exp(g):
                    (gg, h, k, c, last) = CHUNKS[g]
                    m = M_OF_G[g]
                    scalar.wait_ge(a_t.h, g + 1)
                    if g >= 4:
                        scalar.wait_ge(a_tr.h, g - 3)
                    scalar.activation(Pt[g % 4][:, :], Tt[g % 4][:, :], AF.Exp,
                                      bias=bvs[m % 2][:, c:c + 1],
                                      scale=1.0).then_inc(a_p.h, 1)
                    a_p.n += 1

                def emit_evac_a(g):
                    scalar.wait_ge(a_tr.h, g + 1)
                    if g >= 4:
                        scalar.wait_ge(a_av.h, g - 3)
                    scalar.copy(PTs[g % 4][:, :],
                                PT_ps[g % 2][:, 0:512]).then_inc(a_e.h, 1)
                    a_e.n += 1

                def emit_afm(m):
                    (mm_, h, k, nch, g_last) = ITILES[m]
                    scalar.wait_ge(a_ot.h, m + 1)
                    scalar.copy(afm[:, h * R + k * 128: h * R + (k + 1) * 128],
                                otp[:, 0:128]).then_inc(a_of.h, 1)
                    a_of.n += 1

                for g in range(G_TOT + 1):
                    if g < G_TOT:
                        emit_exp(g)
                    if 1 <= g <= G_TOT and (g - 1) % 2 == 1:
                        emit_evac_a(g - 1)
                    if g >= 3 and CHUNKS[g - 3][4]:
                        emit_afm(M_OF_G[g - 3])
                for gx in (G_TOT - 2, G_TOT - 1):
                    if CHUNKS[gx][4]:
                        emit_afm(M_OF_G[gx])

            @blk.gpsimd
            def _(gpsimd):
                if "d_afm" in dump_t:
                    gpsimd.wait_ge(a_of.h, M_TOT)
                    s_gl.dma(gpsimd.dma_start(out=dump_t["d_afm"].ap()[:, :],
                                              in_=afm[:, :]))
                    gpsimd.wait_ge(s_gl.h, s_gl.n)

        st_att.close()
        st_kfm.close()

        with contextlib.ExitStack() as P:
            PE = P.enter_context
            x2 = PE(nc.sbuf_tensor("x2", [128, 8 * D], F32))
            ow = PE(nc.sbuf_tensor("ow", [128, 16 * D], BF16))
            bga = PE(nc.sbuf_tensor("bga", [128, D], F32))
            bob = PE(nc.sbuf_tensor("bob", [128, D], F32))
            t1 = PE(nc.sbuf_tensor("t1", [128, 512], F32))
            t2 = PE(nc.sbuf_tensor("t2", [128, 512], F32))
            po = PE(nc.psum_tensor("po", [128, 8 * 512], F32))
            blk = PE(nc.Block())

            mm4, ve4, gl4, ld4 = s_mm.n, s_ve.n, s_gl.n, s_ld.n

            @blk.gpsimd
            def _(gpsimd):
                for dc in range(16):
                    s_gl.dma(gpsimd.dma_start(
                        out=ow[:, dc * D:(dc + 1) * D],
                        in_=oT_in.ap()[dc * 128:(dc + 1) * 128, :]))

            @blk.sync
            def _(sync):
                s_ld.dma(sync.dma_start(out=bga[:, :],
                                        in_=bass.AP(vec_s, 2 * D, [[0, 128], [1, D]])))
                s_ld.dma(sync.dma_start(out=bob[:, :],
                                        in_=bass.AP(ob_in, 0, [[0, 128], [1, D]])))
                for rt in range(8):
                    s_ld.dma(sync.dma_start(
                        out=x2[:, rt * D:(rt + 1) * D],
                        in_=x_in.ap()[rt * 128:(rt + 1) * 128, :]))

            @blk.tensor
            def _(tensor):
                for fg in range(4):
                    for dc in range(16):
                        if fg == 0:
                            tensor.wait_ge(s_gl.h, gl4 + (dc + 1) * 16)
                        for rt in range(8):
                            if fg >= 1 and dc == 0:
                                tensor.wait_ge(s_ve.h, ve4 + (fg - 1) * 8 + rt + 1)
                            s_mm.one(tensor.matmul(
                                po[:, rt * 512:(rt + 1) * 512],
                                afm[:, dc * R + rt * 128: dc * R + (rt + 1) * 128],
                                ow[:, dc * D + fg * 512: dc * D + (fg + 1) * 512],
                                start=(dc == 0), stop=(dc == 15)))

            @blk.vector
            def _(vector):
                vector.wait_ge(s_ld.h, ld4 + 16 * 10)
                for fg in range(4):
                    for rt in range(8):
                        vector.wait_ge(s_mm.h, mm4 + fg * 128 + 15 * 8 + rt + 1)
                        vector.tensor_tensor(t1[:, :], po[:, rt * 512:(rt + 1) * 512],
                                             bob[:, fg * 512:(fg + 1) * 512], ALU.add)
                        vector.tensor_tensor(t2[:, :], t1[:, :],
                                             bga[:, fg * 512:(fg + 1) * 512], ALU.mult)
                        xsl = x2[:, rt * D + fg * 512: rt * D + (fg + 1) * 512]
                        s_ve.one(vector.tensor_tensor(xsl, xsl, t2[:, :], ALU.add))

            ve4e = s_ve.n

            @blk.gpsimd
            def _(gpsimd):
                gpsimd.wait_ge(s_ve.h, ve4e)
                for rt in range(8):
                    s_gl.dma(gpsimd.dma_start(out=x2_d.ap()[rt * 128:(rt + 1) * 128, :],
                                              in_=x2[:, rt * D:(rt + 1) * D]))
                gpsimd.wait_ge(s_gl.h, s_gl.n)
                if "d_x2" in dump_t:
                    s_gl.dma(gpsimd.dma_start(out=dump_t["d_x2"].ap()[:, :], in_=x2[:, :]))
                    gpsimd.wait_ge(s_gl.h, s_gl.n)

        st_afm.close()

        st_qx = contextlib.ExitStack()
        qxfm = st_qx.enter_context(nc.sbuf_tensor("qxfm", [128, 16 * R], BF16, side="left"))

        with contextlib.ExitStack() as P:
            PE = P.enter_context
            bcb1 = PE(nc.sbuf_tensor("bcb1", [128, D], F32))
            bsb = PE(nc.sbuf_tensor("bsb", [128, D], F32))
            lnt2 = PE(nc.sbuf_tensor("lnt2", [128, D], F32))
            h2t = PE(nc.sbuf_tensor("h2t", [128, D], F32))
            q_rm = [PE(nc.sbuf_tensor(f"q_rm{i}", [128, D], BF16)) for i in range(2)]
            qi = PE(nc.sbuf_tensor("qi", [128, D], I32))
            bnb2 = PE(nc.sbuf_tensor("bnb2", [128, 24], F32))
            c6a = PE(nc.sbuf_tensor("c6a", [128, 6], F32))
            c6b = PE(nc.sbuf_tensor("c6b", [128, 6], F32))
            c6c = PE(nc.sbuf_tensor("c6c", [128, 6], F32))
            sq24b = PE(nc.sbuf_tensor("sq24b", [128, 24], F32))
            d6a = PE(nc.sbuf_tensor("d6a", [128, 6], F32))
            d6b = PE(nc.sbuf_tensor("d6b", [128, 6], F32))
            d6c = PE(nc.sbuf_tensor("d6c", [128, 6], F32))
            vvs2 = PE(nc.sbuf_tensor("vvs2", [128, 12], F32))
            lnu2 = PE(nc.sbuf_tensor("lnu2", [128, D], F32))
            amx2 = PE(nc.sbuf_tensor("amx2", [128, 1], F32))
            mv2 = PE(nc.sbuf_tensor("mv2", [128, 1], F32))
            vva = [PE(nc.sbuf_tensor(f"vva{i}", [128, 1], F32)) for i in range(2)]
            sqa = [PE(nc.sbuf_tensor(f"sqa{i}", [128, 1], F32)) for i in range(2)]
            rst2 = PE(nc.sbuf_tensor("rst2", [128, 1], F32))
            amx = PE(nc.sbuf_tensor("amx", [128, 1], F32))
            r1v = PE(nc.sbuf_tensor("r1v", [128, 1], F32))
            htq = [PE(nc.psum_tensor(f"htq{i}", [128, 1024], BF16)) for i in range(2)]
            x2t = [PE(nc.sbuf_tensor(f"x2t{i}", [128, D], F32)) for i in range(2)]
            blk = PE(nc.Block())

            ve5, ac5, mm5, ld5 = s_ve.n, s_ac.n, s_mm.n, s_ld.n

            @blk.sync
            def _(sync):
                s_ld.dma(sync.dma_start(out=bcb1[:, :],
                                        in_=bass.AP(vec_s, 4 * D, [[0, 128], [1, D]])))
                s_ld.dma(sync.dma_start(out=bsb[:, :],
                                        in_=bass.AP(vec_s, 3 * D, [[0, 128], [1, D]])))
                for rt in range(8):
                    if rt >= 2:
                        sync.wait_ge(s_ve.h, ve5 + (rt - 1) * 2)
                    s_ld.dma(sync.dma_start(out=x2t[rt % 2][:, :],
                                            in_=x2_d.ap()[rt * 128:(rt + 1) * 128, :]))

            @blk.vector
            def _(vector):
                for rt in range(8):
                    xx = x2t[rt % 2][:, :]
                    vector.wait_ge(s_ld.h, ld5 + 32 + (rt + 1) * 16)
                    if rt >= 2:
                        vector.wait_ge(s_mm.h, mm5 + (rt - 1) * 16)
                    for q in range(4):
                        vector.bn_stats(bnb2[:, q * 6:(q + 1) * 6],
                                        xx[:, q * 512:(q + 1) * 512])
                    vector.tensor_tensor(c6a[:, :], bnb2[:, 0:6], bnb2[:, 6:12], ALU.add)
                    vector.tensor_tensor(c6b[:, :], bnb2[:, 12:18], bnb2[:, 18:24], ALU.add)
                    vector.tensor_tensor(sq24b[:, :], bnb2[:, :], bnb2[:, :], ALU.mult)
                    vector.tensor_tensor(c6c[:, :], c6a[:, :], c6b[:, :], ALU.add)
                    vector.tensor_tensor(d6a[:, :], sq24b[:, 0:6], sq24b[:, 6:12], ALU.add)
                    vector.tensor_tensor(d6b[:, :], sq24b[:, 12:18], sq24b[:, 18:24], ALU.add)
                    vector.tensor_tensor(vvs2[:, 0:1], c6c[:, 1:2], c6c[:, 4:5], ALU.add)
                    vector.tensor_tensor(d6c[:, :], d6a[:, :], d6b[:, :], ALU.add)
                    vector.tensor_tensor(vvs2[:, 1:2], c6c[:, 2:3], c6c[:, 5:6], ALU.add)
                    vector.tensor_scalar(mv2[:, :], vvs2[:, 0:1], 1.0 / 8.0, None, ALU.mult)
                    vector.tensor_tensor(vvs2[:, 3:4], d6c[:, 1:2], d6c[:, 4:5], ALU.add)
                    vector.tensor_scalar(vvs2[:, 2:3], vvs2[:, 1:2], 1.0 / D, None, ALU.mult)
                    vector.tensor_scalar(vvs2[:, 4:5], vvs2[:, 3:4], 1.0 / 8.0, None, ALU.mult)
                    vector.tensor_tensor(vvs2[:, 6:7], mv2[:, :], mv2[:, :], ALU.mult)
                    fence(vector, vector.tensor_tensor(vvs2[:, 5:6], vvs2[:, 2:3],
                                                       vvs2[:, 4:5], ALU.add))
                    fence(vector, vector.tensor_tensor(vvs2[:, 7:8], vvs2[:, 5:6],
                                                       vvs2[:, 6:7], ALU.subtract))
                    s_ve.one(vector.tensor_scalar(vva[rt % 2][:, :], vvs2[:, 7:8],
                                                  1.0, EPS, ALU.mult, ALU.add))
                    vector.wait_ge(s_ac.h, ac5 + (rt + 1 if rt <= 1 else 5 * rt - 3))
                    fence(vector, vector.reciprocal(rst2[:, :], sqa[rt % 2][:, :]))
                    vector.tensor_scalar(lnt2[:, :], xx[:, :], mv2[:, :], rst2[:, :],
                                         ALU.subtract, ALU.mult)
                    vector.tensor_tensor(lnu2[:, :], lnt2[:, :], bcb1[:, :], ALU.mult)
                    vector.tensor_tensor(h2t[:, :], lnu2[:, :], bsb[:, :], ALU.add)
                    fence(vector, vector.tensor_reduce(amx[:, :], h2t[:, :], AX.X,
                                                       ALU.max,
                                                       apply_absolute_value=True))
                    fence(vector, vector.tensor_scalar(amx2[:, :], amx[:, :], TINY,
                                                       None, ALU.max))
                    fence(vector, vector.reciprocal(r1v[:, :], amx2[:, :]))
                    vector.tensor_scalar(sxv[:, rt:rt + 1], amx2[:, :], 1.0 / QMAX,
                                         None, ALU.mult)
                    vector.tensor_scalar(lnu2[:, :], h2t[:, :], r1v[:, :], QMAX,
                                         ALU.mult, ALU.mult)
                    vector.tensor_copy(qi[:, :], lnu2[:, :])
                    s_ve.one(vector.tensor_scalar(q_rm[rt % 2][:, :], qi[:, :],
                                                  QMAX, -QMAX, ALU.min, ALU.max))

            evac_cnt5 = {}

            @blk.scalar
            def _(scalar):
                def evacs(rt):
                    for grp in range(4):
                        scalar.wait_ge(s_mm.h, mm5 + rt * 16 + (grp + 1) * 4)
                        s_ac.one(scalar.copy(
                            bass.AP(qxfm, (4 * grp) * R + rt * 128,
                                    [[16 * R, 128], [R, 4], [1, 128]]),
                            htq[grp % 2][:, 0:512]))
                        evac_cnt5[(rt, grp)] = s_ac.n
                for rt in range(8):
                    scalar.wait_ge(s_ve.h, ve5 + rt * 2 + 1)
                    s_ac.one(scalar.activation(sqa[rt % 2][:, :], vva[rt % 2][:, :],
                                               AF.Sqrt))
                    if rt >= 1:
                        evacs(rt - 1)
                evacs(7)

            @blk.tensor
            def _(tensor):
                for rt in range(8):
                    tensor.wait_ge(s_ve.h, ve5 + rt * 2 + 2)
                    for dc in range(16):
                        grp = dc // 4
                        gidx = rt * 4 + grp
                        if gidx >= 2 and dc % 4 == 0:
                            prt, pgrp = (gidx - 2) // 4, (gidx - 2) % 4
                            tensor.wait_ge(s_ac.h, evac_cnt5[(prt, pgrp)])
                        s_mm.one(tensor.transpose(
                            htq[grp % 2][:, (dc % 4) * 128:((dc % 4) + 1) * 128],
                            q_rm[rt % 2][:, dc * 128:(dc + 1) * 128],
                            ident[:, :]))

            @blk.gpsimd
            def _(gpsimd):
                if "d_qx" in dump_t:
                    gpsimd.wait_ge(s_ac.h, s_ac.n)
                    s_gl.dma(gpsimd.dma_start(out=dump_t["d_qx"].ap()[:, :], in_=qxfm[:, :]))
                    gpsimd.wait_ge(s_gl.h, s_gl.n)

        with contextlib.ExitStack() as P:
            PE = P.enter_context
            akv = PE(nc.sbuf_tensor("akv", [128, 16], F32))
            rkv = PE(nc.sbuf_tensor("rkv", [128, 16], F32))
            skv = PE(nc.sbuf_tensor("skv", [128, 16], F32))
            brk = PE(nc.sbuf_tensor("brk", [128, D], F32))
            qk = PE(nc.sbuf_tensor("qk", [128, 16 * D], BF16))
            ki32 = PE(nc.sbuf_tensor("ki32", [128, D], I32))
            p6 = PE(nc.psum_tensor("p6", [128, 8 * 512], F32))
            P6i = PE(contextlib.ExitStack())
            kt = [P6i.enter_context(nc.sbuf_tensor(f"kt{i}", [128, D], F32))
                  for i in range(2)]
            blk = P6i.enter_context(nc.Block())

            ve6, gl6, ld6, mm6, st6 = s_ve.n, s_gl.n, s_ld.n, s_mm.n, s_st.n

            @blk.sync
            def _(sync):
                for ft in range(16):
                    if ft >= 2:
                        sync.wait_ge(s_ve.h, ve6 + ft - 1)
                    s_ld.dma(sync.dma_start(out=kt[ft % 2][:, :],
                                            in_=fkT_in.ap()[ft * 128:(ft + 1) * 128, :]))

            @blk.vector
            def _(vector):
                for ft in range(16):
                    vector.wait_ge(s_ld.h, ld6 + (ft + 1) * 16)
                    s_ve.one(vector.tensor_reduce(
                        akv[:, ft:ft + 1], kt[ft % 2][:, :], AX.X, ALU.max,
                        apply_absolute_value=True))
                    if ft == 15:
                        vector.wait_ge(s_ve.h, s_ve.n)
                s_ve.one(vector.tensor_scalar(skv[:, :], akv[:, :], TINY, 1.0 / QMAX,
                                              ALU.max, ALU.mult))
                vector.wait_ge(s_ve.h, s_ve.n)
                s_ve.one(vector.reciprocal(rkv[:, :], skv[:, :]))

            @blk.gpsimd
            def _(gpsimd):
                gpsimd.wait_ge(s_ve.h, ve6 + 18)
                s_gl.dma(gpsimd.dma_start(
                    out=bass.AP(vec_s, 7 * D, [[1, 128], [128, 16]]), in_=skv[:, :],
                    allow_slow_non_contiguous=True))
                s_gl.dma(gpsimd.dma_start(
                    out=bass.AP(vec_s, 8 * D, [[1, 128], [128, 16]]), in_=rkv[:, :],
                    allow_slow_non_contiguous=True))

            @blk.sync
            def _(sync):
                sync.wait_ge(s_gl.h, gl6 + 32)
                s_ld.dma(sync.dma_start(out=brk[:, :],
                                        in_=bass.AP(vec_s, 8 * D, [[0, 128], [1, D]])))
                for dc in range(16):
                    if dc >= 2:
                        sync.wait_ge(s_ve.h, ve6 + 18 + dc - 1)
                    s_ld.dma(sync.dma_start(out=kt[dc % 2][:, :],
                                            in_=fk_in.ap()[dc * 128:(dc + 1) * 128, :]))

            @blk.vector
            def _(vector):
                for dc in range(16):
                    vector.wait_ge(s_ld.h, ld6 + 16 * 16 + 16 + (dc + 1) * 16)
                    vector.tensor_tensor(ki32[:, :], kt[dc % 2][:, :], brk[:, :], ALU.mult)
                    s_ve.one(vector.tensor_scalar(qk[:, dc * D:(dc + 1) * D], ki32[:, :],
                                                  QMAX, -QMAX, ALU.min, ALU.max))

            P6i.close()
            bsk = PE(nc.sbuf_tensor("bsk", [128, D], F32))
            bbq = PE(nc.sbuf_tensor("bbq", [128, D], F32))
            bgb = PE(nc.sbuf_tensor("bgb", [128, D], F32))
            t3 = PE(nc.sbuf_tensor("t3", [128, D], F32))
            t4 = PE(nc.sbuf_tensor("t4", [128, D], F32))
            t5 = PE(nc.sbuf_tensor("t5", [128, D], F32))
            ay2 = PE(nc.sbuf_tensor("ay2", [128, 1], F32))
            x2u = [PE(nc.sbuf_tensor(f"x2u{i}", [128, D], F32)) for i in range(2)]
            outt = [PE(nc.sbuf_tensor(f"outt{i}", [128, D], F32)) for i in range(2)]
            ay = PE(nc.sbuf_tensor("ay", [128, 1], F32))
            syv = PE(nc.sbuf_tensor("syv", [128, 1], F32))
            ryv = PE(nc.sbuf_tensor("ryv", [128, 1], F32))
            blk2 = PE(nc.Block())
            ld6b = s_ld.n

            @blk2.sync
            def _(sync):
                s_ld.dma(sync.dma_start(out=bsk[:, :],
                                        in_=bass.AP(vec_s, 7 * D, [[0, 128], [1, D]])))
                s_ld.dma(sync.dma_start(out=bbq[:, :],
                                        in_=bass.AP(vec_s, 6 * D, [[0, 128], [1, D]])))
                s_ld.dma(sync.dma_start(out=bgb[:, :],
                                        in_=bass.AP(vec_s, 5 * D, [[0, 128], [1, D]])))
                for rt in range(8):
                    if rt >= 2:
                        sync.wait_ge(s_ve.h, ve6 + 34 + rt - 1)
                    s_ld.dma(sync.dma_start(out=x2u[rt % 2][:, :],
                                            in_=x2_d.ap()[rt * 128:(rt + 1) * 128, :]))

            @blk2.tensor
            def _(tensor):
                for rt in range(8):
                    half = rt % 2
                    if rt >= 2:
                        tensor.wait_ge(s_ve.h, ve6 + 34 + (rt - 1))
                    for dc in range(16):
                        for fg in range(4):
                            s_mm.one(tensor.matmul(
                                p6[:, (half * 4 + fg) * 512:(half * 4 + fg + 1) * 512],
                                qxfm[:, dc * R + rt * 128: dc * R + (rt + 1) * 128],
                                qk[:, dc * D + fg * 512: dc * D + (fg + 1) * 512],
                                start=(dc == 0), stop=(dc == 15)))

            @blk2.vector
            def _(vector):
                for rt in range(8):
                    half = rt % 2
                    vector.wait_ge(s_mm.h, mm6 + (rt + 1) * 64)
                    vector.wait_ge(s_ld.h, ld6b + 48 + (rt + 1) * 16)
                    if rt >= 2:
                        vector.wait_ge(s_st.h, st6 + (rt - 1) * 16)
                    psl = p6[:, half * 2048:(half + 1) * 2048]
                    vector.scalar_tensor_tensor(t3[:, :], psl, sxv[:, rt:rt + 1],
                                                bsk[:, :], ALU.mult, ALU.mult)
                    vector.tensor_tensor(t4[:, :], t3[:, :], bbq[:, :], ALU.add)
                    vector.tensor_scalar(t5[:, :], t4[:, :], 0.0, None, ALU.max)
                    fence(vector, vector.tensor_reduce(ay[:, :], t5[:, :], AX.X,
                                                       ALU.max,
                                                       apply_absolute_value=True))
                    fence(vector, vector.tensor_scalar(ay2[:, :], ay[:, :], TINY,
                                                       None, ALU.max))
                    fence(vector, vector.reciprocal(ryv[:, :], ay2[:, :]))
                    vector.tensor_scalar(syv[:, :], ay2[:, :], 1.0 / QMAX, None, ALU.mult)
                    vector.tensor_scalar(t3[:, :], t5[:, :], ryv[:, :], QMAX,
                                         ALU.mult, ALU.mult)
                    vector.tensor_copy(ki32[:, :], t3[:, :])
                    vector.tensor_scalar(t4[:, :], ki32[:, :], QMAX, -QMAX,
                                         ALU.min, ALU.max)
                    vector.scalar_tensor_tensor(t3[:, :], t4[:, :], syv[:, :],
                                                bgb[:, :], ALU.mult, ALU.mult)
                    s_ve.one(vector.tensor_tensor(outt[rt % 2][:, :],
                                                  x2u[rt % 2][:, :],
                                                  t3[:, :], ALU.add))

            @blk2.gpsimd
            def _(gpsimd):
                for rt in range(8):
                    gpsimd.wait_ge(s_ve.h, ve6 + 34 + rt + 1)
                    s_st.dma(gpsimd.dma_start(out=out_t.ap()[rt * 128:(rt + 1) * 128, :],
                                              in_=outt[rt % 2][:, :]))
                gpsimd.wait_ge(s_st.h, s_st.n)

        st_qx.close()

    return nc


def _host_consts():
    ident = np.eye(128, dtype=ml_dtypes.bfloat16)
    jj = np.arange(512, dtype=np.float32)[None, :]
    ip = np.arange(128, dtype=np.float32)[:, None]
    pat_off = (jj - ip).astype(np.float32)
    t512 = np.broadcast_to(512.0 * np.arange(8, dtype=np.float32)[None, :],
                           (128, 8)).copy()
    return ident, pat_off, t512


def _run(dump, L0_attn_x, L0_attn_q_w, L0_attn_q_b, L0_attn_k_w, L0_attn_k_b,
         L0_attn_v_w, L0_attn_v_b, L0_attn_o_w, L0_attn_o_b,
         L0_ffn_kernel, L0_ffn_bias, L0_c, L0_wmod):
    ident, pat_off, t512 = _host_consts()
    x = np.ascontiguousarray(np.asarray(L0_attn_x, dtype=np.float32).reshape(B * T, D))
    qT = np.ascontiguousarray(np.asarray(L0_attn_q_w, dtype=np.float32).T)
    kT = np.ascontiguousarray(np.asarray(L0_attn_k_w, dtype=np.float32).T)
    vT = np.ascontiguousarray(np.asarray(L0_attn_v_w, dtype=np.float32).T)
    oT = np.ascontiguousarray(np.asarray(L0_attn_o_w, dtype=np.float32).T)
    fk = np.ascontiguousarray(np.asarray(L0_ffn_kernel, dtype=np.float32))
    fkT = np.ascontiguousarray(fk.T)
    qb = np.asarray(L0_attn_q_b, dtype=np.float32).reshape(1, D)
    kb = np.asarray(L0_attn_k_b, dtype=np.float32).reshape(1, D)
    vb = np.asarray(L0_attn_v_b, dtype=np.float32).reshape(1, D)
    ob = np.asarray(L0_attn_o_b, dtype=np.float32).reshape(1, D)
    fb = np.asarray(L0_ffn_bias, dtype=np.float32).reshape(1, D)
    cc = np.asarray(L0_c, dtype=np.float32)
    wm = np.ascontiguousarray(np.asarray(L0_wmod, dtype=np.float32))

    nc = build_nc(dump)
    in_maps = []
    for core in range(NC):
        b, cp = core // 4, core % 4
        gtiles = cp + 4 * np.arange(8)
        rows = (b * T + (gtiles[:, None] * 128 + np.arange(128)[None, :])).reshape(-1)
        x_own = np.ascontiguousarray(x[rows])
        jj = np.arange(512, dtype=np.float32)[None, :]
        ip = np.arange(128, dtype=np.float32)[:, None]
        pat_diag = np.where(jj > 128.0 * cp + ip, np.float32(NEG), jj - ip)
        pats = np.stack([pat_off, pat_diag]).astype(np.float16)
        i0 = (512.0 * np.arange(8) + 128.0 * cp).astype(np.float32)
        i0bc = np.broadcast_to(i0[None, :], (128, 8)).copy()
        in_maps.append({
            "x_own": x_own, "qT": qT, "kT": kT, "vT": vT,
            "qb": qb, "kb": kb, "vb": vb, "oT": oT, "ob": ob,
            "fk": fk, "fkT": fkT, "fb": fb,
            "c_own": np.ascontiguousarray(cc[b].reshape(1, D)),
            "wm_own": np.ascontiguousarray(wm[:, cp * 3072:(cp + 1) * 3072]),
            "ident": ident, "pats": pats,
            "t512c": t512, "i0bc": i0bc,
        })
    res = run_bass_kernel_spmd(nc, in_maps, core_ids=list(range(NC)))
    return res


def core_rows(core):
    b, cp = core // 4, core % 4
    gtiles = cp + 4 * np.arange(8)
    return (b * T + (gtiles[:, None] * 128 + np.arange(128)[None, :])).reshape(-1)


def kernel(**inputs):
    res = _run((), **inputs)
    out = np.empty((B * T, D), dtype=np.float32)
    for core in range(NC):
        out[core_rows(core)] = res.results[core]["out"]
    return out.reshape(B, T, D)
